# version 03
# baseline (speedup 1.0000x reference)
import numpy as np
from contextlib import ExitStack

import concourse.bass as bass
import concourse.mybir as mybir
from concourse.bass_utils import run_bass_kernel_spmd

F32 = mybir.dt.float32
BF16 = mybir.dt.bfloat16
I32 = mybir.dt.int32
AF = mybir.ActivationFunctionType
OP = mybir.AluOpType
AX = mybir.AxisListType

B, T, DM = 4, 4096, 2048
H = 16
TC = 2048
NF = DM // 128
IF = 8192
NIC = IF // 128
NCORES = 8

ENGS = ("sync", "scalar", "vector", "gpsimd", "tensor")


class Ctr:
    __slots__ = ("h", "n")

    def __init__(self, h):
        self.h = h
        self.n = 0

    def bump(self, k=1):
        self.n += k
        return self.n


def build_nc(dbg=()):
    nc = bass.Bass("TRN2", target_bir_lowering=False)

    def _skind(name):
        return "ExternalOutput" if name in dbg else "Internal"

    x_in = nc.dram_tensor("x_in", [TC, DM], F32, kind="ExternalInput")
    wqkv_d = nc.dram_tensor("w_qkv", [DM, 3 * DM], F32, kind="ExternalInput")
    wout_d = nc.dram_tensor("w_out", [DM, DM], F32, kind="ExternalInput")
    dk_d = nc.dram_tensor("decay_rep", [128, H], F32, kind="ExternalInput")
    gw_d = nc.dram_tensor("gate_w", [IF, DM], I32, kind="ExternalInput")
    uw_d = nc.dram_tensor("up_w", [IF, DM], I32, kind="ExternalInput")
    dw_d = nc.dram_tensor("down_w", [DM, IF], I32, kind="ExternalInput")
    sg_d = nc.dram_tensor("sg", [128, NIC], F32, kind="ExternalInput")
    su_d = nc.dram_tensor("su", [128, NIC], F32, kind="ExternalInput")
    sd_d = nc.dram_tensor("sd", [128, NF], F32, kind="ExternalInput")
    rm_d = nc.dram_tensor("rm", [128, 256], F32, kind="ExternalInput")

    y_d = nc.dram_tensor("y", [TC, DM], F32, kind="ExternalOutput")

    xn_s = nc.dram_tensor("xn_s", [TC, DM], BF16, kind=_skind("xn_s"))
    kv_s = nc.dram_tensor("kv_s", [2 * H * 128, TC], BF16, kind=_skind("kv_s"))
    ap_s = nc.dram_tensor("ap_s", [DM, TC], BF16, kind=_skind("ap_s"))
    s_s = nc.dram_tensor("s_s", [DM, TC], BF16, kind=_skind("s_s"))
    wg_s = nc.dram_tensor("wg_s", [IF, DM], BF16, kind=_skind("wg_s"))
    wu_s = nc.dram_tensor("wu_s", [IF, DM], BF16, kind=_skind("wu_s"))
    wd_s = nc.dram_tensor("wd_s", [DM, IF], BF16, kind=_skind("wd_s"))
    h_s = nc.dram_tensor("h_s", [IF, TC], BF16, kind=_skind("h_s"))
    gf_s = nc.dram_tensor("gf_s", [DM, TC], BF16, kind=_skind("gf_s"))
    cc_i = nc.dram_tensor("cc_i", [128, 2 * H], F32)
    cc_o = nc.dram_tensor("cc_o", [128 * NCORES, 2 * H], F32)
    st_dbg = nc.dram_tensor("st_dbg", [128, 4 * H], F32, kind=_skind("st_dbg"))

    streams = {e: [] for e in ENGS}

    def em(eng, fn):
        streams[eng].append(fn)

    def wait(eng, ctr, tgt):
        if tgt and tgt > 0:
            em(eng, lambda e, c=ctr, t=tgt: e.wait_ge(c.h, t))

    def dma(q, out, in_, ctr, transpose=False):
        t = ctr.bump(16)
        em(q, lambda e, o=out, i=in_, c=ctr, tr=transpose:
           e.dma_start(out=o, in_=i, transpose=tr).then_inc(c.h, 16))
        return t

    def op(eng, fn, inc=None):
        if inc is None:
            em(eng, fn)
            return None
        t = inc.bump(1)
        em(eng, lambda e, f=fn, c=inc: f(e).then_inc(c.h, 1))
        return t

    with ExitStack() as root:
        def sbt(scope, name, shape, dt):
            return scope.enter_context(nc.sbuf_tensor(name, shape, dt))

        def sem(name):
            return Ctr(root.enter_context(nc.semaphore(name)))

        cc = sem("cc")
        pe = sem("pe")
        vc = sem("vc")
        sc = sem("sc")
        bar = sem("bar")

        dma_ctrs = []

        def qsem(name, eng, n=1):
            out = []
            for i in range(n):
                c = sem(f"{name}{i}")
                dma_ctrs.append((eng, c))
                out.append(c)
            return out if n > 1 else out[0]

        q_c = qsem("q_c", "gpsimd")
        q_x = qsem("q_x", "gpsimd", 2)
        q_cvi = qsem("q_cvi", "gpsimd", 2)
        q_cvs = qsem("q_cvs", "scalar", 2)
        q_w = qsem("q_w", "gpsimd", 2)
        q_kst = qsem("q_kst", "scalar", 2)
        q_kv4 = qsem("q_kv4", "scalar", 2)
        q_ap = qsem("q_ap", "scalar")
        q_xn = qsem("q_xn", "scalar", 2)
        q_cc2 = qsem("q_cc2", "gpsimd")
        q_t = qsem("q_t", "sync", 2)
        q_apl = qsem("q_apl", "scalar", 2)
        q_wo = qsem("q_wo", "gpsimd")
        q_sev = qsem("q_sev", "scalar", 2)
        q_hst = qsem("q_hst", "scalar", 2)
        q_hp = qsem("q_hp", "scalar", 2)
        q_gf = qsem("q_gf", "scalar", 2)
        q_y = qsem("q_y", "scalar", 2)
        po = sem("po")

        def fence():
            for eng, c in dma_ctrs:
                wait(eng, c, c.n)
            for e in ENGS:
                em(e, lambda eng: eng.sem_inc(bar.h, 1))
            tgt = bar.bump(5)
            for e in ENGS:
                wait(e, bar, tgt)

        ps = [root.enter_context(nc.psum_tensor(f"ps{i}", [128, 512], F32))
              for i in range(8)]
        bank_free = [None] * 8

        def pe_guard(i):
            if bank_free[i] is not None:
                c, t = bank_free[i]
                wait("tensor", c, t)
                bank_free[i] = None

        dk_raw = sbt(root, "dk_raw", [128, H], F32)
        dk_rep = sbt(root, "dk_rep", [128, H], F32)
        sg_b = sbt(root, "sg_b", [128, NIC], F32)
        su_b = sbt(root, "su_b", [128, NIC], F32)
        sd_b = sbt(root, "sd_b", [128, NF], F32)
        rm_b = sbt(root, "rm_b", [128, 256], F32)
        g_b = sbt(root, "g_b", [128, 256], F32)
        gm_b = sbt(root, "gm_b", [128, 256], F32)
        ssel = sbt(root, "ssel", [128, 2 * H], F32)
        states = sbt(root, "states", [128, 2 * H], F32)
        ones_cb = sbt(root, "ones_cb", [128, 1], BF16)
        ones_rb = sbt(root, "ones_rb", [1, 128], BF16)
        recip_b = sbt(root, "recip_b", [1, TC], BF16)
        cst_b = sbt(root, "cst_b", [128, 4], F32)
        den_row = sbt(root, "den_row", [1, TC], F32)
        recip_row = sbt(root, "recip_row", [1, TC], F32)

        nc.const_aps.register(F32, -128.0, cst_b[:, 1:2])
        t_dk = dma("gpsimd", dk_raw[:], dk_d[:, :], q_c)
        for dst, src in ((sg_b, sg_d), (su_b, su_d), (sd_b, sd_d), (rm_b, rm_d)):
            dma("gpsimd", dst[:], src[:, :], q_c)
        op("vector", lambda e: e.memset(ones_cb[:], 1.0))
        op("vector", lambda e: e.memset(cst_b[:, 0:1], 1e-5))
        op("vector", lambda e: e.memset(cst_b[:, 1:2], -128.0))
        op("vector", lambda e: e.memset(cst_b[:, 2:3], -1e-6))
        op("vector", lambda e: e.memset(cst_b[:, 3:4], 1e-6))
        t_ones = op("vector", lambda e: e.memset(ones_rb[:], 1.0), inc=vc)
        wait("scalar", q_c, q_c.n)
        wait("scalar", vc, t_ones)
        t_sig = op("scalar", lambda e: e.activation(
            out=dk_rep[:], in_=dk_raw[:], func=AF.Sigmoid), inc=sc)

        with ExitStack() as scA:
            xnT = sbt(scA, "xnT", [128, NF * TC], BF16)

            def xnT_sl(fc, t0, tn):
                return xnT[:, fc * TC + t0: fc * TC + t0 + tn]

            cv_in = [sbt(scA, f"cv_in{i}", [128, DM], BF16) for i in range(2)]
            cv_out = [sbt(scA, f"cv_out{i}", [128, DM], BF16) for i in range(2)]
            cv_state = [None, None]
            units = []
            for icx in range(NIC):
                units.append((gw_d, wg_s, icx * 128, 0))
                units.append((uw_d, wu_s, icx * 128, 0))
            for j in range(64):
                units.append((dw_d, wd_s, (j // 4) * 128, (j % 4) * DM))
            unit_idx = [0]

            def emit_conv_units(k):
                for _ in range(k):
                    u = unit_idx[0]
                    if u >= len(units):
                        return
                    unit_idx[0] += 1
                    src, dst, r0, c0 = units[u]
                    bi = u % 2
                    st = cv_state[bi]
                    if st is not None:
                        wait("gpsimd", st["pc"], st["pt"])
                    t_ld = dma("gpsimd", cv_in[bi][:],
                               src[r0:r0 + 128, c0:c0 + DM], q_cvi[bi])
                    if u % 2 == 0:
                        wait("vector", q_cvi[bi], t_ld)
                        if st is not None:
                            wait("vector", q_cvs[bi], st["st"])
                        t_pass = op("vector", lambda e, bi=bi: e.tensor_scalar(
                            out=cv_out[bi][:], in0=cv_in[bi][:], scalar1=-128.0,
                            scalar2=None, op0=OP.add), inc=vc)
                        pc = vc
                    else:
                        wait("scalar", q_cvi[bi], t_ld)
                        if st is not None:
                            wait("scalar", q_cvs[bi], st["st"])
                        t_pass = op("scalar", lambda e, bi=bi: e.add(
                            out=cv_out[bi][:], in_=cv_in[bi][:], add=-128.0),
                            inc=sc)
                        pc = sc
                    wait("scalar", pc, t_pass)
                    t_st = dma("scalar", dst[r0:r0 + 128, c0:c0 + DM],
                               cv_out[bi][:], q_cvs[bi])
                    cv_state[bi] = dict(pc=pc, pt=t_pass, st=t_st)

            with ExitStack() as scL:
                xt = [sbt(scL, f"xt{i}", [128, DM], F32) for i in range(2)]
                xc = [sbt(scL, f"xc{i}", [128, DM], F32) for i in range(2)]
                sqw = [sbt(scL, f"sqw{i}", [128, DM], F32) for i in range(2)]
                s1 = [sbt(scL, f"s1_{i}", [128, 1], F32) for i in range(2)]
                s2 = [sbt(scL, f"s2_{i}", [128, 1], F32) for i in range(2)]
                s3 = [sbt(scL, f"s3_{i}", [128, 1], F32) for i in range(2)]
                xnb = [sbt(scL, f"xnb{i}", [128, DM], BF16) for i in range(2)]
                ln_st = [0, 0]
                ln_use = [0, 0]
                t_r_par = [0, 0]
                for tt in range(NF):
                    bi = tt % 2
                    if ln_use[bi]:
                        wait("gpsimd", vc, ln_use[bi])
                    t_x = dma("gpsimd", xt[bi][:],
                              x_in[tt * 128:(tt + 1) * 128, :], q_x[bi])
                    wait("vector", q_x[bi], t_x)
                    wait("vector", sc, t_r_par[bi])
                    t_sum = op("vector", lambda e, bi=bi: e.reduce_sum(
                        out=s1[bi][:], in_=xt[bi][:], axis=AX.X), inc=vc)
                    wait("scalar", vc, t_sum)
                    t_m = op("scalar", lambda e, bi=bi: e.activation(
                        out=s2[bi][:], in_=s1[bi][:], func=AF.Copy,
                        scale=1.0 / DM), inc=sc)
                    wait("vector", sc, t_m)
                    t_c = op("vector", lambda e, bi=bi: e.tensor_scalar(
                        out=xc[bi][:], in0=xt[bi][:], scalar1=s2[bi][:, 0:1],
                        scalar2=None, op0=OP.subtract), inc=vc)
                    wait("scalar", vc, t_c)
                    t_sq = op("scalar", lambda e, bi=bi: e.activation(
                        out=sqw[bi][:], in_=xc[bi][:], func=AF.Square), inc=sc)
                    wait("vector", sc, t_sq)
                    t_rs = op("vector", lambda e, bi=bi: e.reduce_sum(
                        out=s1[bi][:], in_=sqw[bi][:], axis=AX.X), inc=vc)
                    wait("scalar", vc, t_rs)
                    op("scalar", lambda e, bi=bi: e.activation(
                        out=s2[bi][:], in_=s1[bi][:], func=AF.Ln, scale=1.0 / DM,
                        bias=cst_b[:, 0:1]))
                    t_r = op("scalar", lambda e, bi=bi: e.activation(
                        out=s3[bi][:], in_=s2[bi][:], func=AF.Exp,
                        scale=-0.5), inc=sc)
                    t_r_par[bi] = t_r
                    if ln_st[bi]:
                        wait("vector", q_xn[bi], ln_st[bi])
                    wait("vector", sc, t_r)
                    t_xn = op("vector", lambda e, bi=bi: e.tensor_scalar(
                        out=xnb[bi][:], in0=xc[bi][:], scalar1=s3[bi][:, 0:1],
                        scalar2=None, op0=OP.mult), inc=vc)
                    ln_use[bi] = t_xn
                    wait("scalar", vc, t_xn)
                    ln_st[bi] = dma("scalar", xn_s[tt * 128:(tt + 1) * 128, :],
                                    xnb[bi][:], q_xn[bi])
                wait("sync", q_xn[0], q_xn[0].n)
                wait("sync", q_xn[1], q_xn[1].n)
                for fc in range(NF):
                    dma("sync", xnT_sl(fc, 0, TC),
                        xn_s[0:TC, fc * 128:(fc + 1) * 128], q_t[0],
                        transpose=True)
                t_xnT = q_t[0].n

            fence()

            with ExitStack() as scH:
                ones_t = sbt(scH, "ones_t", [128, TC], BF16)
                dk_tl = [sbt(scH, f"dk_tl{i}", [128, TC], F32) for i in range(2)]
                t_onest = op("vector", lambda e: e.memset(ones_t[:], 1.0),
                             inc=vc)
                wkv_b = [sbt(scH, f"wkv{i}", [128, 2 * NF * 128], BF16)
                         for i in range(2)]
                phik = [sbt(scH, f"phik{i}", [128, TC], BF16) for i in range(2)]
                v_b = [sbt(scH, f"vb{i}", [128, TC], BF16) for i in range(2)]
                bkv = sbt(scH, "bkv", [128, TC], F32)
                t1b = sbt(scH, "t1b", [128, 512], F32)
                t2b = [sbt(scH, f"t2b{i}", [128, 512], F32) for i in range(2)]
                q_f = sbt(scH, "q_f", [128, TC], F32)
                rkv_f = sbt(scH, "rkv_f", [128, TC], F32)
                rks_f = sbt(scH, "rks_f", [128, TC], F32)
                prod_b = sbt(scH, "prod_b", [128, TC], BF16)
                num_f = bkv
                apb = sbt(scH, "apb", [128, TC], BF16)

                QCOL, KCOL, VCOL = 0, DM, 2 * DM
                stt_busy = [0, 0]

                def emit_phi(bank_pe_t, bank, tt, dst_ap):
                    i2 = tt % 2
                    wait("scalar", pe, bank_pe_t)
                    if stt_busy[i2]:
                        wait("scalar", vc, stt_busy[i2])
                    op("scalar", lambda e, b=bank: e.activation(
                        out=t1b[:], in_=ps[b][:, :], func=AF.Relu,
                        scale=-1.0))
                    t2t = op("scalar", lambda e, i2=i2: e.activation(
                        out=t2b[i2][:], in_=t1b[:], func=AF.Exp,
                        scale=-1.0), inc=sc)
                    wait("vector", sc, t2t)
                    t = op("vector", lambda e, b=bank, i2=i2, d=dst_ap:
                           e.scalar_tensor_tensor(
                               out=d, in0=ps[b][:, :], scalar=0.0,
                               in1=t2b[i2][:], op0=OP.max, op1=OP.add), inc=vc)
                    stt_busy[i2] = t
                    bank_free[bank] = (vc, t)
                    return t

                wkv_load = [0, 0]
                wkv_inuse = [0, 0]
                kv_store = [0, 0]
                dk_fill = [0, 0]
                dk_used = [0, 0]
                st_copied = [0]

                def load_wkv(h):
                    bi = h % 2
                    if wkv_inuse[bi]:
                        wait("gpsimd", pe, wkv_inuse[bi])
                    for j, col0 in ((0, KCOL + h * 128), (1, VCOL + h * 128)):
                        wkv_load[bi] = dma(
                            "gpsimd",
                            wkv_b[bi][:, j * NF * 128:(j + 1) * NF * 128]
                            .rearrange("p (f c) -> p f c", f=NF),
                            wqkv_d[:, col0:col0 + 128]
                            .rearrange("(f p) c -> p f c", p=128), q_w[bi])

                load_wkv(0)
                load_wkv(1)

                for h in range(H):
                    bi = h % 2
                    wait("tensor", q_w[bi], wkv_load[bi])
                    if h == 0:
                        wait("tensor", q_t[0], t_xnT)
                    kv_pe = [0] * 8
                    for j, base in ((0, 0), (1, 4)):
                        for bk in range(4):
                            pe_guard(base + bk)
                        for fc in range(NF):
                            for tt in range(4):
                                b_ = base + tt
                                last = (fc == NF - 1)
                                lh = wkv_b[bi][:, (j * NF + fc) * 128:
                                               (j * NF + fc) * 128 + 128]
                                t = op("tensor",
                                       lambda e, lh=lh, fc=fc, tt=tt, b_=b_,
                                       last=last:
                                       e.matmul(out=ps[b_][:, :], lhsT=lh,
                                                rhs=xnT_sl(fc, tt * 512, 512),
                                                start=(fc == 0), stop=last),
                                       inc=pe if last else None)
                                if last:
                                    kv_pe[b_] = t
                    wkv_inuse[bi] = kv_pe[7]
                    if h + 1 < H:
                        load_wkv(h + 1)
                    emit_conv_units(6)
                    if kv_store[bi]:
                        wait("vector", q_kst[bi], kv_store[bi])
                    phi_t = 0
                    for tt in range(4):
                        phi_t = emit_phi(kv_pe[tt], tt, tt,
                                         phik[bi][:, tt * 512:(tt + 1) * 512])
                    vcp_t = 0
                    for tt in range(4):
                        wait("vector", pe, kv_pe[4 + tt])
                        vcp_t = op("vector", lambda e, tt=tt, bi=bi:
                                   e.tensor_copy(
                                       out=v_b[bi][:, tt * 512:(tt + 1) * 512],
                                       in_=ps[4 + tt][:, :]), inc=vc)
                        bank_free[4 + tt] = (vc, vcp_t)
                    wait("gpsimd", vc, max(phi_t, vcp_t))
                    t_bkv = op("gpsimd", lambda e, bi=bi: e.tensor_mul(
                        out=bkv[:], in0=phik[bi][:], in1=v_b[bi][:]), inc=po)
                    wait("vector", po, t_bkv)
                    if st_copied[0]:
                        wait("vector", sc, st_copied[0])
                    if dk_used[bi]:
                        wait("scalar", vc, dk_used[bi])
                    if h < 2:
                        wait("scalar", sc, t_sig)
                        wait("scalar", vc, t_onest)
                    dk_fill[bi] = op("scalar", lambda e, bi=bi, h=h: e.mul(
                        out=dk_tl[bi][:], in_=ones_t[:],
                        mul=dk_rep[:, h:h + 1]), inc=sc)
                    wait("vector", sc, dk_fill[bi])
                    op("vector", lambda e, bi=bi: e.tensor_tensor_scan(
                        out=rkv_f[:], data0=dk_tl[bi][:], data1=bkv[:],
                        initial=0.0, op0=OP.mult, op1=OP.add))
                    dk_used[bi] = op(
                        "vector", lambda e, bi=bi: e.tensor_tensor_scan(
                            out=rks_f[:], data0=dk_tl[bi][:], data1=phik[bi][:],
                            initial=0.0, op0=OP.mult, op1=OP.add), inc=vc)
                    wait("scalar", vc, dk_used[bi])
                    op("scalar", lambda e, h=h: e.copy(
                        out=states[:, 2 * h:2 * h + 1],
                        in_=rkv_f[:, TC - 1:TC]))
                    st_copied[0] = op("scalar", lambda e, h=h: e.copy(
                        out=states[:, 2 * h + 1:2 * h + 2],
                        in_=rks_f[:, TC - 1:TC]), inc=sc)
                    wait("scalar", vc, max(vcp_t, phi_t))
                    dma("scalar", kv_s[(2 * h) * 128:(2 * h + 1) * 128, :],
                        phik[bi][:], q_kst[bi])
                    kv_store[bi] = dma(
                        "scalar", kv_s[(2 * h + 1) * 128:(2 * h + 2) * 128, :],
                        v_b[bi][:], q_kst[bi])

                fence()

                dma("gpsimd", cc_i[:, :], states[:], q_cc2)
                wait("gpsimd", q_cc2, q_cc2.n)
                em("gpsimd", lambda e: e.collective_compute(
                    "AllGather", OP.bypass,
                    replica_groups=[list(range(NCORES))],
                    ins=[cc_i.ap().opt()], outs=[cc_o.ap().opt()],
                ).then_inc(cc.h))
                cc.bump(1)
                wait("gpsimd", cc, 1)
                t_g = dma("gpsimd", g_b[:].rearrange("p (r s) -> p r s", r=8),
                          cc_o.ap().rearrange("(r p) s -> p r s", p=128), q_cc2)
                wait("vector", q_cc2, t_g)
                t_gm = op("vector", lambda e: e.tensor_mul(
                    out=gm_b[:], in0=g_b[:], in1=rm_b[:]), inc=vc)
                wait("scalar", vc, t_gm)
                t_gm2 = op("scalar", lambda e: e.copy(
                    out=g_b[:], in_=gm_b[:]), inc=sc)
                wait("vector", sc, t_gm2)
                t_sel = op("vector", lambda e: e.tensor_reduce(
                    out=ssel[:], in_=g_b[:].rearrange("p (r s) -> p s r", r=8),
                    axis=AX.X, op=OP.add), inc=vc)
                if "st_dbg" in dbg:
                    wait("gpsimd", vc, t_sel)
                    dma("gpsimd", st_dbg[:, 0:2 * H], states[:], q_cc2)
                    dma("gpsimd", st_dbg[:, 2 * H:4 * H], ssel[:], q_cc2)

                wq_load = [0, 0]
                wq_inuse = [0, 0]
                kv4_load = [0, 0]
                kv4_used = [0, 0]
                ap_store = 0
                prod_peuse = 0
                den_used = 0
                recip_used = 0

                def load_wq(h):
                    bi = h % 2
                    if wq_inuse[bi]:
                        wait("gpsimd", pe, wq_inuse[bi])
                    wq_load[bi] = dma(
                        "gpsimd",
                        wkv_b[bi][:, 0:NF * 128]
                        .rearrange("p (f c) -> p f c", f=NF),
                        wqkv_d[:, QCOL + h * 128:QCOL + h * 128 + 128]
                        .rearrange("(f p) c -> p f c", p=128), q_w[bi])

                def load_kv4(h):
                    bi = h % 2
                    if kv4_used[bi]:
                        wait("scalar", vc, kv4_used[bi])
                    dma("scalar", phik[bi][:],
                        kv_s[(2 * h) * 128:(2 * h + 1) * 128, :], q_kv4[bi])
                    kv4_load[bi] = dma(
                        "scalar", v_b[bi][:],
                        kv_s[(2 * h + 1) * 128:(2 * h + 2) * 128, :], q_kv4[bi])

                load_wq(0)
                load_kv4(0)
                load_wq(1)
                load_kv4(1)

                for h in range(H):
                    bi = h % 2
                    wait("tensor", q_w[bi], wq_load[bi])
                    q_pe = [0] * 4
                    for bk in range(4):
                        pe_guard(bk)
                    for fc in range(NF):
                        for tt in range(4):
                            last = (fc == NF - 1)
                            lh = wkv_b[bi][:, fc * 128:(fc + 1) * 128]
                            t = op("tensor", lambda e, lh=lh, fc=fc, tt=tt,
                                   last=last:
                                   e.matmul(out=ps[tt][:, :], lhsT=lh,
                                            rhs=xnT_sl(fc, tt * 512, 512),
                                            start=(fc == 0), stop=last),
                                   inc=pe if last else None)
                            if last:
                                q_pe[tt] = t
                    wq_inuse[bi] = q_pe[3]
                    if h + 1 < H:
                        load_wq(h + 1)
                        load_kv4(h + 1)
                    emit_conv_units(4)
                    for tt in range(4):
                        emit_phi(q_pe[tt], tt, tt,
                                 q_f[:, tt * 512:(tt + 1) * 512])
                    wait("gpsimd", q_kv4[bi], kv4_load[bi])
                    t_bkv = op("gpsimd", lambda e, bi=bi: e.tensor_mul(
                        out=bkv[:], in0=phik[bi][:], in1=v_b[bi][:]), inc=po)
                    wait("vector", po, t_bkv)
                    if dk_used[bi]:
                        wait("scalar", vc, dk_used[bi])
                    dk_fill[bi] = op("scalar", lambda e, bi=bi, h=h: e.mul(
                        out=dk_tl[bi][:], in_=ones_t[:],
                        mul=dk_rep[:, h:h + 1]), inc=sc)
                    wait("vector", sc, dk_fill[bi])
                    op("vector", lambda e, bi=bi, h=h: e.tensor_tensor_scan(
                        out=rkv_f[:], data0=dk_tl[bi][:], data1=bkv[:],
                        initial=ssel[:, 2 * h:2 * h + 1],
                        op0=OP.mult, op1=OP.add))
                    kv4_used[bi] = op(
                        "vector", lambda e, bi=bi, h=h: e.tensor_tensor_scan(
                            out=rks_f[:], data0=dk_tl[bi][:], data1=phik[bi][:],
                            initial=ssel[:, 2 * h + 1:2 * h + 2],
                            op0=OP.mult, op1=OP.add), inc=vc)
                    dk_used[bi] = kv4_used[bi]
                    if prod_peuse:
                        wait("gpsimd", pe, prod_peuse)
                    wait("gpsimd", vc, max(kv4_used[bi], stt_busy[0],
                                           stt_busy[1]))
                    t_prod = op("gpsimd", lambda e: e.tensor_mul(
                        out=prod_b[:], in0=q_f[:], in1=rks_f[:]), inc=po)
                    wait("tensor", po, t_prod)
                    if h == 0:
                        wait("tensor", vc, t_ones)
                    den_pe = [0] * 4
                    for c in range(4):
                        pe_guard(4 + c)
                        den_pe[c] = op("tensor", lambda e, c=c: e.matmul(
                            out=ps[4 + c][0:1, :], lhsT=ones_cb[:],
                            rhs=prod_b[:, c * 512:(c + 1) * 512],
                            start=True, stop=True), inc=pe)
                    prod_peuse = den_pe[3]
                    if recip_used:
                        wait("scalar", pe, recip_used)
                    t_dr = 0
                    for c in range(4):
                        wait("scalar", pe, den_pe[c])
                        t_dr = op("scalar", lambda e, c=c: e.activation(
                            out=den_row[0:1, c * 512:(c + 1) * 512],
                            in_=ps[4 + c][0:1, :], func=AF.Relu,
                            bias=cst_b[0:1, 2:3]), inc=sc)
                        bank_free[4 + c] = (sc, t_dr)
                    op("scalar", lambda e: e.activation(
                        out=recip_row[0:1, :], in_=den_row[0:1, :], func=AF.Ln,
                        bias=cst_b[0:1, 3:4]))
                    t_rc = op("scalar", lambda e: e.activation(
                        out=recip_b[0:1, :], in_=recip_row[0:1, :], func=AF.Exp,
                        scale=-1.0), inc=sc)
                    den_used = t_rc
                    wait("tensor", sc, t_rc)
                    rep_pe = [0] * 4
                    for c in range(4):
                        pe_guard(4 + c)
                        rep_pe[c] = op("tensor", lambda e, c=c: e.matmul(
                            out=ps[4 + c][:, :], lhsT=ones_rb[:],
                            rhs=recip_b[0:1, c * 512:(c + 1) * 512],
                            start=True, stop=True), inc=pe)
                    recip_used = rep_pe[3]
                    t_num = op("gpsimd", lambda e: e.tensor_mul(
                        out=num_f[:], in0=q_f[:], in1=rkv_f[:]), inc=po)
                    wait("vector", po, t_num)
                    if ap_store:
                        wait("vector", q_ap, ap_store)
                    t_ap = 0
                    for c in range(4):
                        wait("vector", pe, rep_pe[c])
                        t_ap = op("vector", lambda e, c=c: e.tensor_tensor(
                            out=apb[:, c * 512:(c + 1) * 512],
                            in0=num_f[:, c * 512:(c + 1) * 512],
                            in1=ps[4 + c][:, :], op=OP.mult), inc=vc)
                        bank_free[4 + c] = (vc, t_ap)
                    wait("scalar", vc, t_ap)
                    ap_store = dma("scalar", ap_s[h * 128:(h + 1) * 128, :],
                                   apb[:], q_ap)
                emit_conv_units(len(units))

            fence()

            with ExitStack() as sc5:
                wo_b = sbt(sc5, "wo_b", [128, NF * DM], BF16)
                ap_sl = [sbt(sc5, f"ap_sl{i}", [128, NF * 512], BF16)
                         for i in range(2)]
                sev = [sbt(sc5, f"sev{i}", [128, 512], BF16) for i in range(2)]
                for fc in range(NF):
                    dma("gpsimd", wo_b[:, fc * DM:(fc + 1) * DM],
                        wout_d[fc * 128:(fc + 1) * 128, :], q_wo)
                wait("tensor", q_wo, q_wo.n)
                ap_use = [0, 0]
                sev_store = [0, 0]
                for tt in range(4):
                    bi = tt % 2
                    if ap_use[bi]:
                        wait("scalar", pe, ap_use[bi])
                    t_apl = dma(
                        "scalar",
                        ap_sl[bi][:].rearrange("p (f t) -> p f t", f=NF),
                        ap_s.ap().rearrange("(f p) t -> p f t", p=128)
                        [:, :, tt * 512:(tt + 1) * 512], q_apl[bi])
                    wait("tensor", q_apl[bi], t_apl)
                    last_mm = 0
                    for oc in range(NF):
                        bk = oc % 2
                        si = oc % 2
                        pe_guard(bk)
                        for fc in range(NF):
                            last = (fc == NF - 1)
                            t = op("tensor", lambda e, bi=bi, fc=fc, oc=oc,
                                   bk=bk, last=last:
                                   e.matmul(
                                       out=ps[bk][:, :],
                                       lhsT=wo_b[:, fc * DM + oc * 128:
                                                 fc * DM + oc * 128 + 128],
                                       rhs=ap_sl[bi][:, fc * 512:
                                                     (fc + 1) * 512],
                                       start=(fc == 0), stop=last),
                                   inc=pe if last else None)
                            if last:
                                last_mm = t
                        wait("scalar", pe, last_mm)
                        if sev_store[si]:
                            wait("scalar", q_sev[si], sev_store[si])
                        t_ev = op("scalar", lambda e, si=si, bk=bk: e.copy(
                            out=sev[si][:], in_=ps[bk][:, :]), inc=sc)
                        bank_free[bk] = (sc, t_ev)
                        sev_store[si] = dma(
                            "scalar",
                            s_s[oc * 128:(oc + 1) * 128,
                                tt * 512:(tt + 1) * 512], sev[si][:], q_sev[si])
                    ap_use[bi] = last_mm

            fence()

            with ExitStack() as sc6:
                wgT = [sbt(sc6, f"wgT{i}", [128, NF * 512], BF16)
                       for i in range(2)]
                wuT = [sbt(sc6, f"wuT{i}", [128, NF * 512], BF16)
                       for i in range(2)]
                h_b = [sbt(sc6, f"h_b{i}", [128, TC], BF16) for i in range(2)]
                gact = [sbt(sc6, f"gact{i}", [128, 512], F32) for i in range(2)]
                wT_load = [0, 0]
                wT_use = [0, 0]
                hb_store = [0, 0]
                gact_use = [0, 0]

                def load_wT(ig):
                    bi = ig % 2
                    if wT_use[bi]:
                        wait("sync", pe, wT_use[bi])
                    for fc in range(NF):
                        dma("sync", wgT[bi][:, fc * 512:(fc + 1) * 512],
                            wg_s[ig * 512:(ig + 1) * 512,
                                 fc * 128:(fc + 1) * 128], q_t[bi],
                            transpose=True)
                    for fc in range(NF):
                        wT_load[bi] = dma(
                            "sync", wuT[bi][:, fc * 512:(fc + 1) * 512],
                            wu_s[ig * 512:(ig + 1) * 512,
                                 fc * 128:(fc + 1) * 128], q_t[bi],
                            transpose=True)

                load_wT(0)
                load_wT(1)
                for ig in range(16):
                    bi = ig % 2
                    wait("tensor", q_t[bi], wT_load[bi])
                    ig_last = 0
                    for ic4 in range(4):
                        ic = ig * 4 + ic4
                        hbi = ic % 2
                        if hb_store[hbi]:
                            wait("vector", q_hst[hbi], hb_store[hbi])
                        t_h = 0
                        for tt in range(4):
                            bg = 2 * (tt % 2)
                            bu = bg + 1
                            pe_guard(bg)
                            pe_guard(bu)
                            g_t = u_t = 0
                            for fc in range(NF):
                                last = (fc == NF - 1)
                                lh = wgT[bi][:, fc * 512 + ic4 * 128:
                                             fc * 512 + ic4 * 128 + 128]
                                tg = op("tensor", lambda e, lh=lh, fc=fc,
                                        tt=tt, bg=bg, last=last:
                                        e.matmul(out=ps[bg][:, :], lhsT=lh,
                                                 rhs=xnT_sl(fc, tt * 512, 512),
                                                 start=(fc == 0), stop=last),
                                        inc=pe if last else None)
                                if last:
                                    g_t = tg
                            for fc in range(NF):
                                last = (fc == NF - 1)
                                lh = wuT[bi][:, fc * 512 + ic4 * 128:
                                             fc * 512 + ic4 * 128 + 128]
                                tu = op("tensor", lambda e, lh=lh, fc=fc,
                                        tt=tt, bu=bu, last=last:
                                        e.matmul(out=ps[bu][:, :], lhsT=lh,
                                                 rhs=xnT_sl(fc, tt * 512, 512),
                                                 start=(fc == 0), stop=last),
                                        inc=pe if last else None)
                                if last:
                                    u_t = tu
                            ig_last = u_t
                            gi = tt % 2
                            wait("scalar", pe, g_t)
                            if gact_use[gi]:
                                wait("scalar", vc, gact_use[gi])
                            t_ga = op("scalar", lambda e, bg=bg, gi=gi, ic=ic:
                                      e.activation(
                                          out=gact[gi][:], in_=ps[bg][:, :],
                                          func=AF.Silu,
                                          scale=sg_b[:, ic:ic + 1]), inc=sc)
                            bank_free[bg] = (sc, t_ga)
                            wait("vector", sc, t_ga)
                            wait("vector", pe, u_t)
                            t_h = op("vector",
                                     lambda e, bu=bu, gi=gi, ic=ic, hbi=hbi,
                                     tt=tt:
                                     e.scalar_tensor_tensor(
                                         out=h_b[hbi][:, tt * 512:
                                                      (tt + 1) * 512],
                                         in0=ps[bu][:, :],
                                         scalar=su_b[:, ic:ic + 1],
                                         in1=gact[gi][:],
                                         op0=OP.mult, op1=OP.mult), inc=vc)
                            gact_use[gi] = t_h
                            bank_free[bu] = (vc, t_h)
                        wait("scalar", vc, t_h)
                        hb_store[hbi] = dma(
                            "scalar", h_s[ic * 128:(ic + 1) * 128, :],
                            h_b[hbi][:], q_hst[hbi])
                    wT_use[bi] = ig_last
                    if ig + 2 < 16:
                        load_wT(ig + 2)

        fence()

        with ExitStack() as sc7:
            wdT = sbt(sc7, "wdT", [128, NIC * 1024], BF16)
            h_p = [sbt(sc7, f"h_p{i}", [128, 16 * 512], BF16) for i in range(2)]
            gf_t = [sbt(sc7, f"gf_t{i}", [128, 8 * 512], BF16)
                    for i in range(2)]
            gf_store = [0, 0]
            hp_use = [0, 0]
            t_blk = [[0] * 4, [0] * 4]
            blk_pe = [[0] * 4, [0] * 4]
            for half in range(2):
                for p4 in range(4):
                    if half == 1 and blk_pe[0][p4]:
                        wait("sync", pe, blk_pe[0][p4])
                    for icl in range(16):
                        ic = p4 * 16 + icl
                        dma("sync", wdT[:, ic * 1024:(ic + 1) * 1024],
                            wd_s[half * 1024:(half + 1) * 1024,
                                 ic * 128:(ic + 1) * 128], q_t[0],
                            transpose=True)
                    t_blk[half][p4] = q_t[0].n
                for slab in range(4):
                    sbi = slab % 2
                    dn_pe = [0] * 8
                    for piece in range(4):
                        pbi = piece % 2
                        if slab == 0:
                            wait("tensor", q_t[0], t_blk[half][piece])
                        if hp_use[pbi]:
                            wait("scalar", pe, hp_use[pbi])
                        t_hp = dma(
                            "scalar",
                            h_p[pbi][:].rearrange("p (c t) -> p c t", c=16),
                            h_s.ap().rearrange("(c p) t -> p c t", p=128)
                            [:, piece * 16:(piece + 1) * 16,
                             slab * 512:(slab + 1) * 512], q_hp[pbi])
                        wait("tensor", q_hp[pbi], t_hp)
                        if piece == 0:
                            for hc in range(8):
                                pe_guard(hc)
                        for icl in range(16):
                            ic = piece * 16 + icl
                            for hc in range(8):
                                first = (ic == 0)
                                last = (ic == NIC - 1)
                                piece_end = (icl == 15 and hc == 7)
                                want_inc = last or piece_end
                                t = op("tensor",
                                       lambda e, ic=ic, hc=hc, pbi=pbi,
                                       icl=icl, first=first, last=last:
                                       e.matmul(
                                           out=ps[hc][:, :],
                                           lhsT=wdT[:, ic * 1024 + hc * 128:
                                                    ic * 1024 + hc * 128 + 128],
                                           rhs=h_p[pbi][:, icl * 512:
                                                        (icl + 1) * 512],
                                           start=first, stop=last),
                                       inc=pe if want_inc else None)
                                if last:
                                    dn_pe[hc] = t
                                if piece_end:
                                    hp_use[pbi] = t
                                    if slab == 3:
                                        blk_pe[half][piece] = t
                    if gf_store[sbi]:
                        wait("scalar", q_gf[sbi], gf_store[sbi])
                    t_gf = 0
                    for hc in range(8):
                        wait("scalar", pe, dn_pe[hc])
                        t_gf = op("scalar", lambda e, hc=hc, half=half,
                                  sbi=sbi:
                                  e.activation(
                                      out=gf_t[sbi][:, hc * 512:
                                                    (hc + 1) * 512],
                                      in_=ps[hc][:, :], func=AF.Silu,
                                      scale=sd_b[:, half * 8 + hc:
                                                 half * 8 + hc + 1]), inc=sc)
                        bank_free[hc] = (sc, t_gf)
                    wait("scalar", sc, t_gf)
                    gf_store[sbi] = dma(
                        "scalar",
                        gf_s.ap().rearrange("(o p) t -> p o t", p=128)
                        [:, half * 8:(half + 1) * 8,
                         slab * 512:(slab + 1) * 512],
                        gf_t[sbi][:].rearrange("p (o t) -> p o t", o=8),
                        q_gf[sbi])

        fence()

        with ExitStack() as sc8:
            NB8 = 4
            q8t = qsem("q8t", "sync", NB8)
            q8x = qsem("q8x", "gpsimd", NB8)
            q8y = qsem("q8y", "scalar", NB8)
            s_t8 = [sbt(sc8, f"s_t8{i}", [128, DM], BF16) for i in range(NB8)]
            g_t8 = [sbt(sc8, f"g_t8{i}", [128, DM], BF16) for i in range(NB8)]
            x_t8 = [sbt(sc8, f"x_t8{i}", [128, DM], F32) for i in range(NB8)]
            sgp = sbt(sc8, "sgp", [128, DM], F32)
            o_t8 = [sbt(sc8, f"o_t8{i}", [128, DM], F32) for i in range(NB8)]
            tl = [0] * NB8
            xl = [0] * NB8
            ost = [0] * NB8
            ouse = [0] * NB8
            for tt in range(NF):
                bi = tt % NB8
                if ouse[bi]:
                    wait("sync", vc, ouse[bi])
                    wait("gpsimd", vc, ouse[bi])
                dma("sync", s_t8[bi][:],
                    s_s[0:DM, tt * 128:(tt + 1) * 128], q8t[bi],
                    transpose=True)
                tl[bi] = dma("sync", g_t8[bi][:],
                             gf_s[0:DM, tt * 128:(tt + 1) * 128], q8t[bi],
                             transpose=True)
                xl[bi] = dma("gpsimd", x_t8[bi][:],
                             x_in[tt * 128:(tt + 1) * 128, :], q8x[bi])
                wait("vector", q8t[bi], tl[bi])
                wait("vector", q8x[bi], xl[bi])
                if ost[bi]:
                    wait("vector", q8y[bi], ost[bi])
                op("vector", lambda e, bi=bi: e.tensor_mul(
                    out=sgp[:], in0=s_t8[bi][:], in1=g_t8[bi][:]))
                ouse[bi] = op("vector", lambda e, bi=bi: e.tensor_add(
                    out=o_t8[bi][:], in0=sgp[:], in1=x_t8[bi][:]), inc=vc)
                wait("scalar", vc, ouse[bi])
                ost[bi] = dma("scalar", y_d[tt * 128:(tt + 1) * 128, :],
                              o_t8[bi][:], q8y[bi])
            for i in range(NB8):
                wait("scalar", q8y[i], q8y[i].n)
                wait("sync", q8t[i], q8t[i].n)
                wait("gpsimd", q8x[i], q8x[i].n)

        with nc.Block() as block:
            @block.sync
            def _(e):
                for f in streams["sync"]:
                    f(e)

            @block.scalar
            def _(e):
                for f in streams["scalar"]:
                    f(e)

            @block.vector
            def _(e):
                for f in streams["vector"]:
                    f(e)

            @block.gpsimd
            def _(e):
                for f in streams["gpsimd"]:
                    f(e)

            @block.tensor
            def _(e):
                for f in streams["tensor"]:
                    f(e)

    return nc


_NC_CACHE = []


def _get_nc():
    if not _NC_CACHE:
        _NC_CACHE.append(build_nc())
    return _NC_CACHE[0]


def kernel(L0_attn_x, L0_attn_w_qkv, L0_attn_w_out, L0_attn_decay_param,
           L0_ffn_gate_w, L0_ffn_gate_scale, L0_ffn_up_w, L0_ffn_up_scale,
           L0_ffn_down_w, L0_ffn_down_scale):
    x = np.ascontiguousarray(np.asarray(L0_attn_x, dtype=np.float32))
    wqkv = np.ascontiguousarray(np.asarray(L0_attn_w_qkv, dtype=np.float32))
    wout = np.ascontiguousarray(np.asarray(L0_attn_w_out, dtype=np.float32))
    dk = np.asarray(L0_attn_decay_param, dtype=np.float32)
    gw = np.ascontiguousarray(np.asarray(L0_ffn_gate_w, dtype=np.int32))
    uw = np.ascontiguousarray(np.asarray(L0_ffn_up_w, dtype=np.int32))
    dw = np.ascontiguousarray(np.asarray(L0_ffn_down_w, dtype=np.int32))
    gs = np.asarray(L0_ffn_gate_scale, dtype=np.float32)
    us = np.asarray(L0_ffn_up_scale, dtype=np.float32)
    ds = np.asarray(L0_ffn_down_scale, dtype=np.float32)

    dk_rep = np.ascontiguousarray(np.broadcast_to(dk[None, :], (128, H)))
    sg = np.ascontiguousarray(gs.reshape(NIC, 128).T)
    su = np.ascontiguousarray(us.reshape(NIC, 128).T)
    sd = np.ascontiguousarray(ds.reshape(NF, 128).T)

    nc = _get_nc()
    in_maps = []
    for c in range(NCORES):
        b, half = c // 2, c % 2
        rm = np.zeros((128, 256), np.float32)
        if half == 1:
            r = c - 1
            rm[:, r * 32:(r + 1) * 32] = 1.0
        in_maps.append(dict(
            x_in=np.ascontiguousarray(x[b, half * TC:(half + 1) * TC, :]),
            w_qkv=wqkv, w_out=wout, decay_rep=dk_rep,
            gate_w=gw, up_w=uw, down_w=dw,
            sg=sg, su=su, sd=sd, rm=rm,
        ))
    res = run_bass_kernel_spmd(nc, in_maps, core_ids=list(range(NCORES)))
    out = np.empty((B, T, DM), np.float32)
    for c in range(NCORES):
        b, half = c // 2, c % 2
        out[b, half * TC:(half + 1) * TC, :] = res.results[c]["y"]
    return out


# version 04
# speedup vs baseline: 1.0261x; 1.0261x over previous
import numpy as np
from contextlib import ExitStack

import concourse.bass as bass
import concourse.mybir as mybir
from concourse.bass_utils import run_bass_kernel_spmd

F32 = mybir.dt.float32
BF16 = mybir.dt.bfloat16
I32 = mybir.dt.int32
AF = mybir.ActivationFunctionType
OP = mybir.AluOpType
AX = mybir.AxisListType

B, T, DM = 4, 4096, 2048
H = 16
TC = 2048
NF = DM // 128
IF = 8192
NIC = IF // 128
NCORES = 8

ENGS = ("sync", "scalar", "vector", "gpsimd", "tensor")


class Ctr:
    __slots__ = ("h", "n")

    def __init__(self, h):
        self.h = h
        self.n = 0

    def bump(self, k=1):
        self.n += k
        return self.n


def build_nc(dbg=()):
    nc = bass.Bass("TRN2", target_bir_lowering=False)

    def _skind(name):
        return "ExternalOutput" if name in dbg else "Internal"

    x_in = nc.dram_tensor("x_in", [TC, DM], F32, kind="ExternalInput")
    wqkv_d = nc.dram_tensor("w_qkv", [DM, 3 * DM], F32, kind="ExternalInput")
    wout_d = nc.dram_tensor("w_out", [DM, DM], F32, kind="ExternalInput")
    dk_d = nc.dram_tensor("decay_rep", [128, H], F32, kind="ExternalInput")
    gw_d = nc.dram_tensor("gate_w", [IF, DM], I32, kind="ExternalInput")
    uw_d = nc.dram_tensor("up_w", [IF, DM], I32, kind="ExternalInput")
    dw_d = nc.dram_tensor("down_w", [DM, IF], I32, kind="ExternalInput")
    sg_d = nc.dram_tensor("sg", [128, NIC], F32, kind="ExternalInput")
    su_d = nc.dram_tensor("su", [128, NIC], F32, kind="ExternalInput")
    sd_d = nc.dram_tensor("sd", [128, NF], F32, kind="ExternalInput")
    rm_d = nc.dram_tensor("rm", [128, 256], F32, kind="ExternalInput")

    y_d = nc.dram_tensor("y", [TC, DM], F32, kind="ExternalOutput")

    xn_s = nc.dram_tensor("xn_s", [TC, DM], BF16, kind=_skind("xn_s"))
    kv_s = nc.dram_tensor("kv_s", [2 * H * 128, TC], BF16, kind=_skind("kv_s"))
    ap_s = nc.dram_tensor("ap_s", [DM, TC], BF16, kind=_skind("ap_s"))
    s_s = nc.dram_tensor("s_s", [DM, TC], BF16, kind=_skind("s_s"))
    wg_s = nc.dram_tensor("wg_s", [IF, DM], BF16, kind=_skind("wg_s"))
    wu_s = nc.dram_tensor("wu_s", [IF, DM], BF16, kind=_skind("wu_s"))
    wd_s = nc.dram_tensor("wd_s", [DM, IF], BF16, kind=_skind("wd_s"))
    h_s = nc.dram_tensor("h_s", [IF, TC], BF16, kind=_skind("h_s"))
    gf_s = nc.dram_tensor("gf_s", [DM, TC], BF16, kind=_skind("gf_s"))
    cc_i = nc.dram_tensor("cc_i", [128, 2 * H], F32)
    cc_o = nc.dram_tensor("cc_o", [128 * NCORES, 2 * H], F32)
    st_dbg = nc.dram_tensor("st_dbg", [128, 4 * H], F32, kind=_skind("st_dbg"))

    streams = {e: [] for e in ENGS}

    def em(eng, fn):
        streams[eng].append(fn)

    def wait(eng, ctr, tgt):
        if tgt and tgt > 0:
            em(eng, lambda e, c=ctr, t=tgt: e.wait_ge(c.h, t))

    def dma(q, out, in_, ctr, transpose=False):
        t = ctr.bump(16)
        em(q, lambda e, o=out, i=in_, c=ctr, tr=transpose:
           e.dma_start(out=o, in_=i, transpose=tr).then_inc(c.h, 16))
        return t

    def op(eng, fn, inc=None):
        if inc is None:
            em(eng, fn)
            return None
        t = inc.bump(1)
        em(eng, lambda e, f=fn, c=inc: f(e).then_inc(c.h, 1))
        return t

    with ExitStack() as root:
        def sbt(scope, name, shape, dt):
            return scope.enter_context(nc.sbuf_tensor(name, shape, dt))

        def sem(name):
            return Ctr(root.enter_context(nc.semaphore(name)))

        cc = sem("cc")
        pe = sem("pe")
        vc = sem("vc")
        sc = sem("sc")
        bar = sem("bar")

        dma_ctrs = []

        def qsem(name, eng, n=1):
            out = []
            for i in range(n):
                c = sem(f"{name}{i}")
                dma_ctrs.append((eng, c))
                out.append(c)
            return out if n > 1 else out[0]

        q_c = qsem("q_c", "gpsimd")
        q_x = qsem("q_x", "gpsimd", 2)
        q_cvi = qsem("q_cvi", "gpsimd", 2)
        q_cvs = qsem("q_cvs", "scalar", 2)
        q_w = qsem("q_w", "gpsimd", 2)
        q_kst = qsem("q_kst", "scalar", 2)
        q_kv4 = qsem("q_kv4", "scalar", 2)
        q_ap = qsem("q_ap", "scalar")
        q_xn = qsem("q_xn", "scalar", 2)
        q_cc2 = qsem("q_cc2", "gpsimd")
        q_t = qsem("q_t", "sync", 2)
        q_apl = qsem("q_apl", "scalar", 2)
        q_wo = qsem("q_wo", "gpsimd")
        q_sev = qsem("q_sev", "scalar", 2)
        q_hst = qsem("q_hst", "scalar", 2)
        q_hp = qsem("q_hp", "scalar", 2)
        q_gf = qsem("q_gf", "scalar", 2)
        q_y = qsem("q_y", "scalar", 2)
        po = sem("po")

        def fence():
            for eng, c in dma_ctrs:
                wait(eng, c, c.n)
            for e in ENGS:
                em(e, lambda eng: eng.sem_inc(bar.h, 1))
            tgt = bar.bump(5)
            for e in ENGS:
                wait(e, bar, tgt)

        ps = [root.enter_context(nc.psum_tensor(f"ps{i}", [128, 512], F32))
              for i in range(8)]
        bank_free = [None] * 8

        def pe_guard(i):
            if bank_free[i] is not None:
                c, t = bank_free[i]
                wait("tensor", c, t)
                bank_free[i] = None

        dk_raw = sbt(root, "dk_raw", [128, H], F32)
        dk_rep = sbt(root, "dk_rep", [128, H], F32)
        sg_b = sbt(root, "sg_b", [128, NIC], F32)
        su_b = sbt(root, "su_b", [128, NIC], F32)
        sd_b = sbt(root, "sd_b", [128, NF], F32)
        rm_b = sbt(root, "rm_b", [128, 256], F32)
        g_b = sbt(root, "g_b", [128, 256], F32)
        gm_b = sbt(root, "gm_b", [128, 256], F32)
        ssel = sbt(root, "ssel", [128, 2 * H], F32)
        states = sbt(root, "states", [128, 2 * H], F32)
        ones_cb = sbt(root, "ones_cb", [128, 1], BF16)
        ones_rb = sbt(root, "ones_rb", [1, 128], BF16)
        recip_b = sbt(root, "recip_b", [1, TC], BF16)
        cst_b = sbt(root, "cst_b", [128, 4], F32)
        den_row = sbt(root, "den_row", [1, TC], F32)
        recip_row = sbt(root, "recip_row", [1, TC], F32)

        nc.const_aps.register(F32, -128.0, cst_b[:, 1:2])
        t_dk = dma("gpsimd", dk_raw[:], dk_d[:, :], q_c)
        for dst, src in ((sg_b, sg_d), (su_b, su_d), (sd_b, sd_d), (rm_b, rm_d)):
            dma("gpsimd", dst[:], src[:, :], q_c)
        op("vector", lambda e: e.memset(ones_cb[:], 1.0))
        op("vector", lambda e: e.memset(cst_b[:, 0:1], 1e-5))
        op("vector", lambda e: e.memset(cst_b[:, 1:2], -128.0))
        op("vector", lambda e: e.memset(cst_b[:, 2:3], -1e-6))
        op("vector", lambda e: e.memset(cst_b[:, 3:4], 1e-6))
        t_ones = op("vector", lambda e: e.memset(ones_rb[:], 1.0), inc=vc)
        wait("scalar", q_c, q_c.n)
        wait("scalar", vc, t_ones)
        t_sig = op("scalar", lambda e: e.activation(
            out=dk_rep[:], in_=dk_raw[:], func=AF.Sigmoid), inc=sc)

        with ExitStack() as scA:
            xnT = sbt(scA, "xnT", [128, NF * TC], BF16)

            def xnT_sl(fc, t0, tn):
                return xnT[:, fc * TC + t0: fc * TC + t0 + tn]

            cv_in = [sbt(scA, f"cv_in{i}", [128, DM], BF16) for i in range(2)]
            cv_out = [sbt(scA, f"cv_out{i}", [128, DM], BF16) for i in range(2)]
            cv_state = [None, None]
            units = []
            for icx in range(NIC):
                units.append((gw_d, wg_s, icx * 128, 0))
                units.append((uw_d, wu_s, icx * 128, 0))
            for j in range(64):
                units.append((dw_d, wd_s, (j // 4) * 128, (j % 4) * DM))
            unit_idx = [0]

            def emit_conv_units(k):
                for _ in range(k):
                    u = unit_idx[0]
                    if u >= len(units):
                        return
                    unit_idx[0] += 1
                    src, dst, r0, c0 = units[u]
                    bi = u % 2
                    st = cv_state[bi]
                    if st is not None:
                        wait("gpsimd", st["pc"], st["pt"])
                    t_ld = dma("gpsimd", cv_in[bi][:],
                               src[r0:r0 + 128, c0:c0 + DM], q_cvi[bi])
                    if u % 2 == 0:
                        wait("vector", q_cvi[bi], t_ld)
                        if st is not None:
                            wait("vector", q_cvs[bi], st["st"])
                        t_pass = op("vector", lambda e, bi=bi: e.tensor_scalar(
                            out=cv_out[bi][:], in0=cv_in[bi][:], scalar1=-128.0,
                            scalar2=None, op0=OP.add), inc=vc)
                        pc = vc
                    else:
                        wait("scalar", q_cvi[bi], t_ld)
                        if st is not None:
                            wait("scalar", q_cvs[bi], st["st"])
                        t_pass = op("scalar", lambda e, bi=bi: e.add(
                            out=cv_out[bi][:], in_=cv_in[bi][:], add=-128.0),
                            inc=sc)
                        pc = sc
                    wait("scalar", pc, t_pass)
                    t_st = dma("scalar", dst[r0:r0 + 128, c0:c0 + DM],
                               cv_out[bi][:], q_cvs[bi])
                    cv_state[bi] = dict(pc=pc, pt=t_pass, st=t_st)

            with ExitStack() as scL:
                xt = [sbt(scL, f"xt{i}", [128, DM], F32) for i in range(2)]
                xc = [sbt(scL, f"xc{i}", [128, DM], F32) for i in range(2)]
                sqw = [sbt(scL, f"sqw{i}", [128, DM], F32) for i in range(2)]
                s1 = [sbt(scL, f"s1_{i}", [128, 1], F32) for i in range(2)]
                s2 = [sbt(scL, f"s2_{i}", [128, 1], F32) for i in range(2)]
                s3 = [sbt(scL, f"s3_{i}", [128, 1], F32) for i in range(2)]
                xnb = [sbt(scL, f"xnb{i}", [128, DM], BF16) for i in range(2)]
                ln_st = [0, 0]
                ln_use = [0, 0]
                t_r_par = [0, 0]
                for tt in range(NF):
                    bi = tt % 2
                    if ln_use[bi]:
                        wait("gpsimd", vc, ln_use[bi])
                    t_x = dma("gpsimd", xt[bi][:],
                              x_in[tt * 128:(tt + 1) * 128, :], q_x[bi])
                    wait("vector", q_x[bi], t_x)
                    wait("vector", sc, t_r_par[bi])
                    t_sum = op("vector", lambda e, bi=bi: e.reduce_sum(
                        out=s1[bi][:], in_=xt[bi][:], axis=AX.X), inc=vc)
                    wait("scalar", vc, t_sum)
                    t_m = op("scalar", lambda e, bi=bi: e.activation(
                        out=s2[bi][:], in_=s1[bi][:], func=AF.Copy,
                        scale=1.0 / DM), inc=sc)
                    wait("vector", sc, t_m)
                    t_c = op("vector", lambda e, bi=bi: e.tensor_scalar(
                        out=xc[bi][:], in0=xt[bi][:], scalar1=s2[bi][:, 0:1],
                        scalar2=None, op0=OP.subtract), inc=vc)
                    wait("scalar", vc, t_c)
                    t_sq = op("scalar", lambda e, bi=bi: e.activation(
                        out=sqw[bi][:], in_=xc[bi][:], func=AF.Square), inc=sc)
                    wait("vector", sc, t_sq)
                    t_rs = op("vector", lambda e, bi=bi: e.reduce_sum(
                        out=s1[bi][:], in_=sqw[bi][:], axis=AX.X), inc=vc)
                    wait("scalar", vc, t_rs)
                    op("scalar", lambda e, bi=bi: e.activation(
                        out=s2[bi][:], in_=s1[bi][:], func=AF.Ln, scale=1.0 / DM,
                        bias=cst_b[:, 0:1]))
                    t_r = op("scalar", lambda e, bi=bi: e.activation(
                        out=s3[bi][:], in_=s2[bi][:], func=AF.Exp,
                        scale=-0.5), inc=sc)
                    t_r_par[bi] = t_r
                    if ln_st[bi]:
                        wait("vector", q_xn[bi], ln_st[bi])
                    wait("vector", sc, t_r)
                    t_xn = op("vector", lambda e, bi=bi: e.tensor_scalar(
                        out=xnb[bi][:], in0=xc[bi][:], scalar1=s3[bi][:, 0:1],
                        scalar2=None, op0=OP.mult), inc=vc)
                    ln_use[bi] = t_xn
                    wait("scalar", vc, t_xn)
                    ln_st[bi] = dma("scalar", xn_s[tt * 128:(tt + 1) * 128, :],
                                    xnb[bi][:], q_xn[bi])
                wait("sync", q_xn[0], q_xn[0].n)
                wait("sync", q_xn[1], q_xn[1].n)
                for fc in range(NF):
                    dma("sync", xnT_sl(fc, 0, TC),
                        xn_s[0:TC, fc * 128:(fc + 1) * 128], q_t[0],
                        transpose=True)
                t_xnT = q_t[0].n

            fence()

            with ExitStack() as scH:
                ones_t = sbt(scH, "ones_t", [128, TC], BF16)
                dk_tl = [sbt(scH, f"dk_tl{i}", [128, TC], F32) for i in range(2)]
                t_onest = op("vector", lambda e: e.memset(ones_t[:], 1.0),
                             inc=vc)
                wkv_b = [sbt(scH, f"wkv{i}", [128, 2 * NF * 128], BF16)
                         for i in range(2)]
                phik = [sbt(scH, f"phik{i}", [128, TC], BF16) for i in range(2)]
                v_b = [sbt(scH, f"vb{i}", [128, TC], BF16) for i in range(2)]
                bkv = sbt(scH, "bkv", [128, TC], F32)
                t1b = sbt(scH, "t1b", [128, 512], F32)
                t2b = [sbt(scH, f"t2b{i}", [128, 512], F32) for i in range(2)]
                q_f = sbt(scH, "q_f", [128, TC], F32)
                rkv_f = sbt(scH, "rkv_f", [128, TC], F32)
                rks_f = sbt(scH, "rks_f", [128, TC], F32)
                prod_b = sbt(scH, "prod_b", [128, TC], BF16)
                num_f = bkv
                apb = sbt(scH, "apb", [128, TC], BF16)

                QCOL, KCOL, VCOL = 0, DM, 2 * DM
                stt_busy = [0, 0]

                def emit_phi(bank_pe_t, bank, tt, dst_ap):
                    i2 = tt % 2
                    wait("scalar", pe, bank_pe_t)
                    if stt_busy[i2]:
                        wait("scalar", vc, stt_busy[i2])
                    op("scalar", lambda e, b=bank: e.activation(
                        out=t1b[:], in_=ps[b][:, :], func=AF.Relu,
                        scale=-1.0))
                    t2t = op("scalar", lambda e, i2=i2: e.activation(
                        out=t2b[i2][:], in_=t1b[:], func=AF.Exp,
                        scale=-1.0), inc=sc)
                    wait("vector", sc, t2t)
                    t = op("vector", lambda e, b=bank, i2=i2, d=dst_ap:
                           e.scalar_tensor_tensor(
                               out=d, in0=ps[b][:, :], scalar=0.0,
                               in1=t2b[i2][:], op0=OP.max, op1=OP.add), inc=vc)
                    stt_busy[i2] = t
                    bank_free[bank] = (vc, t)
                    return t

                wkv_load = [0, 0]
                wkv_inuse = [0, 0]
                kv_store = [0, 0]
                dk_fill = [0, 0]
                dk_used = [0, 0]
                st_copied = [0]

                def load_wkv(h):
                    bi = h % 2
                    if wkv_inuse[bi]:
                        wait("gpsimd", pe, wkv_inuse[bi])
                    for j, col0 in ((0, KCOL + h * 128), (1, VCOL + h * 128)):
                        wkv_load[bi] = dma(
                            "gpsimd",
                            wkv_b[bi][:, j * NF * 128:(j + 1) * NF * 128]
                            .rearrange("p (f c) -> p f c", f=NF),
                            wqkv_d[:, col0:col0 + 128]
                            .rearrange("(f p) c -> p f c", p=128), q_w[bi])

                load_wkv(0)
                load_wkv(1)

                for h in range(H):
                    bi = h % 2
                    wait("tensor", q_w[bi], wkv_load[bi])
                    if h == 0:
                        wait("tensor", q_t[0], t_xnT)
                    kv_pe = [0] * 8
                    for j, base in ((0, 0), (1, 4)):
                        for bk in range(4):
                            pe_guard(base + bk)
                        for fc in range(NF):
                            for tt in range(4):
                                b_ = base + tt
                                last = (fc == NF - 1)
                                lh = wkv_b[bi][:, (j * NF + fc) * 128:
                                               (j * NF + fc) * 128 + 128]
                                t = op("tensor",
                                       lambda e, lh=lh, fc=fc, tt=tt, b_=b_,
                                       last=last:
                                       e.matmul(out=ps[b_][:, :], lhsT=lh,
                                                rhs=xnT_sl(fc, tt * 512, 512),
                                                start=(fc == 0), stop=last),
                                       inc=pe if last else None)
                                if last:
                                    kv_pe[b_] = t
                    wkv_inuse[bi] = kv_pe[7]
                    if h + 1 < H:
                        load_wkv(h + 1)
                    if kv_store[bi]:
                        wait("vector", q_kst[bi], kv_store[bi])
                    phi_t = 0
                    for tt in range(4):
                        phi_t = emit_phi(kv_pe[tt], tt, tt,
                                         phik[bi][:, tt * 512:(tt + 1) * 512])
                    vcp_t = 0
                    for tt in range(4):
                        wait("vector", pe, kv_pe[4 + tt])
                        vcp_t = op("vector", lambda e, tt=tt, bi=bi:
                                   e.tensor_copy(
                                       out=v_b[bi][:, tt * 512:(tt + 1) * 512],
                                       in_=ps[4 + tt][:, :]), inc=vc)
                        bank_free[4 + tt] = (vc, vcp_t)
                    op("vector", lambda e, bi=bi: e.tensor_mul(
                        out=bkv[:], in0=phik[bi][:], in1=v_b[bi][:]))
                    if st_copied[0]:
                        wait("vector", sc, st_copied[0])
                    if dk_used[bi]:
                        wait("scalar", vc, dk_used[bi])
                    if h < 2:
                        wait("scalar", sc, t_sig)
                        wait("scalar", vc, t_onest)
                    dk_fill[bi] = op("scalar", lambda e, bi=bi, h=h: e.mul(
                        out=dk_tl[bi][:], in_=ones_t[:],
                        mul=dk_rep[:, h:h + 1]), inc=sc)
                    wait("vector", sc, dk_fill[bi])
                    op("vector", lambda e, bi=bi: e.tensor_tensor_scan(
                        out=rkv_f[:], data0=dk_tl[bi][:], data1=bkv[:],
                        initial=0.0, op0=OP.mult, op1=OP.add))
                    dk_used[bi] = op(
                        "vector", lambda e, bi=bi: e.tensor_tensor_scan(
                            out=rks_f[:], data0=dk_tl[bi][:], data1=phik[bi][:],
                            initial=0.0, op0=OP.mult, op1=OP.add), inc=vc)
                    wait("scalar", vc, dk_used[bi])
                    op("scalar", lambda e, h=h: e.copy(
                        out=states[:, 2 * h:2 * h + 1],
                        in_=rkv_f[:, TC - 1:TC]))
                    st_copied[0] = op("scalar", lambda e, h=h: e.copy(
                        out=states[:, 2 * h + 1:2 * h + 2],
                        in_=rks_f[:, TC - 1:TC]), inc=sc)
                    wait("scalar", vc, max(vcp_t, phi_t))
                    dma("scalar", kv_s[(2 * h) * 128:(2 * h + 1) * 128, :],
                        phik[bi][:], q_kst[bi])
                    kv_store[bi] = dma(
                        "scalar", kv_s[(2 * h + 1) * 128:(2 * h + 2) * 128, :],
                        v_b[bi][:], q_kst[bi])
                    emit_conv_units(6)

                fence()

                dma("gpsimd", cc_i[:, :], states[:], q_cc2)
                wait("gpsimd", q_cc2, q_cc2.n)
                em("gpsimd", lambda e: e.collective_compute(
                    "AllGather", OP.bypass,
                    replica_groups=[list(range(NCORES))],
                    ins=[cc_i.ap().opt()], outs=[cc_o.ap().opt()],
                ).then_inc(cc.h))
                cc.bump(1)
                wait("gpsimd", cc, 1)
                t_g = dma("gpsimd", g_b[:].rearrange("p (r s) -> p r s", r=8),
                          cc_o.ap().rearrange("(r p) s -> p r s", p=128), q_cc2)
                wait("vector", q_cc2, t_g)
                t_gm = op("vector", lambda e: e.tensor_mul(
                    out=gm_b[:], in0=g_b[:], in1=rm_b[:]), inc=vc)
                wait("scalar", vc, t_gm)
                t_gm2 = op("scalar", lambda e: e.copy(
                    out=g_b[:], in_=gm_b[:]), inc=sc)
                wait("vector", sc, t_gm2)
                t_sel = op("vector", lambda e: e.tensor_reduce(
                    out=ssel[:], in_=g_b[:].rearrange("p (r s) -> p s r", r=8),
                    axis=AX.X, op=OP.add), inc=vc)
                if "st_dbg" in dbg:
                    wait("gpsimd", vc, t_sel)
                    dma("gpsimd", st_dbg[:, 0:2 * H], states[:], q_cc2)
                    dma("gpsimd", st_dbg[:, 2 * H:4 * H], ssel[:], q_cc2)

                wq_load = [0, 0]
                wq_inuse = [0, 0]
                kv4_load = [0, 0]
                kv4_used = [0, 0]
                ap_store = 0
                prod_peuse = 0
                den_used = 0
                recip_used = 0

                def load_wq(h):
                    bi = h % 2
                    if wq_inuse[bi]:
                        wait("gpsimd", pe, wq_inuse[bi])
                    wq_load[bi] = dma(
                        "gpsimd",
                        wkv_b[bi][:, 0:NF * 128]
                        .rearrange("p (f c) -> p f c", f=NF),
                        wqkv_d[:, QCOL + h * 128:QCOL + h * 128 + 128]
                        .rearrange("(f p) c -> p f c", p=128), q_w[bi])

                def load_kv4(h):
                    bi = h % 2
                    if kv4_used[bi]:
                        wait("scalar", vc, kv4_used[bi])
                    dma("scalar", phik[bi][:],
                        kv_s[(2 * h) * 128:(2 * h + 1) * 128, :], q_kv4[bi])
                    kv4_load[bi] = dma(
                        "scalar", v_b[bi][:],
                        kv_s[(2 * h + 1) * 128:(2 * h + 2) * 128, :], q_kv4[bi])

                load_wq(0)
                load_kv4(0)
                load_wq(1)
                load_kv4(1)

                for h in range(H):
                    bi = h % 2
                    wait("tensor", q_w[bi], wq_load[bi])
                    q_pe = [0] * 4
                    for bk in range(4):
                        pe_guard(bk)
                    for fc in range(NF):
                        for tt in range(4):
                            last = (fc == NF - 1)
                            lh = wkv_b[bi][:, fc * 128:(fc + 1) * 128]
                            t = op("tensor", lambda e, lh=lh, fc=fc, tt=tt,
                                   last=last:
                                   e.matmul(out=ps[tt][:, :], lhsT=lh,
                                            rhs=xnT_sl(fc, tt * 512, 512),
                                            start=(fc == 0), stop=last),
                                   inc=pe if last else None)
                            if last:
                                q_pe[tt] = t
                    wq_inuse[bi] = q_pe[3]
                    if h + 1 < H:
                        load_wq(h + 1)
                        load_kv4(h + 1)
                    for tt in range(4):
                        emit_phi(q_pe[tt], tt, tt,
                                 q_f[:, tt * 512:(tt + 1) * 512])
                    wait("vector", q_kv4[bi], kv4_load[bi])
                    op("vector", lambda e, bi=bi: e.tensor_mul(
                        out=bkv[:], in0=phik[bi][:], in1=v_b[bi][:]))
                    if dk_used[bi]:
                        wait("scalar", vc, dk_used[bi])
                    dk_fill[bi] = op("scalar", lambda e, bi=bi, h=h: e.mul(
                        out=dk_tl[bi][:], in_=ones_t[:],
                        mul=dk_rep[:, h:h + 1]), inc=sc)
                    wait("vector", sc, dk_fill[bi])
                    op("vector", lambda e, bi=bi, h=h: e.tensor_tensor_scan(
                        out=rkv_f[:], data0=dk_tl[bi][:], data1=bkv[:],
                        initial=ssel[:, 2 * h:2 * h + 1],
                        op0=OP.mult, op1=OP.add))
                    kv4_used[bi] = op(
                        "vector", lambda e, bi=bi, h=h: e.tensor_tensor_scan(
                            out=rks_f[:], data0=dk_tl[bi][:], data1=phik[bi][:],
                            initial=ssel[:, 2 * h + 1:2 * h + 2],
                            op0=OP.mult, op1=OP.add), inc=vc)
                    dk_used[bi] = kv4_used[bi]
                    if prod_peuse:
                        wait("vector", pe, prod_peuse)
                    t_prod = op("vector", lambda e: e.tensor_mul(
                        out=prod_b[:], in0=q_f[:], in1=rks_f[:]), inc=vc)
                    wait("tensor", vc, t_prod)
                    if h == 0:
                        wait("tensor", vc, t_ones)
                    den_pe = [0] * 4
                    for c in range(4):
                        pe_guard(4 + c)
                        den_pe[c] = op("tensor", lambda e, c=c: e.matmul(
                            out=ps[4 + c][0:1, :], lhsT=ones_cb[:],
                            rhs=prod_b[:, c * 512:(c + 1) * 512],
                            start=True, stop=True), inc=pe)
                    prod_peuse = den_pe[3]
                    if recip_used:
                        wait("scalar", pe, recip_used)
                    t_dr = 0
                    for c in range(4):
                        wait("scalar", pe, den_pe[c])
                        t_dr = op("scalar", lambda e, c=c: e.activation(
                            out=den_row[0:1, c * 512:(c + 1) * 512],
                            in_=ps[4 + c][0:1, :], func=AF.Relu,
                            bias=cst_b[0:1, 2:3]), inc=sc)
                        bank_free[4 + c] = (sc, t_dr)
                    op("scalar", lambda e: e.activation(
                        out=recip_row[0:1, :], in_=den_row[0:1, :], func=AF.Ln,
                        bias=cst_b[0:1, 3:4]))
                    t_rc = op("scalar", lambda e: e.activation(
                        out=recip_b[0:1, :], in_=recip_row[0:1, :], func=AF.Exp,
                        scale=-1.0), inc=sc)
                    den_used = t_rc
                    wait("tensor", sc, t_rc)
                    rep_pe = [0] * 4
                    for c in range(4):
                        pe_guard(4 + c)
                        rep_pe[c] = op("tensor", lambda e, c=c: e.matmul(
                            out=ps[4 + c][:, :], lhsT=ones_rb[:],
                            rhs=recip_b[0:1, c * 512:(c + 1) * 512],
                            start=True, stop=True), inc=pe)
                    recip_used = rep_pe[3]
                    op("vector", lambda e: e.tensor_mul(
                        out=num_f[:], in0=q_f[:], in1=rkv_f[:]))
                    if ap_store:
                        wait("vector", q_ap, ap_store)
                    t_ap = 0
                    for c in range(4):
                        wait("vector", pe, rep_pe[c])
                        t_ap = op("vector", lambda e, c=c: e.tensor_tensor(
                            out=apb[:, c * 512:(c + 1) * 512],
                            in0=num_f[:, c * 512:(c + 1) * 512],
                            in1=ps[4 + c][:, :], op=OP.mult), inc=vc)
                        bank_free[4 + c] = (vc, t_ap)
                    wait("scalar", vc, t_ap)
                    ap_store = dma("scalar", ap_s[h * 128:(h + 1) * 128, :],
                                   apb[:], q_ap)
                    emit_conv_units(4)
                emit_conv_units(len(units))

            fence()

            with ExitStack() as sc5:
                wo_b = sbt(sc5, "wo_b", [128, NF * DM], BF16)
                ap_sl = [sbt(sc5, f"ap_sl{i}", [128, NF * 512], BF16)
                         for i in range(2)]
                sev = [sbt(sc5, f"sev{i}", [128, 512], BF16) for i in range(2)]
                for fc in range(NF):
                    dma("gpsimd", wo_b[:, fc * DM:(fc + 1) * DM],
                        wout_d[fc * 128:(fc + 1) * 128, :], q_wo)
                wait("tensor", q_wo, q_wo.n)
                ap_use = [0, 0]
                sev_store = [0, 0]
                for tt in range(4):
                    bi = tt % 2
                    if ap_use[bi]:
                        wait("scalar", pe, ap_use[bi])
                    t_apl = dma(
                        "scalar",
                        ap_sl[bi][:].rearrange("p (f t) -> p f t", f=NF),
                        ap_s.ap().rearrange("(f p) t -> p f t", p=128)
                        [:, :, tt * 512:(tt + 1) * 512], q_apl[bi])
                    wait("tensor", q_apl[bi], t_apl)
                    last_mm = 0
                    for oc in range(NF):
                        bk = oc % 2
                        si = oc % 2
                        pe_guard(bk)
                        for fc in range(NF):
                            last = (fc == NF - 1)
                            t = op("tensor", lambda e, bi=bi, fc=fc, oc=oc,
                                   bk=bk, last=last:
                                   e.matmul(
                                       out=ps[bk][:, :],
                                       lhsT=wo_b[:, fc * DM + oc * 128:
                                                 fc * DM + oc * 128 + 128],
                                       rhs=ap_sl[bi][:, fc * 512:
                                                     (fc + 1) * 512],
                                       start=(fc == 0), stop=last),
                                   inc=pe if last else None)
                            if last:
                                last_mm = t
                        wait("scalar", pe, last_mm)
                        if sev_store[si]:
                            wait("scalar", q_sev[si], sev_store[si])
                        t_ev = op("scalar", lambda e, si=si, bk=bk: e.copy(
                            out=sev[si][:], in_=ps[bk][:, :]), inc=sc)
                        bank_free[bk] = (sc, t_ev)
                        sev_store[si] = dma(
                            "scalar",
                            s_s[oc * 128:(oc + 1) * 128,
                                tt * 512:(tt + 1) * 512], sev[si][:], q_sev[si])
                    ap_use[bi] = last_mm

            fence()

            with ExitStack() as sc6:
                wgT = [sbt(sc6, f"wgT{i}", [128, NF * 512], BF16)
                       for i in range(2)]
                wuT = [sbt(sc6, f"wuT{i}", [128, NF * 512], BF16)
                       for i in range(2)]
                h_b = [sbt(sc6, f"h_b{i}", [128, TC], BF16) for i in range(2)]
                gact = [sbt(sc6, f"gact{i}", [128, 512], F32) for i in range(2)]
                wT_load = [0, 0]
                wT_use = [0, 0]
                hb_store = [0, 0]
                gact_use = [0, 0]

                def load_wT(ig):
                    bi = ig % 2
                    if wT_use[bi]:
                        wait("sync", pe, wT_use[bi])
                    for fc in range(NF):
                        dma("sync", wgT[bi][:, fc * 512:(fc + 1) * 512],
                            wg_s[ig * 512:(ig + 1) * 512,
                                 fc * 128:(fc + 1) * 128], q_t[bi],
                            transpose=True)
                    for fc in range(NF):
                        wT_load[bi] = dma(
                            "sync", wuT[bi][:, fc * 512:(fc + 1) * 512],
                            wu_s[ig * 512:(ig + 1) * 512,
                                 fc * 128:(fc + 1) * 128], q_t[bi],
                            transpose=True)

                load_wT(0)
                load_wT(1)
                for ig in range(16):
                    bi = ig % 2
                    wait("tensor", q_t[bi], wT_load[bi])
                    ig_last = 0
                    for ic4 in range(4):
                        ic = ig * 4 + ic4
                        hbi = ic % 2
                        if hb_store[hbi]:
                            wait("vector", q_hst[hbi], hb_store[hbi])
                        t_h = 0
                        for tt in range(4):
                            bg = 2 * (tt % 2)
                            bu = bg + 1
                            pe_guard(bg)
                            pe_guard(bu)
                            g_t = u_t = 0
                            for fc in range(NF):
                                last = (fc == NF - 1)
                                lh = wgT[bi][:, fc * 512 + ic4 * 128:
                                             fc * 512 + ic4 * 128 + 128]
                                tg = op("tensor", lambda e, lh=lh, fc=fc,
                                        tt=tt, bg=bg, last=last:
                                        e.matmul(out=ps[bg][:, :], lhsT=lh,
                                                 rhs=xnT_sl(fc, tt * 512, 512),
                                                 start=(fc == 0), stop=last),
                                        inc=pe if last else None)
                                if last:
                                    g_t = tg
                            for fc in range(NF):
                                last = (fc == NF - 1)
                                lh = wuT[bi][:, fc * 512 + ic4 * 128:
                                             fc * 512 + ic4 * 128 + 128]
                                tu = op("tensor", lambda e, lh=lh, fc=fc,
                                        tt=tt, bu=bu, last=last:
                                        e.matmul(out=ps[bu][:, :], lhsT=lh,
                                                 rhs=xnT_sl(fc, tt * 512, 512),
                                                 start=(fc == 0), stop=last),
                                        inc=pe if last else None)
                                if last:
                                    u_t = tu
                            ig_last = u_t
                            gi = tt % 2
                            wait("scalar", pe, g_t)
                            if gact_use[gi]:
                                wait("scalar", vc, gact_use[gi])
                            t_ga = op("scalar", lambda e, bg=bg, gi=gi, ic=ic:
                                      e.activation(
                                          out=gact[gi][:], in_=ps[bg][:, :],
                                          func=AF.Silu,
                                          scale=sg_b[:, ic:ic + 1]), inc=sc)
                            bank_free[bg] = (sc, t_ga)
                            wait("vector", sc, t_ga)
                            wait("vector", pe, u_t)
                            t_h = op("vector",
                                     lambda e, bu=bu, gi=gi, ic=ic, hbi=hbi,
                                     tt=tt:
                                     e.scalar_tensor_tensor(
                                         out=h_b[hbi][:, tt * 512:
                                                      (tt + 1) * 512],
                                         in0=ps[bu][:, :],
                                         scalar=su_b[:, ic:ic + 1],
                                         in1=gact[gi][:],
                                         op0=OP.mult, op1=OP.mult), inc=vc)
                            gact_use[gi] = t_h
                            bank_free[bu] = (vc, t_h)
                        wait("scalar", vc, t_h)
                        hb_store[hbi] = dma(
                            "scalar", h_s[ic * 128:(ic + 1) * 128, :],
                            h_b[hbi][:], q_hst[hbi])
                    wT_use[bi] = ig_last
                    if ig + 2 < 16:
                        load_wT(ig + 2)

        fence()

        with ExitStack() as sc7:
            wdT = sbt(sc7, "wdT", [128, NIC * 1024], BF16)
            h_p = [sbt(sc7, f"h_p{i}", [128, 16 * 512], BF16) for i in range(2)]
            gf_t = [sbt(sc7, f"gf_t{i}", [128, 8 * 512], BF16)
                    for i in range(2)]
            gf_store = [0, 0]
            hp_use = [0, 0]
            t_blk = [[0] * 4, [0] * 4]
            blk_pe = [[0] * 4, [0] * 4]
            for half in range(2):
                for p4 in range(4):
                    if half == 1 and blk_pe[0][p4]:
                        wait("sync", pe, blk_pe[0][p4])
                    for icl in range(16):
                        ic = p4 * 16 + icl
                        dma("sync", wdT[:, ic * 1024:(ic + 1) * 1024],
                            wd_s[half * 1024:(half + 1) * 1024,
                                 ic * 128:(ic + 1) * 128], q_t[0],
                            transpose=True)
                    t_blk[half][p4] = q_t[0].n
                for slab in range(4):
                    sbi = slab % 2
                    dn_pe = [0] * 8
                    for piece in range(4):
                        pbi = piece % 2
                        if slab == 0:
                            wait("tensor", q_t[0], t_blk[half][piece])
                        if hp_use[pbi]:
                            wait("scalar", pe, hp_use[pbi])
                        t_hp = dma(
                            "scalar",
                            h_p[pbi][:].rearrange("p (c t) -> p c t", c=16),
                            h_s.ap().rearrange("(c p) t -> p c t", p=128)
                            [:, piece * 16:(piece + 1) * 16,
                             slab * 512:(slab + 1) * 512], q_hp[pbi])
                        wait("tensor", q_hp[pbi], t_hp)
                        if piece == 0:
                            for hc in range(8):
                                pe_guard(hc)
                        for icl in range(16):
                            ic = piece * 16 + icl
                            for hc in range(8):
                                first = (ic == 0)
                                last = (ic == NIC - 1)
                                piece_end = (icl == 15 and hc == 7)
                                want_inc = last or piece_end
                                t = op("tensor",
                                       lambda e, ic=ic, hc=hc, pbi=pbi,
                                       icl=icl, first=first, last=last:
                                       e.matmul(
                                           out=ps[hc][:, :],
                                           lhsT=wdT[:, ic * 1024 + hc * 128:
                                                    ic * 1024 + hc * 128 + 128],
                                           rhs=h_p[pbi][:, icl * 512:
                                                        (icl + 1) * 512],
                                           start=first, stop=last),
                                       inc=pe if want_inc else None)
                                if last:
                                    dn_pe[hc] = t
                                if piece_end:
                                    hp_use[pbi] = t
                                    if slab == 3:
                                        blk_pe[half][piece] = t
                    if gf_store[sbi]:
                        wait("scalar", q_gf[sbi], gf_store[sbi])
                    t_gf = 0
                    for hc in range(8):
                        wait("scalar", pe, dn_pe[hc])
                        t_gf = op("scalar", lambda e, hc=hc, half=half,
                                  sbi=sbi:
                                  e.activation(
                                      out=gf_t[sbi][:, hc * 512:
                                                    (hc + 1) * 512],
                                      in_=ps[hc][:, :], func=AF.Silu,
                                      scale=sd_b[:, half * 8 + hc:
                                                 half * 8 + hc + 1]), inc=sc)
                        bank_free[hc] = (sc, t_gf)
                    wait("scalar", sc, t_gf)
                    gf_store[sbi] = dma(
                        "scalar",
                        gf_s.ap().rearrange("(o p) t -> p o t", p=128)
                        [:, half * 8:(half + 1) * 8,
                         slab * 512:(slab + 1) * 512],
                        gf_t[sbi][:].rearrange("p (o t) -> p o t", o=8),
                        q_gf[sbi])

        fence()

        with ExitStack() as sc8:
            NB8 = 4
            q8t = qsem("q8t", "sync", NB8)
            q8x = qsem("q8x", "gpsimd", NB8)
            q8y = qsem("q8y", "scalar", NB8)
            s_t8 = [sbt(sc8, f"s_t8{i}", [128, DM], BF16) for i in range(NB8)]
            g_t8 = [sbt(sc8, f"g_t8{i}", [128, DM], BF16) for i in range(NB8)]
            x_t8 = [sbt(sc8, f"x_t8{i}", [128, DM], F32) for i in range(NB8)]
            sgp = sbt(sc8, "sgp", [128, DM], F32)
            o_t8 = [sbt(sc8, f"o_t8{i}", [128, DM], F32) for i in range(NB8)]
            tl = [0] * NB8
            xl = [0] * NB8
            ost = [0] * NB8
            ouse = [0] * NB8
            for tt in range(NF):
                bi = tt % NB8
                if ouse[bi]:
                    wait("sync", vc, ouse[bi])
                    wait("gpsimd", vc, ouse[bi])
                dma("sync", s_t8[bi][:],
                    s_s[0:DM, tt * 128:(tt + 1) * 128], q8t[bi],
                    transpose=True)
                tl[bi] = dma("sync", g_t8[bi][:],
                             gf_s[0:DM, tt * 128:(tt + 1) * 128], q8t[bi],
                             transpose=True)
                xl[bi] = dma("gpsimd", x_t8[bi][:],
                             x_in[tt * 128:(tt + 1) * 128, :], q8x[bi])
                wait("vector", q8t[bi], tl[bi])
                wait("vector", q8x[bi], xl[bi])
                if ost[bi]:
                    wait("vector", q8y[bi], ost[bi])
                op("vector", lambda e, bi=bi: e.tensor_mul(
                    out=sgp[:], in0=s_t8[bi][:], in1=g_t8[bi][:]))
                ouse[bi] = op("vector", lambda e, bi=bi: e.tensor_add(
                    out=o_t8[bi][:], in0=sgp[:], in1=x_t8[bi][:]), inc=vc)
                wait("scalar", vc, ouse[bi])
                ost[bi] = dma("scalar", y_d[tt * 128:(tt + 1) * 128, :],
                              o_t8[bi][:], q8y[bi])
            for i in range(NB8):
                wait("scalar", q8y[i], q8y[i].n)
                wait("sync", q8t[i], q8t[i].n)
                wait("gpsimd", q8x[i], q8x[i].n)

        with nc.Block() as block:
            @block.sync
            def _(e):
                for f in streams["sync"]:
                    f(e)

            @block.scalar
            def _(e):
                for f in streams["scalar"]:
                    f(e)

            @block.vector
            def _(e):
                for f in streams["vector"]:
                    f(e)

            @block.gpsimd
            def _(e):
                for f in streams["gpsimd"]:
                    f(e)

            @block.tensor
            def _(e):
                for f in streams["tensor"]:
                    f(e)

    return nc


_NC_CACHE = []


def _get_nc():
    if not _NC_CACHE:
        _NC_CACHE.append(build_nc())
    return _NC_CACHE[0]


def kernel(L0_attn_x, L0_attn_w_qkv, L0_attn_w_out, L0_attn_decay_param,
           L0_ffn_gate_w, L0_ffn_gate_scale, L0_ffn_up_w, L0_ffn_up_scale,
           L0_ffn_down_w, L0_ffn_down_scale):
    x = np.ascontiguousarray(np.asarray(L0_attn_x, dtype=np.float32))
    wqkv = np.ascontiguousarray(np.asarray(L0_attn_w_qkv, dtype=np.float32))
    wout = np.ascontiguousarray(np.asarray(L0_attn_w_out, dtype=np.float32))
    dk = np.asarray(L0_attn_decay_param, dtype=np.float32)
    gw = np.ascontiguousarray(np.asarray(L0_ffn_gate_w, dtype=np.int32))
    uw = np.ascontiguousarray(np.asarray(L0_ffn_up_w, dtype=np.int32))
    dw = np.ascontiguousarray(np.asarray(L0_ffn_down_w, dtype=np.int32))
    gs = np.asarray(L0_ffn_gate_scale, dtype=np.float32)
    us = np.asarray(L0_ffn_up_scale, dtype=np.float32)
    ds = np.asarray(L0_ffn_down_scale, dtype=np.float32)

    dk_rep = np.ascontiguousarray(np.broadcast_to(dk[None, :], (128, H)))
    sg = np.ascontiguousarray(gs.reshape(NIC, 128).T)
    su = np.ascontiguousarray(us.reshape(NIC, 128).T)
    sd = np.ascontiguousarray(ds.reshape(NF, 128).T)

    nc = _get_nc()
    in_maps = []
    for c in range(NCORES):
        b, half = c // 2, c % 2
        rm = np.zeros((128, 256), np.float32)
        if half == 1:
            r = c - 1
            rm[:, r * 32:(r + 1) * 32] = 1.0
        in_maps.append(dict(
            x_in=np.ascontiguousarray(x[b, half * TC:(half + 1) * TC, :]),
            w_qkv=wqkv, w_out=wout, decay_rep=dk_rep,
            gate_w=gw, up_w=uw, down_w=dw,
            sg=sg, su=su, sd=sd, rm=rm,
        ))
    res = run_bass_kernel_spmd(nc, in_maps, core_ids=list(range(NCORES)))
    out = np.empty((B, T, DM), np.float32)
    for c in range(NCORES):
        b, half = c // 2, c % 2
        out[b, half * TC:(half + 1) * TC, :] = res.results[c]["y"]
    return out


# version 05
# speedup vs baseline: 1.0521x; 1.0253x over previous
import numpy as np
from contextlib import ExitStack

import concourse.bass as bass
import concourse.mybir as mybir
from concourse.bass_utils import run_bass_kernel_spmd

F32 = mybir.dt.float32
BF16 = mybir.dt.bfloat16
I32 = mybir.dt.int32
AF = mybir.ActivationFunctionType
OP = mybir.AluOpType
AX = mybir.AxisListType

B, T, DM = 4, 4096, 2048
H = 16
TC = 2048
NF = DM // 128
IF = 8192
NIC = IF // 128
NCORES = 8

ENGS = ("sync", "scalar", "vector", "gpsimd", "tensor")


class Ctr:
    __slots__ = ("h", "n")

    def __init__(self, h):
        self.h = h
        self.n = 0

    def bump(self, k=1):
        self.n += k
        return self.n


def build_nc(dbg=()):
    nc = bass.Bass("TRN2", target_bir_lowering=False)

    def _skind(name):
        return "ExternalOutput" if name in dbg else "Internal"

    x_in = nc.dram_tensor("x_in", [TC, DM], F32, kind="ExternalInput")
    wqkv_d = nc.dram_tensor("w_qkv", [DM, 3 * DM], F32, kind="ExternalInput")
    wout_d = nc.dram_tensor("w_out", [DM, DM], F32, kind="ExternalInput")
    dk_d = nc.dram_tensor("decay_rep", [128, H], F32, kind="ExternalInput")
    gw_d = nc.dram_tensor("gate_w", [IF, DM], I32, kind="ExternalInput")
    uw_d = nc.dram_tensor("up_w", [IF, DM], I32, kind="ExternalInput")
    dw_d = nc.dram_tensor("down_w", [DM, IF], I32, kind="ExternalInput")
    sg_d = nc.dram_tensor("sg", [128, NIC], F32, kind="ExternalInput")
    su_d = nc.dram_tensor("su", [128, NIC], F32, kind="ExternalInput")
    sd_d = nc.dram_tensor("sd", [128, NF], F32, kind="ExternalInput")
    rm_d = nc.dram_tensor("rm", [128, 256], F32, kind="ExternalInput")

    y_d = nc.dram_tensor("y", [TC, DM], F32, kind="ExternalOutput")

    xn_s = nc.dram_tensor("xn_s", [TC, DM], BF16, kind=_skind("xn_s"))
    kv_s = nc.dram_tensor("kv_s", [2 * H * 128, TC], BF16, kind=_skind("kv_s"))
    ap_s = nc.dram_tensor("ap_s", [DM, TC], BF16, kind=_skind("ap_s"))
    s_s = nc.dram_tensor("s_s", [DM, TC], BF16, kind=_skind("s_s"))
    wg_s = nc.dram_tensor("wg_s", [IF, DM], BF16, kind=_skind("wg_s"))
    wu_s = nc.dram_tensor("wu_s", [IF, DM], BF16, kind=_skind("wu_s"))
    wd_s = nc.dram_tensor("wd_s", [DM, IF], BF16, kind=_skind("wd_s"))
    h_s = nc.dram_tensor("h_s", [IF, TC], BF16, kind=_skind("h_s"))
    gf_s = nc.dram_tensor("gf_s", [DM, TC], BF16, kind=_skind("gf_s"))
    cc_i = nc.dram_tensor("cc_i", [128, 2 * H], F32)
    cc_o = nc.dram_tensor("cc_o", [128 * NCORES, 2 * H], F32)
    st_dbg = nc.dram_tensor("st_dbg", [128, 4 * H], F32, kind=_skind("st_dbg"))

    streams = {e: [] for e in ENGS}

    def em(eng, fn):
        streams[eng].append(fn)

    def wait(eng, ctr, tgt):
        if tgt and tgt > 0:
            em(eng, lambda e, c=ctr, t=tgt: e.wait_ge(c.h, t))

    def dma(q, out, in_, ctr, transpose=False):
        t = ctr.bump(16)
        em(q, lambda e, o=out, i=in_, c=ctr, tr=transpose:
           e.dma_start(out=o, in_=i, transpose=tr).then_inc(c.h, 16))
        return t

    def op(eng, fn, inc=None):
        if inc is None:
            em(eng, fn)
            return None
        t = inc.bump(1)
        em(eng, lambda e, f=fn, c=inc: f(e).then_inc(c.h, 1))
        return t

    with ExitStack() as root:
        def sbt(scope, name, shape, dt):
            return scope.enter_context(nc.sbuf_tensor(name, shape, dt))

        def sem(name):
            return Ctr(root.enter_context(nc.semaphore(name)))

        cc = sem("cc")
        pe = sem("pe")
        vc = sem("vc")
        sc = sem("sc")
        bar = sem("bar")

        dma_ctrs = []

        def qsem(name, eng, n=1):
            out = []
            for i in range(n):
                c = sem(f"{name}{i}")
                dma_ctrs.append((eng, c))
                out.append(c)
            return out if n > 1 else out[0]

        q_c = qsem("q_c", "gpsimd")
        q_x = qsem("q_x", "gpsimd", 2)
        q_cvi = qsem("q_cvi", "gpsimd", 2)
        q_cvs = qsem("q_cvs", "scalar", 2)
        q_w = qsem("q_w", "gpsimd", 2)
        q_kst = qsem("q_kst", "scalar", 2)
        q_kv4 = qsem("q_kv4", "scalar", 2)
        q_ap = qsem("q_ap", "scalar")
        q_xn = qsem("q_xn", "scalar", 2)
        q_cc2 = qsem("q_cc2", "gpsimd")
        q_t = qsem("q_t", "sync", 2)
        q_apl = qsem("q_apl", "scalar", 2)
        q_wo = qsem("q_wo", "gpsimd")
        q_sev = qsem("q_sev", "scalar", 2)
        q_hst = qsem("q_hst", "scalar", 2)
        q_hp = qsem("q_hp", "scalar", 2)
        q_gf = qsem("q_gf", "scalar", 2)
        q_y = qsem("q_y", "scalar", 2)
        po = sem("po")

        def fence():
            for eng, c in dma_ctrs:
                wait(eng, c, c.n)
            for e in ENGS:
                em(e, lambda eng: eng.sem_inc(bar.h, 1))
            tgt = bar.bump(5)
            for e in ENGS:
                wait(e, bar, tgt)

        ps = [root.enter_context(nc.psum_tensor(f"ps{i}", [128, 512], F32))
              for i in range(8)]
        bank_free = [None] * 8

        def pe_guard(i):
            if bank_free[i] is not None:
                c, t = bank_free[i]
                wait("tensor", c, t)
                bank_free[i] = None

        dk_raw = sbt(root, "dk_raw", [128, H], F32)
        dk_rep = sbt(root, "dk_rep", [128, H], F32)
        sg_b = sbt(root, "sg_b", [128, NIC], F32)
        su_b = sbt(root, "su_b", [128, NIC], F32)
        sd_b = sbt(root, "sd_b", [128, NF], F32)
        rm_b = sbt(root, "rm_b", [128, 256], F32)
        g_b = sbt(root, "g_b", [128, 256], F32)
        gm_b = sbt(root, "gm_b", [128, 256], F32)
        ssel = sbt(root, "ssel", [128, 2 * H], F32)
        states = sbt(root, "states", [128, 2 * H], F32)
        ones_cb = sbt(root, "ones_cb", [128, 1], BF16)
        ones_rb = sbt(root, "ones_rb", [1, 128], BF16)
        recip_b = sbt(root, "recip_b", [1, TC], BF16)
        cst_b = sbt(root, "cst_b", [128, 4], F32)
        den_row = sbt(root, "den_row", [1, TC], F32)
        recip_row = sbt(root, "recip_row", [1, TC], F32)

        nc.const_aps.register(F32, -128.0, cst_b[:, 1:2])
        t_dk = dma("gpsimd", dk_raw[:], dk_d[:, :], q_c)
        for dst, src in ((sg_b, sg_d), (su_b, su_d), (sd_b, sd_d), (rm_b, rm_d)):
            dma("gpsimd", dst[:], src[:, :], q_c)
        op("vector", lambda e: e.memset(ones_cb[:], 1.0))
        op("vector", lambda e: e.memset(cst_b[:, 0:1], 1e-5))
        op("vector", lambda e: e.memset(cst_b[:, 1:2], -128.0))
        op("vector", lambda e: e.memset(cst_b[:, 2:3], -1e-6))
        op("vector", lambda e: e.memset(cst_b[:, 3:4], 1e-6))
        t_ones = op("vector", lambda e: e.memset(ones_rb[:], 1.0), inc=vc)
        wait("scalar", q_c, q_c.n)
        wait("scalar", vc, t_ones)
        t_sig = op("scalar", lambda e: e.activation(
            out=dk_rep[:], in_=dk_raw[:], func=AF.Sigmoid), inc=sc)

        with ExitStack() as scA:
            xnT = sbt(scA, "xnT", [128, NF * TC], BF16)

            def xnT_sl(fc, t0, tn):
                return xnT[:, fc * TC + t0: fc * TC + t0 + tn]

            cv_in = [sbt(scA, f"cv_in{i}", [128, DM], BF16) for i in range(2)]
            cv_out = [sbt(scA, f"cv_out{i}", [128, DM], BF16) for i in range(2)]
            cv_state = [None, None]
            units = []
            for icx in range(NIC):
                units.append((gw_d, wg_s, icx * 128, 0))
                units.append((uw_d, wu_s, icx * 128, 0))
            for j in range(64):
                units.append((dw_d, wd_s, (j // 4) * 128, (j % 4) * DM))
            unit_idx = [0]
            unit_cvs = []

            def emit_conv_units(k):
                for _ in range(k):
                    u = unit_idx[0]
                    if u >= len(units):
                        return
                    unit_idx[0] += 1
                    src, dst, r0, c0 = units[u]
                    bi = u % 2
                    st = cv_state[bi]
                    if st is not None:
                        wait("gpsimd", st["pc"], st["pt"])
                    t_ld = dma("gpsimd", cv_in[bi][:],
                               src[r0:r0 + 128, c0:c0 + DM], q_cvi[bi])
                    if u % 2 == 0:
                        wait("vector", q_cvi[bi], t_ld)
                        if st is not None:
                            wait("vector", q_cvs[bi], st["st"])
                        t_pass = op("vector", lambda e, bi=bi: e.tensor_scalar(
                            out=cv_out[bi][:], in0=cv_in[bi][:], scalar1=-128.0,
                            scalar2=None, op0=OP.add), inc=vc)
                        pc = vc
                    else:
                        wait("scalar", q_cvi[bi], t_ld)
                        if st is not None:
                            wait("scalar", q_cvs[bi], st["st"])
                        t_pass = op("scalar", lambda e, bi=bi: e.add(
                            out=cv_out[bi][:], in_=cv_in[bi][:], add=-128.0),
                            inc=sc)
                        pc = sc
                    wait("scalar", pc, t_pass)
                    t_st = dma("scalar", dst[r0:r0 + 128, c0:c0 + DM],
                               cv_out[bi][:], q_cvs[bi])
                    cv_state[bi] = dict(pc=pc, pt=t_pass, st=t_st)
                    unit_cvs.append((q_cvs[0].n, q_cvs[1].n))

            with ExitStack() as scL:
                xt = [sbt(scL, f"xt{i}", [128, DM], F32) for i in range(2)]
                xc = [sbt(scL, f"xc{i}", [128, DM], F32) for i in range(2)]
                sqw = [sbt(scL, f"sqw{i}", [128, DM], F32) for i in range(2)]
                s1 = [sbt(scL, f"s1_{i}", [128, 1], F32) for i in range(2)]
                s2 = [sbt(scL, f"s2_{i}", [128, 1], F32) for i in range(2)]
                s3 = [sbt(scL, f"s3_{i}", [128, 1], F32) for i in range(2)]
                xnb = [sbt(scL, f"xnb{i}", [128, DM], BF16) for i in range(2)]
                ln_st = [0, 0]
                ln_use = [0, 0]
                t_r_par = [0, 0]
                for tt in range(NF):
                    bi = tt % 2
                    if ln_use[bi]:
                        wait("gpsimd", vc, ln_use[bi])
                    t_x = dma("gpsimd", xt[bi][:],
                              x_in[tt * 128:(tt + 1) * 128, :], q_x[bi])
                    wait("vector", q_x[bi], t_x)
                    wait("vector", sc, t_r_par[bi])
                    t_sum = op("vector", lambda e, bi=bi: e.reduce_sum(
                        out=s1[bi][:], in_=xt[bi][:], axis=AX.X), inc=vc)
                    wait("scalar", vc, t_sum)
                    t_m = op("scalar", lambda e, bi=bi: e.activation(
                        out=s2[bi][:], in_=s1[bi][:], func=AF.Copy,
                        scale=1.0 / DM), inc=sc)
                    wait("vector", sc, t_m)
                    t_c = op("vector", lambda e, bi=bi: e.tensor_scalar(
                        out=xc[bi][:], in0=xt[bi][:], scalar1=s2[bi][:, 0:1],
                        scalar2=None, op0=OP.subtract), inc=vc)
                    wait("scalar", vc, t_c)
                    t_sq = op("scalar", lambda e, bi=bi: e.activation(
                        out=sqw[bi][:], in_=xc[bi][:], func=AF.Square), inc=sc)
                    wait("vector", sc, t_sq)
                    t_rs = op("vector", lambda e, bi=bi: e.reduce_sum(
                        out=s1[bi][:], in_=sqw[bi][:], axis=AX.X), inc=vc)
                    wait("scalar", vc, t_rs)
                    op("scalar", lambda e, bi=bi: e.activation(
                        out=s2[bi][:], in_=s1[bi][:], func=AF.Ln, scale=1.0 / DM,
                        bias=cst_b[:, 0:1]))
                    t_r = op("scalar", lambda e, bi=bi: e.activation(
                        out=s3[bi][:], in_=s2[bi][:], func=AF.Exp,
                        scale=-0.5), inc=sc)
                    t_r_par[bi] = t_r
                    if ln_st[bi]:
                        wait("vector", q_xn[bi], ln_st[bi])
                    wait("vector", sc, t_r)
                    t_xn = op("vector", lambda e, bi=bi: e.tensor_scalar(
                        out=xnb[bi][:], in0=xc[bi][:], scalar1=s3[bi][:, 0:1],
                        scalar2=None, op0=OP.mult), inc=vc)
                    ln_use[bi] = t_xn
                    wait("scalar", vc, t_xn)
                    ln_st[bi] = dma("scalar", xn_s[tt * 128:(tt + 1) * 128, :],
                                    xnb[bi][:], q_xn[bi])
                wait("sync", q_xn[0], q_xn[0].n)
                wait("sync", q_xn[1], q_xn[1].n)
                for fc in range(NF):
                    dma("sync", xnT_sl(fc, 0, TC),
                        xn_s[0:TC, fc * 128:(fc + 1) * 128], q_t[0],
                        transpose=True)
                t_xnT = q_t[0].n

            fence()

            with ExitStack() as scH:
                ones_t = sbt(scH, "ones_t", [128, TC], BF16)
                dk_tl = [sbt(scH, f"dk_tl{i}", [128, TC], F32) for i in range(2)]
                t_onest = op("vector", lambda e: e.memset(ones_t[:], 1.0),
                             inc=vc)
                wkv_b = [sbt(scH, f"wkv{i}", [128, 2 * NF * 128], BF16)
                         for i in range(2)]
                phik = [sbt(scH, f"phik{i}", [128, TC], BF16) for i in range(2)]
                v_b = [sbt(scH, f"vb{i}", [128, TC], BF16) for i in range(2)]
                bkv = sbt(scH, "bkv", [128, TC], F32)
                t1b = sbt(scH, "t1b", [128, 512], F32)
                t2b = [sbt(scH, f"t2b{i}", [128, 512], F32) for i in range(2)]
                q_f = sbt(scH, "q_f", [128, TC], F32)
                rkv_f = sbt(scH, "rkv_f", [128, TC], F32)
                rks_f = sbt(scH, "rks_f", [128, TC], F32)
                prod_b = sbt(scH, "prod_b", [128, TC], BF16)
                num_f = bkv
                apb = sbt(scH, "apb", [128, TC], BF16)

                QCOL, KCOL, VCOL = 0, DM, 2 * DM
                stt_busy = [0, 0]

                def emit_phi(bank_pe_t, bank, tt, dst_ap):
                    i2 = tt % 2
                    wait("scalar", pe, bank_pe_t)
                    if stt_busy[i2]:
                        wait("scalar", vc, stt_busy[i2])
                    op("scalar", lambda e, b=bank: e.activation(
                        out=t1b[:], in_=ps[b][:, :], func=AF.Relu,
                        scale=-1.0))
                    t2t = op("scalar", lambda e, i2=i2: e.activation(
                        out=t2b[i2][:], in_=t1b[:], func=AF.Exp,
                        scale=-1.0), inc=sc)
                    wait("vector", sc, t2t)
                    t = op("vector", lambda e, b=bank, i2=i2, d=dst_ap:
                           e.scalar_tensor_tensor(
                               out=d, in0=ps[b][:, :], scalar=0.0,
                               in1=t2b[i2][:], op0=OP.max, op1=OP.add), inc=vc)
                    stt_busy[i2] = t
                    bank_free[bank] = (vc, t)
                    return t

                wkv_load = [0, 0]
                wkv_inuse = [0, 0]
                kv_store = [0, 0]
                dk_fill = [0, 0]
                dk_used = [0, 0]
                st_copied = [0]

                def load_wkv(h):
                    bi = h % 2
                    if wkv_inuse[bi]:
                        wait("gpsimd", pe, wkv_inuse[bi])
                    for j, col0 in ((0, KCOL + h * 128), (1, VCOL + h * 128)):
                        wkv_load[bi] = dma(
                            "gpsimd",
                            wkv_b[bi][:, j * NF * 128:(j + 1) * NF * 128]
                            .rearrange("p (f c) -> p f c", f=NF),
                            wqkv_d[:, col0:col0 + 128]
                            .rearrange("(f p) c -> p f c", p=128), q_w[bi])

                load_wkv(0)
                load_wkv(1)

                for h in range(H):
                    bi = h % 2
                    wait("tensor", q_w[bi], wkv_load[bi])
                    if h == 0:
                        wait("tensor", q_t[0], t_xnT)
                    kv_pe = [0] * 8
                    for j, base in ((0, 0), (1, 4)):
                        for bk in range(4):
                            pe_guard(base + bk)
                        for fc in range(NF):
                            for tt in range(4):
                                b_ = base + tt
                                last = (fc == NF - 1)
                                lh = wkv_b[bi][:, (j * NF + fc) * 128:
                                               (j * NF + fc) * 128 + 128]
                                t = op("tensor",
                                       lambda e, lh=lh, fc=fc, tt=tt, b_=b_,
                                       last=last:
                                       e.matmul(out=ps[b_][:, :], lhsT=lh,
                                                rhs=xnT_sl(fc, tt * 512, 512),
                                                start=(fc == 0), stop=last),
                                       inc=pe if last else None)
                                if last:
                                    kv_pe[b_] = t
                    wkv_inuse[bi] = kv_pe[7]
                    if h + 1 < H:
                        load_wkv(h + 1)
                    if kv_store[bi]:
                        wait("vector", q_kst[bi], kv_store[bi])
                    phi_t = 0
                    for tt in range(4):
                        phi_t = emit_phi(kv_pe[tt], tt, tt,
                                         phik[bi][:, tt * 512:(tt + 1) * 512])
                    vcp_t = 0
                    for tt in range(4):
                        wait("vector", pe, kv_pe[4 + tt])
                        vcp_t = op("vector", lambda e, tt=tt, bi=bi:
                                   e.tensor_copy(
                                       out=v_b[bi][:, tt * 512:(tt + 1) * 512],
                                       in_=ps[4 + tt][:, :]), inc=vc)
                        bank_free[4 + tt] = (vc, vcp_t)
                    op("vector", lambda e, bi=bi: e.tensor_mul(
                        out=bkv[:], in0=phik[bi][:], in1=v_b[bi][:]))
                    if st_copied[0]:
                        wait("vector", sc, st_copied[0])
                    if dk_used[bi]:
                        wait("scalar", vc, dk_used[bi])
                    if h < 2:
                        wait("scalar", sc, t_sig)
                        wait("scalar", vc, t_onest)
                    dk_fill[bi] = op("scalar", lambda e, bi=bi, h=h: e.mul(
                        out=dk_tl[bi][:], in_=ones_t[:],
                        mul=dk_rep[:, h:h + 1]), inc=sc)
                    wait("vector", sc, dk_fill[bi])
                    op("vector", lambda e, bi=bi: e.tensor_tensor_scan(
                        out=rkv_f[:], data0=dk_tl[bi][:], data1=bkv[:],
                        initial=0.0, op0=OP.mult, op1=OP.add))
                    dk_used[bi] = op(
                        "vector", lambda e, bi=bi: e.tensor_tensor_scan(
                            out=rks_f[:], data0=dk_tl[bi][:], data1=phik[bi][:],
                            initial=0.0, op0=OP.mult, op1=OP.add), inc=vc)
                    wait("scalar", vc, dk_used[bi])
                    op("scalar", lambda e, h=h: e.copy(
                        out=states[:, 2 * h:2 * h + 1],
                        in_=rkv_f[:, TC - 1:TC]))
                    st_copied[0] = op("scalar", lambda e, h=h: e.copy(
                        out=states[:, 2 * h + 1:2 * h + 2],
                        in_=rks_f[:, TC - 1:TC]), inc=sc)
                    wait("scalar", vc, max(vcp_t, phi_t))
                    dma("scalar", kv_s[(2 * h) * 128:(2 * h + 1) * 128, :],
                        phik[bi][:], q_kst[bi])
                    kv_store[bi] = dma(
                        "scalar", kv_s[(2 * h + 1) * 128:(2 * h + 2) * 128, :],
                        v_b[bi][:], q_kst[bi])
                    emit_conv_units(2)

                fence()

                dma("gpsimd", cc_i[:, :], states[:], q_cc2)
                wait("gpsimd", q_cc2, q_cc2.n)
                em("gpsimd", lambda e: e.collective_compute(
                    "AllGather", OP.bypass,
                    replica_groups=[list(range(NCORES))],
                    ins=[cc_i.ap().opt()], outs=[cc_o.ap().opt()],
                ).then_inc(cc.h))
                cc.bump(1)
                wait("gpsimd", cc, 1)
                t_g = dma("gpsimd", g_b[:].rearrange("p (r s) -> p r s", r=8),
                          cc_o.ap().rearrange("(r p) s -> p r s", p=128), q_cc2)
                wait("vector", q_cc2, t_g)
                t_gm = op("vector", lambda e: e.tensor_mul(
                    out=gm_b[:], in0=g_b[:], in1=rm_b[:]), inc=vc)
                wait("scalar", vc, t_gm)
                t_gm2 = op("scalar", lambda e: e.copy(
                    out=g_b[:], in_=gm_b[:]), inc=sc)
                wait("vector", sc, t_gm2)
                t_sel = op("vector", lambda e: e.tensor_reduce(
                    out=ssel[:], in_=g_b[:].rearrange("p (r s) -> p s r", r=8),
                    axis=AX.X, op=OP.add), inc=vc)
                if "st_dbg" in dbg:
                    wait("gpsimd", vc, t_sel)
                    dma("gpsimd", st_dbg[:, 0:2 * H], states[:], q_cc2)
                    dma("gpsimd", st_dbg[:, 2 * H:4 * H], ssel[:], q_cc2)

                wq_load = [0, 0]
                wq_inuse = [0, 0]
                kv4_load = [0, 0]
                kv4_used = [0, 0]
                ap_store = 0
                prod_peuse = 0
                den_used = 0
                recip_used = 0

                def load_wq(h):
                    bi = h % 2
                    if wq_inuse[bi]:
                        wait("gpsimd", pe, wq_inuse[bi])
                    wq_load[bi] = dma(
                        "gpsimd",
                        wkv_b[bi][:, 0:NF * 128]
                        .rearrange("p (f c) -> p f c", f=NF),
                        wqkv_d[:, QCOL + h * 128:QCOL + h * 128 + 128]
                        .rearrange("(f p) c -> p f c", p=128), q_w[bi])

                def load_kv4(h):
                    bi = h % 2
                    if kv4_used[bi]:
                        wait("scalar", vc, kv4_used[bi])
                    dma("scalar", phik[bi][:],
                        kv_s[(2 * h) * 128:(2 * h + 1) * 128, :], q_kv4[bi])
                    kv4_load[bi] = dma(
                        "scalar", v_b[bi][:],
                        kv_s[(2 * h + 1) * 128:(2 * h + 2) * 128, :], q_kv4[bi])

                load_wq(0)
                load_kv4(0)
                load_wq(1)
                load_kv4(1)

                for h in range(H):
                    bi = h % 2
                    wait("tensor", q_w[bi], wq_load[bi])
                    q_pe = [0] * 4
                    for bk in range(4):
                        pe_guard(bk)
                    for fc in range(NF):
                        for tt in range(4):
                            last = (fc == NF - 1)
                            lh = wkv_b[bi][:, fc * 128:(fc + 1) * 128]
                            t = op("tensor", lambda e, lh=lh, fc=fc, tt=tt,
                                   last=last:
                                   e.matmul(out=ps[tt][:, :], lhsT=lh,
                                            rhs=xnT_sl(fc, tt * 512, 512),
                                            start=(fc == 0), stop=last),
                                   inc=pe if last else None)
                            if last:
                                q_pe[tt] = t
                    wq_inuse[bi] = q_pe[3]
                    if h + 1 < H:
                        load_wq(h + 1)
                        load_kv4(h + 1)
                    for tt in range(4):
                        emit_phi(q_pe[tt], tt, tt,
                                 q_f[:, tt * 512:(tt + 1) * 512])
                    wait("vector", q_kv4[bi], kv4_load[bi])
                    op("vector", lambda e, bi=bi: e.tensor_mul(
                        out=bkv[:], in0=phik[bi][:], in1=v_b[bi][:]))
                    if dk_used[bi]:
                        wait("scalar", vc, dk_used[bi])
                    dk_fill[bi] = op("scalar", lambda e, bi=bi, h=h: e.mul(
                        out=dk_tl[bi][:], in_=ones_t[:],
                        mul=dk_rep[:, h:h + 1]), inc=sc)
                    wait("vector", sc, dk_fill[bi])
                    op("vector", lambda e, bi=bi, h=h: e.tensor_tensor_scan(
                        out=rkv_f[:], data0=dk_tl[bi][:], data1=bkv[:],
                        initial=ssel[:, 2 * h:2 * h + 1],
                        op0=OP.mult, op1=OP.add))
                    kv4_used[bi] = op(
                        "vector", lambda e, bi=bi, h=h: e.tensor_tensor_scan(
                            out=rks_f[:], data0=dk_tl[bi][:], data1=phik[bi][:],
                            initial=ssel[:, 2 * h + 1:2 * h + 2],
                            op0=OP.mult, op1=OP.add), inc=vc)
                    dk_used[bi] = kv4_used[bi]
                    if prod_peuse:
                        wait("vector", pe, prod_peuse)
                    t_prod = op("vector", lambda e: e.tensor_mul(
                        out=prod_b[:], in0=q_f[:], in1=rks_f[:]), inc=vc)
                    wait("tensor", vc, t_prod)
                    if h == 0:
                        wait("tensor", vc, t_ones)
                    den_pe = [0] * 4
                    for c in range(4):
                        pe_guard(4 + c)
                        den_pe[c] = op("tensor", lambda e, c=c: e.matmul(
                            out=ps[4 + c][0:1, :], lhsT=ones_cb[:],
                            rhs=prod_b[:, c * 512:(c + 1) * 512],
                            start=True, stop=True), inc=pe)
                    prod_peuse = den_pe[3]
                    if recip_used:
                        wait("scalar", pe, recip_used)
                    t_dr = 0
                    for c in range(4):
                        wait("scalar", pe, den_pe[c])
                        t_dr = op("scalar", lambda e, c=c: e.activation(
                            out=den_row[0:1, c * 512:(c + 1) * 512],
                            in_=ps[4 + c][0:1, :], func=AF.Relu,
                            bias=cst_b[0:1, 2:3]), inc=sc)
                        bank_free[4 + c] = (sc, t_dr)
                    op("scalar", lambda e: e.activation(
                        out=recip_row[0:1, :], in_=den_row[0:1, :], func=AF.Ln,
                        bias=cst_b[0:1, 3:4]))
                    t_rc = op("scalar", lambda e: e.activation(
                        out=recip_b[0:1, :], in_=recip_row[0:1, :], func=AF.Exp,
                        scale=-1.0), inc=sc)
                    den_used = t_rc
                    wait("tensor", sc, t_rc)
                    rep_pe = [0] * 4
                    for c in range(4):
                        pe_guard(4 + c)
                        rep_pe[c] = op("tensor", lambda e, c=c: e.matmul(
                            out=ps[4 + c][:, :], lhsT=ones_rb[:],
                            rhs=recip_b[0:1, c * 512:(c + 1) * 512],
                            start=True, stop=True), inc=pe)
                    recip_used = rep_pe[3]
                    op("vector", lambda e: e.tensor_mul(
                        out=num_f[:], in0=q_f[:], in1=rkv_f[:]))
                    if ap_store:
                        wait("vector", q_ap, ap_store)
                    t_ap = 0
                    for c in range(4):
                        wait("vector", pe, rep_pe[c])
                        t_ap = op("vector", lambda e, c=c: e.tensor_tensor(
                            out=apb[:, c * 512:(c + 1) * 512],
                            in0=num_f[:, c * 512:(c + 1) * 512],
                            in1=ps[4 + c][:, :], op=OP.mult), inc=vc)
                        bank_free[4 + c] = (vc, t_ap)
                    wait("scalar", vc, t_ap)
                    ap_store = dma("scalar", ap_s[h * 128:(h + 1) * 128, :],
                                   apb[:], q_ap)
                    emit_conv_units(2)

            fence()

            with ExitStack() as sc5:
                wo_b = sbt(sc5, "wo_b", [128, NF * DM], BF16)
                ap_sl = [sbt(sc5, f"ap_sl{i}", [128, NF * 512], BF16)
                         for i in range(2)]
                sev = [sbt(sc5, f"sev{i}", [128, 512], BF16) for i in range(2)]
                for fc in range(NF):
                    dma("gpsimd", wo_b[:, fc * DM:(fc + 1) * DM],
                        wout_d[fc * 128:(fc + 1) * 128, :], q_wo)
                wait("tensor", q_wo, q_wo.n)
                ap_use = [0, 0]
                sev_store = [0, 0]
                for tt in range(4):
                    bi = tt % 2
                    if ap_use[bi]:
                        wait("scalar", pe, ap_use[bi])
                    t_apl = dma(
                        "scalar",
                        ap_sl[bi][:].rearrange("p (f t) -> p f t", f=NF),
                        ap_s.ap().rearrange("(f p) t -> p f t", p=128)
                        [:, :, tt * 512:(tt + 1) * 512], q_apl[bi])
                    wait("tensor", q_apl[bi], t_apl)
                    last_mm = 0
                    for oc in range(NF):
                        bk = oc % 2
                        si = oc % 2
                        pe_guard(bk)
                        for fc in range(NF):
                            last = (fc == NF - 1)
                            t = op("tensor", lambda e, bi=bi, fc=fc, oc=oc,
                                   bk=bk, last=last:
                                   e.matmul(
                                       out=ps[bk][:, :],
                                       lhsT=wo_b[:, fc * DM + oc * 128:
                                                 fc * DM + oc * 128 + 128],
                                       rhs=ap_sl[bi][:, fc * 512:
                                                     (fc + 1) * 512],
                                       start=(fc == 0), stop=last),
                                   inc=pe if last else None)
                            if last:
                                last_mm = t
                        wait("scalar", pe, last_mm)
                        if sev_store[si]:
                            wait("scalar", q_sev[si], sev_store[si])
                        t_ev = op("scalar", lambda e, si=si, bk=bk: e.copy(
                            out=sev[si][:], in_=ps[bk][:, :]), inc=sc)
                        bank_free[bk] = (sc, t_ev)
                        sev_store[si] = dma(
                            "scalar",
                            s_s[oc * 128:(oc + 1) * 128,
                                tt * 512:(tt + 1) * 512], sev[si][:], q_sev[si])
                    ap_use[bi] = last_mm
                    emit_conv_units(4)

            fence()

            with ExitStack() as sc6:
                wgT = [sbt(sc6, f"wgT{i}", [128, NF * 512], BF16)
                       for i in range(2)]
                wuT = [sbt(sc6, f"wuT{i}", [128, NF * 512], BF16)
                       for i in range(2)]
                h_b = [sbt(sc6, f"h_b{i}", [128, TC], BF16) for i in range(2)]
                gact = [sbt(sc6, f"gact{i}", [128, 512], F32) for i in range(2)]
                wT_load = [0, 0]
                wT_use = [0, 0]
                hb_store = [0, 0]
                gact_use = [0, 0]

                def load_wT(ig):
                    bi = ig % 2
                    if wT_use[bi]:
                        wait("sync", pe, wT_use[bi])
                    u_need = 8 * ig + 7
                    if u_need < len(unit_cvs):
                        wait("sync", q_cvs[0], unit_cvs[u_need][0])
                        wait("sync", q_cvs[1], unit_cvs[u_need][1])
                    else:
                        wait("sync", q_cvs[0], unit_cvs[-1][0])
                        wait("sync", q_cvs[1], unit_cvs[-1][1])
                    for fc in range(NF):
                        dma("sync", wgT[bi][:, fc * 512:(fc + 1) * 512],
                            wg_s[ig * 512:(ig + 1) * 512,
                                 fc * 128:(fc + 1) * 128], q_t[bi],
                            transpose=True)
                    for fc in range(NF):
                        wT_load[bi] = dma(
                            "sync", wuT[bi][:, fc * 512:(fc + 1) * 512],
                            wu_s[ig * 512:(ig + 1) * 512,
                                 fc * 128:(fc + 1) * 128], q_t[bi],
                            transpose=True)

                load_wT(0)
                load_wT(1)
                for ig in range(16):
                    bi = ig % 2
                    wait("tensor", q_t[bi], wT_load[bi])
                    ig_last = 0
                    for ic4 in range(4):
                        ic = ig * 4 + ic4
                        hbi = ic % 2
                        if hb_store[hbi]:
                            wait("vector", q_hst[hbi], hb_store[hbi])
                        t_h = 0
                        for tt in range(4):
                            bg = 2 * (tt % 2)
                            bu = bg + 1
                            pe_guard(bg)
                            pe_guard(bu)
                            g_t = u_t = 0
                            for fc in range(NF):
                                last = (fc == NF - 1)
                                lh = wgT[bi][:, fc * 512 + ic4 * 128:
                                             fc * 512 + ic4 * 128 + 128]
                                tg = op("tensor", lambda e, lh=lh, fc=fc,
                                        tt=tt, bg=bg, last=last:
                                        e.matmul(out=ps[bg][:, :], lhsT=lh,
                                                 rhs=xnT_sl(fc, tt * 512, 512),
                                                 start=(fc == 0), stop=last),
                                        inc=pe if last else None)
                                if last:
                                    g_t = tg
                            for fc in range(NF):
                                last = (fc == NF - 1)
                                lh = wuT[bi][:, fc * 512 + ic4 * 128:
                                             fc * 512 + ic4 * 128 + 128]
                                tu = op("tensor", lambda e, lh=lh, fc=fc,
                                        tt=tt, bu=bu, last=last:
                                        e.matmul(out=ps[bu][:, :], lhsT=lh,
                                                 rhs=xnT_sl(fc, tt * 512, 512),
                                                 start=(fc == 0), stop=last),
                                        inc=pe if last else None)
                                if last:
                                    u_t = tu
                            ig_last = u_t
                            gi = tt % 2
                            wait("scalar", pe, g_t)
                            if gact_use[gi]:
                                wait("scalar", vc, gact_use[gi])
                            t_ga = op("scalar", lambda e, bg=bg, gi=gi, ic=ic:
                                      e.activation(
                                          out=gact[gi][:], in_=ps[bg][:, :],
                                          func=AF.Silu,
                                          scale=sg_b[:, ic:ic + 1]), inc=sc)
                            bank_free[bg] = (sc, t_ga)
                            wait("vector", sc, t_ga)
                            wait("vector", pe, u_t)
                            t_h = op("vector",
                                     lambda e, bu=bu, gi=gi, ic=ic, hbi=hbi,
                                     tt=tt:
                                     e.scalar_tensor_tensor(
                                         out=h_b[hbi][:, tt * 512:
                                                      (tt + 1) * 512],
                                         in0=ps[bu][:, :],
                                         scalar=su_b[:, ic:ic + 1],
                                         in1=gact[gi][:],
                                         op0=OP.mult, op1=OP.mult), inc=vc)
                            gact_use[gi] = t_h
                            bank_free[bu] = (vc, t_h)
                        wait("scalar", vc, t_h)
                        hb_store[hbi] = dma(
                            "scalar", h_s[ic * 128:(ic + 1) * 128, :],
                            h_b[hbi][:], q_hst[hbi])
                    wT_use[bi] = ig_last
                    emit_conv_units(7)
                    if ig + 2 < 16:
                        load_wT(ig + 2)
                emit_conv_units(len(units))

        fence()

        with ExitStack() as sc7:
            wdT = sbt(sc7, "wdT", [128, NIC * 1024], BF16)
            h_p = [sbt(sc7, f"h_p{i}", [128, 16 * 512], BF16) for i in range(2)]
            gf_t = [sbt(sc7, f"gf_t{i}", [128, 8 * 512], BF16)
                    for i in range(2)]
            gf_store = [0, 0]
            hp_use = [0, 0]
            t_blk = [[0] * 4, [0] * 4]
            blk_pe = [[0] * 4, [0] * 4]
            for half in range(2):
                for p4 in range(4):
                    if half == 1 and blk_pe[0][p4]:
                        wait("sync", pe, blk_pe[0][p4])
                    for icl in range(16):
                        ic = p4 * 16 + icl
                        dma("sync", wdT[:, ic * 1024:(ic + 1) * 1024],
                            wd_s[half * 1024:(half + 1) * 1024,
                                 ic * 128:(ic + 1) * 128], q_t[0],
                            transpose=True)
                    t_blk[half][p4] = q_t[0].n
                for slab in range(4):
                    sbi = slab % 2
                    dn_pe = [0] * 8
                    for piece in range(4):
                        pbi = piece % 2
                        if slab == 0:
                            wait("tensor", q_t[0], t_blk[half][piece])
                        if hp_use[pbi]:
                            wait("scalar", pe, hp_use[pbi])
                        t_hp = dma(
                            "scalar",
                            h_p[pbi][:].rearrange("p (c t) -> p c t", c=16),
                            h_s.ap().rearrange("(c p) t -> p c t", p=128)
                            [:, piece * 16:(piece + 1) * 16,
                             slab * 512:(slab + 1) * 512], q_hp[pbi])
                        wait("tensor", q_hp[pbi], t_hp)
                        if piece == 0:
                            for hc in range(8):
                                pe_guard(hc)
                        for icl in range(16):
                            ic = piece * 16 + icl
                            for hc in range(8):
                                first = (ic == 0)
                                last = (ic == NIC - 1)
                                piece_end = (icl == 15 and hc == 7)
                                want_inc = last or piece_end
                                t = op("tensor",
                                       lambda e, ic=ic, hc=hc, pbi=pbi,
                                       icl=icl, first=first, last=last:
                                       e.matmul(
                                           out=ps[hc][:, :],
                                           lhsT=wdT[:, ic * 1024 + hc * 128:
                                                    ic * 1024 + hc * 128 + 128],
                                           rhs=h_p[pbi][:, icl * 512:
                                                        (icl + 1) * 512],
                                           start=first, stop=last),
                                       inc=pe if want_inc else None)
                                if last:
                                    dn_pe[hc] = t
                                if piece_end:
                                    hp_use[pbi] = t
                                    if slab == 3:
                                        blk_pe[half][piece] = t
                    if gf_store[sbi]:
                        wait("scalar", q_gf[sbi], gf_store[sbi])
                    t_gf = 0
                    for hc in range(8):
                        wait("scalar", pe, dn_pe[hc])
                        t_gf = op("scalar", lambda e, hc=hc, half=half,
                                  sbi=sbi:
                                  e.activation(
                                      out=gf_t[sbi][:, hc * 512:
                                                    (hc + 1) * 512],
                                      in_=ps[hc][:, :], func=AF.Silu,
                                      scale=sd_b[:, half * 8 + hc:
                                                 half * 8 + hc + 1]), inc=sc)
                        bank_free[hc] = (sc, t_gf)
                    wait("scalar", sc, t_gf)
                    gf_store[sbi] = dma(
                        "scalar",
                        gf_s.ap().rearrange("(o p) t -> p o t", p=128)
                        [:, half * 8:(half + 1) * 8,
                         slab * 512:(slab + 1) * 512],
                        gf_t[sbi][:].rearrange("p (o t) -> p o t", o=8),
                        q_gf[sbi])

        fence()

        with ExitStack() as sc8:
            NB8 = 4
            q8t = qsem("q8t", "sync", NB8)
            q8x = qsem("q8x", "gpsimd", NB8)
            q8y = qsem("q8y", "scalar", NB8)
            s_t8 = [sbt(sc8, f"s_t8{i}", [128, DM], BF16) for i in range(NB8)]
            g_t8 = [sbt(sc8, f"g_t8{i}", [128, DM], BF16) for i in range(NB8)]
            x_t8 = [sbt(sc8, f"x_t8{i}", [128, DM], F32) for i in range(NB8)]
            sgp = sbt(sc8, "sgp", [128, DM], F32)
            o_t8 = [sbt(sc8, f"o_t8{i}", [128, DM], F32) for i in range(NB8)]
            tl = [0] * NB8
            xl = [0] * NB8
            ost = [0] * NB8
            ouse = [0] * NB8
            for tt in range(NF):
                bi = tt % NB8
                if ouse[bi]:
                    wait("sync", vc, ouse[bi])
                    wait("gpsimd", vc, ouse[bi])
                dma("sync", s_t8[bi][:],
                    s_s[0:DM, tt * 128:(tt + 1) * 128], q8t[bi],
                    transpose=True)
                tl[bi] = dma("sync", g_t8[bi][:],
                             gf_s[0:DM, tt * 128:(tt + 1) * 128], q8t[bi],
                             transpose=True)
                xl[bi] = dma("gpsimd", x_t8[bi][:],
                             x_in[tt * 128:(tt + 1) * 128, :], q8x[bi])
                wait("vector", q8t[bi], tl[bi])
                wait("vector", q8x[bi], xl[bi])
                if ost[bi]:
                    wait("vector", q8y[bi], ost[bi])
                op("vector", lambda e, bi=bi: e.tensor_mul(
                    out=sgp[:], in0=s_t8[bi][:], in1=g_t8[bi][:]))
                ouse[bi] = op("vector", lambda e, bi=bi: e.tensor_add(
                    out=o_t8[bi][:], in0=sgp[:], in1=x_t8[bi][:]), inc=vc)
                wait("scalar", vc, ouse[bi])
                ost[bi] = dma("scalar", y_d[tt * 128:(tt + 1) * 128, :],
                              o_t8[bi][:], q8y[bi])
            for i in range(NB8):
                wait("scalar", q8y[i], q8y[i].n)
                wait("sync", q8t[i], q8t[i].n)
                wait("gpsimd", q8x[i], q8x[i].n)

        with nc.Block() as block:
            @block.sync
            def _(e):
                for f in streams["sync"]:
                    f(e)

            @block.scalar
            def _(e):
                for f in streams["scalar"]:
                    f(e)

            @block.vector
            def _(e):
                for f in streams["vector"]:
                    f(e)

            @block.gpsimd
            def _(e):
                for f in streams["gpsimd"]:
                    f(e)

            @block.tensor
            def _(e):
                for f in streams["tensor"]:
                    f(e)

    return nc


_NC_CACHE = []


def _get_nc():
    if not _NC_CACHE:
        _NC_CACHE.append(build_nc())
    return _NC_CACHE[0]


def kernel(L0_attn_x, L0_attn_w_qkv, L0_attn_w_out, L0_attn_decay_param,
           L0_ffn_gate_w, L0_ffn_gate_scale, L0_ffn_up_w, L0_ffn_up_scale,
           L0_ffn_down_w, L0_ffn_down_scale):
    x = np.ascontiguousarray(np.asarray(L0_attn_x, dtype=np.float32))
    wqkv = np.ascontiguousarray(np.asarray(L0_attn_w_qkv, dtype=np.float32))
    wout = np.ascontiguousarray(np.asarray(L0_attn_w_out, dtype=np.float32))
    dk = np.asarray(L0_attn_decay_param, dtype=np.float32)
    gw = np.ascontiguousarray(np.asarray(L0_ffn_gate_w, dtype=np.int32))
    uw = np.ascontiguousarray(np.asarray(L0_ffn_up_w, dtype=np.int32))
    dw = np.ascontiguousarray(np.asarray(L0_ffn_down_w, dtype=np.int32))
    gs = np.asarray(L0_ffn_gate_scale, dtype=np.float32)
    us = np.asarray(L0_ffn_up_scale, dtype=np.float32)
    ds = np.asarray(L0_ffn_down_scale, dtype=np.float32)

    dk_rep = np.ascontiguousarray(np.broadcast_to(dk[None, :], (128, H)))
    sg = np.ascontiguousarray(gs.reshape(NIC, 128).T)
    su = np.ascontiguousarray(us.reshape(NIC, 128).T)
    sd = np.ascontiguousarray(ds.reshape(NF, 128).T)

    nc = _get_nc()
    in_maps = []
    for c in range(NCORES):
        b, half = c // 2, c % 2
        rm = np.zeros((128, 256), np.float32)
        if half == 1:
            r = c - 1
            rm[:, r * 32:(r + 1) * 32] = 1.0
        in_maps.append(dict(
            x_in=np.ascontiguousarray(x[b, half * TC:(half + 1) * TC, :]),
            w_qkv=wqkv, w_out=wout, decay_rep=dk_rep,
            gate_w=gw, up_w=uw, down_w=dw,
            sg=sg, su=su, sd=sd, rm=rm,
        ))
    res = run_bass_kernel_spmd(nc, in_maps, core_ids=list(range(NCORES)))
    out = np.empty((B, T, DM), np.float32)
    for c in range(NCORES):
        b, half = c // 2, c % 2
        out[b, half * TC:(half + 1) * TC, :] = res.results[c]["y"]
    return out


# version 07
# speedup vs baseline: 1.1500x; 1.0931x over previous
import numpy as np
from contextlib import ExitStack

import concourse.bass as bass
import concourse.mybir as mybir
from concourse.bass_utils import run_bass_kernel_spmd

F32 = mybir.dt.float32
BF16 = mybir.dt.bfloat16
I32 = mybir.dt.int32
AF = mybir.ActivationFunctionType
OP = mybir.AluOpType
AX = mybir.AxisListType

B, T, DM = 4, 4096, 2048
H = 16
TC = 2048
NF = DM // 128
IF = 8192
NIC = IF // 128
NCORES = 8

ENGS = ("sync", "scalar", "vector", "gpsimd", "tensor")


class Ctr:
    __slots__ = ("h", "n")

    def __init__(self, h):
        self.h = h
        self.n = 0

    def bump(self, k=1):
        self.n += k
        return self.n


def build_nc(dbg=()):
    nc = bass.Bass("TRN2", target_bir_lowering=False)

    def _skind(name):
        return "ExternalOutput" if name in dbg else "Internal"

    x_in = nc.dram_tensor("x_in", [TC, DM], F32, kind="ExternalInput")
    wqkv_d = nc.dram_tensor("w_qkv", [DM, 3 * DM], F32, kind="ExternalInput")
    wout_d = nc.dram_tensor("w_out", [DM, DM], F32, kind="ExternalInput")
    dk_d = nc.dram_tensor("decay_rep", [128, H], F32, kind="ExternalInput")
    U8 = mybir.dt.uint8
    gw_d = nc.dram_tensor("gate_wT", [DM, IF], U8, kind="ExternalInput")
    uw_d = nc.dram_tensor("up_wT", [DM, IF], U8, kind="ExternalInput")
    dw_d = nc.dram_tensor("down_wT", [IF, DM], U8, kind="ExternalInput")
    sg_d = nc.dram_tensor("sg", [128, NIC], F32, kind="ExternalInput")
    su_d = nc.dram_tensor("su", [128, NIC], F32, kind="ExternalInput")
    sd_d = nc.dram_tensor("sd", [128, NF], F32, kind="ExternalInput")
    rm_d = nc.dram_tensor("rm", [128, 256], F32, kind="ExternalInput")

    y_d = nc.dram_tensor("y", [TC, DM], F32, kind="ExternalOutput")

    xn_s = nc.dram_tensor("xn_s", [TC, DM], BF16, kind=_skind("xn_s"))
    kv_s = nc.dram_tensor("kv_s", [2 * H * 128, TC], BF16, kind=_skind("kv_s"))
    ap_s = nc.dram_tensor("ap_s", [DM, TC], BF16, kind=_skind("ap_s"))
    s_s = nc.dram_tensor("s_s", [DM, TC], BF16, kind=_skind("s_s"))
    h_s = nc.dram_tensor("h_s", [IF, TC], BF16, kind=_skind("h_s"))
    gf_s = nc.dram_tensor("gf_s", [DM, TC], BF16, kind=_skind("gf_s"))
    cc_i = nc.dram_tensor("cc_i", [128, 2 * H], F32)
    cc_o = nc.dram_tensor("cc_o", [128 * NCORES, 2 * H], F32)
    st_dbg = nc.dram_tensor("st_dbg", [128, 4 * H], F32, kind=_skind("st_dbg"))

    streams = {e: [] for e in ENGS}

    def em(eng, fn):
        streams[eng].append(fn)

    def wait(eng, ctr, tgt):
        if tgt and tgt > 0:
            em(eng, lambda e, c=ctr, t=tgt: e.wait_ge(c.h, t))

    def dma(q, out, in_, ctr, transpose=False):
        t = ctr.bump(16)
        em(q, lambda e, o=out, i=in_, c=ctr, tr=transpose:
           e.dma_start(out=o, in_=i, transpose=tr).then_inc(c.h, 16))
        return t

    def op(eng, fn, inc=None):
        if inc is None:
            em(eng, fn)
            return None
        t = inc.bump(1)
        em(eng, lambda e, f=fn, c=inc: f(e).then_inc(c.h, 1))
        return t

    with ExitStack() as root:
        def sbt(scope, name, shape, dt):
            return scope.enter_context(nc.sbuf_tensor(name, shape, dt))

        def sem(name):
            return Ctr(root.enter_context(nc.semaphore(name)))

        cc = sem("cc")
        pe = sem("pe")
        vc = sem("vc")
        sc = sem("sc")
        bar = sem("bar")

        dma_ctrs = []

        def qsem(name, eng, n=1):
            out = []
            for i in range(n):
                c = sem(f"{name}{i}")
                dma_ctrs.append((eng, c))
                out.append(c)
            return out if n > 1 else out[0]

        q_c = qsem("q_c", "gpsimd")
        q_x = qsem("q_x", "gpsimd", 2)
        q_wug = qsem("q_wug", "gpsimd", 3)
        q_wuu = qsem("q_wuu", "gpsimd", 3)
        q_w = qsem("q_w", "gpsimd", 2)
        q_kst = qsem("q_kst", "scalar", 2)
        q_kv4 = qsem("q_kv4", "sync", 2)
        q_ap = qsem("q_ap", "scalar")
        q_xn = qsem("q_xn", "scalar", 2)
        q_cc2 = qsem("q_cc2", "gpsimd")
        q_t = qsem("q_t", "sync", 2)
        q_apl = qsem("q_apl", "scalar", 2)
        q_wo = qsem("q_wo", "gpsimd")
        q_sev = qsem("q_sev", "scalar", 2)
        q_hst = qsem("q_hst", "scalar", 2)
        q_hp = qsem("q_hp", "scalar", 2)
        q_gf = qsem("q_gf", "scalar", 2)
        q_y = qsem("q_y", "scalar", 2)
        po = sem("po")

        def fence():
            for eng, c in dma_ctrs:
                wait(eng, c, c.n)
            for e in ENGS:
                em(e, lambda eng: eng.sem_inc(bar.h, 1))
            tgt = bar.bump(5)
            for e in ENGS:
                wait(e, bar, tgt)

        ps = [root.enter_context(nc.psum_tensor(f"ps{i}", [128, 512], F32))
              for i in range(8)]
        bank_free = [None] * 8

        def pe_guard(i):
            if bank_free[i] is not None:
                c, t = bank_free[i]
                wait("tensor", c, t)
                bank_free[i] = None

        dk_raw = sbt(root, "dk_raw", [128, H], F32)
        dk_rep = sbt(root, "dk_rep", [128, H], F32)
        sg_b = sbt(root, "sg_b", [128, NIC], F32)
        su_b = sbt(root, "su_b", [128, NIC], F32)
        sd_b = sbt(root, "sd_b", [128, NF], F32)
        rm_b = sbt(root, "rm_b", [128, 256], F32)
        g_b = sbt(root, "g_b", [128, 256], F32)
        gm_b = sbt(root, "gm_b", [128, 256], F32)
        ssel = sbt(root, "ssel", [128, 2 * H], F32)
        states = sbt(root, "states", [128, 2 * H], F32)
        ones_cb = sbt(root, "ones_cb", [128, 1], BF16)
        ones_rb = sbt(root, "ones_rb", [1, 128], BF16)
        recip_b = sbt(root, "recip_b", [1, TC], BF16)
        cst_b = sbt(root, "cst_b", [128, 4], F32)
        den_row = sbt(root, "den_row", [1, TC], F32)
        recip_row = sbt(root, "recip_row", [1, TC], F32)

        nc.const_aps.register(F32, -128.0, cst_b[:, 1:2])
        t_dk = dma("gpsimd", dk_raw[:], dk_d[:, :], q_c)
        for dst, src in ((sg_b, sg_d), (su_b, su_d), (sd_b, sd_d), (rm_b, rm_d)):
            dma("gpsimd", dst[:], src[:, :], q_c)
        op("vector", lambda e: e.memset(ones_cb[:], 1.0))
        op("vector", lambda e: e.memset(cst_b[:, 0:1], 1e-5))
        op("vector", lambda e: e.memset(cst_b[:, 1:2], -128.0))
        op("vector", lambda e: e.memset(cst_b[:, 2:3], -1e-6))
        op("vector", lambda e: e.memset(cst_b[:, 3:4], 1e-6))
        t_ones = op("vector", lambda e: e.memset(ones_rb[:], 1.0), inc=vc)
        wait("scalar", q_c, q_c.n)
        wait("scalar", vc, t_ones)
        t_sig = op("scalar", lambda e: e.activation(
            out=dk_rep[:], in_=dk_raw[:], func=AF.Sigmoid), inc=sc)

        with ExitStack() as scA:
            xnT = sbt(scA, "xnT", [128, NF * TC], BF16)

            def xnT_sl(fc, t0, tn):
                return xnT[:, fc * TC + t0: fc * TC + t0 + tn]

            with ExitStack() as scL:
                xt = [sbt(scL, f"xt{i}", [128, DM], F32) for i in range(2)]
                xc = [sbt(scL, f"xc{i}", [128, DM], F32) for i in range(2)]
                sqw = [sbt(scL, f"sqw{i}", [128, DM], F32) for i in range(2)]
                s1 = [sbt(scL, f"s1_{i}", [128, 1], F32) for i in range(2)]
                s2 = [sbt(scL, f"s2_{i}", [128, 1], F32) for i in range(2)]
                s3 = [sbt(scL, f"s3_{i}", [128, 1], F32) for i in range(2)]
                xnb = [sbt(scL, f"xnb{i}", [128, DM], BF16) for i in range(2)]
                ln_st = [0, 0]
                ln_use = [0, 0]
                t_r_par = [0, 0]
                for tt in range(NF):
                    bi = tt % 2
                    if ln_use[bi]:
                        wait("gpsimd", vc, ln_use[bi])
                    t_x = dma("gpsimd", xt[bi][:],
                              x_in[tt * 128:(tt + 1) * 128, :], q_x[bi])
                    wait("vector", q_x[bi], t_x)
                    wait("vector", sc, t_r_par[bi])
                    t_sum = op("vector", lambda e, bi=bi: e.reduce_sum(
                        out=s1[bi][:], in_=xt[bi][:], axis=AX.X), inc=vc)
                    wait("scalar", vc, t_sum)
                    t_m = op("scalar", lambda e, bi=bi: e.activation(
                        out=s2[bi][:], in_=s1[bi][:], func=AF.Copy,
                        scale=1.0 / DM), inc=sc)
                    wait("vector", sc, t_m)
                    t_c = op("vector", lambda e, bi=bi: e.tensor_scalar(
                        out=xc[bi][:], in0=xt[bi][:], scalar1=s2[bi][:, 0:1],
                        scalar2=None, op0=OP.subtract), inc=vc)
                    wait("scalar", vc, t_c)
                    t_sq = op("scalar", lambda e, bi=bi: e.activation(
                        out=sqw[bi][:], in_=xc[bi][:], func=AF.Square), inc=sc)
                    wait("vector", sc, t_sq)
                    t_rs = op("vector", lambda e, bi=bi: e.reduce_sum(
                        out=s1[bi][:], in_=sqw[bi][:], axis=AX.X), inc=vc)
                    wait("scalar", vc, t_rs)
                    op("scalar", lambda e, bi=bi: e.activation(
                        out=s2[bi][:], in_=s1[bi][:], func=AF.Ln, scale=1.0 / DM,
                        bias=cst_b[:, 0:1]))
                    t_r = op("scalar", lambda e, bi=bi: e.activation(
                        out=s3[bi][:], in_=s2[bi][:], func=AF.Exp,
                        scale=-0.5), inc=sc)
                    t_r_par[bi] = t_r
                    if ln_st[bi]:
                        wait("vector", q_xn[bi], ln_st[bi])
                    wait("vector", sc, t_r)
                    t_xn = op("vector", lambda e, bi=bi: e.tensor_scalar(
                        out=xnb[bi][:], in0=xc[bi][:], scalar1=s3[bi][:, 0:1],
                        scalar2=None, op0=OP.mult), inc=vc)
                    ln_use[bi] = t_xn
                    wait("scalar", vc, t_xn)
                    ln_st[bi] = dma("scalar", xn_s[tt * 128:(tt + 1) * 128, :],
                                    xnb[bi][:], q_xn[bi])
                wait("sync", q_xn[0], q_xn[0].n)
                wait("sync", q_xn[1], q_xn[1].n)
                for fc in range(NF):
                    dma("sync", xnT_sl(fc, 0, TC),
                        xn_s[0:TC, fc * 128:(fc + 1) * 128], q_t[0],
                        transpose=True)
                t_xnT = q_t[0].n

            fence()

            with ExitStack() as scH:
                ones_t = sbt(scH, "ones_t", [128, TC], BF16)
                dk_tl = [sbt(scH, f"dk_tl{i}", [128, TC], F32) for i in range(2)]
                t_onest = op("vector", lambda e: e.memset(ones_t[:], 1.0),
                             inc=vc)
                wkv_b = [sbt(scH, f"wkv{i}", [128, 2 * NF * 128], BF16)
                         for i in range(2)]
                phik = [sbt(scH, f"phik{i}", [128, TC], BF16) for i in range(2)]
                v_b = [sbt(scH, f"vb{i}", [128, TC], BF16) for i in range(2)]
                bkv = sbt(scH, "bkv", [128, TC], F32)
                t1b = sbt(scH, "t1b", [128, 512], F32)
                t2b = [sbt(scH, f"t2b{i}", [128, 512], F32) for i in range(2)]
                q_f = sbt(scH, "q_f", [128, TC], F32)
                rkv_f = sbt(scH, "rkv_f", [128, TC], F32)
                rks_f = sbt(scH, "rks_f", [128, TC], F32)
                prod_b = sbt(scH, "prod_b", [128, TC], BF16)
                num_f = bkv
                apb = sbt(scH, "apb", [128, TC], BF16)

                QCOL, KCOL, VCOL = 0, DM, 2 * DM
                stt_busy = [0, 0]

                def emit_phi(bank_pe_t, bank, tt, dst_ap):
                    i2 = tt % 2
                    wait("scalar", pe, bank_pe_t)
                    if stt_busy[i2]:
                        wait("scalar", vc, stt_busy[i2])
                    op("scalar", lambda e, b=bank: e.activation(
                        out=t1b[:], in_=ps[b][:, :], func=AF.Relu,
                        scale=-1.0))
                    t2t = op("scalar", lambda e, i2=i2: e.activation(
                        out=t2b[i2][:], in_=t1b[:], func=AF.Exp,
                        scale=-1.0), inc=sc)
                    wait("vector", sc, t2t)
                    t = op("vector", lambda e, b=bank, i2=i2, d=dst_ap:
                           e.scalar_tensor_tensor(
                               out=d, in0=ps[b][:, :], scalar=0.0,
                               in1=t2b[i2][:], op0=OP.max, op1=OP.add), inc=vc)
                    stt_busy[i2] = t
                    bank_free[bank] = (vc, t)
                    return t

                wkv_load = [0, 0]
                wkv_inuse = [0, 0]
                kv_store = [0, 0]
                dk_fill = [0, 0]
                dk_used = [0, 0]
                st_copied = [0]

                def load_wkv(h):
                    bi = h % 2
                    if wkv_inuse[bi]:
                        wait("gpsimd", pe, wkv_inuse[bi])
                    for j, col0 in ((0, KCOL + h * 128), (1, VCOL + h * 128)):
                        wkv_load[bi] = dma(
                            "gpsimd",
                            wkv_b[bi][:, j * NF * 128:(j + 1) * NF * 128]
                            .rearrange("p (f c) -> p f c", f=NF),
                            wqkv_d[:, col0:col0 + 128]
                            .rearrange("(f p) c -> p f c", p=128), q_w[bi])

                load_wkv(0)
                load_wkv(1)

                for h in range(H):
                    bi = h % 2
                    wait("tensor", q_w[bi], wkv_load[bi])
                    if h == 0:
                        wait("tensor", q_t[0], t_xnT)
                    kv_pe = [0] * 8
                    for j, base in ((0, 0), (1, 4)):
                        for bk in range(4):
                            pe_guard(base + bk)
                        for fc in range(NF):
                            for tt in range(4):
                                b_ = base + tt
                                last = (fc == NF - 1)
                                lh = wkv_b[bi][:, (j * NF + fc) * 128:
                                               (j * NF + fc) * 128 + 128]
                                t = op("tensor",
                                       lambda e, lh=lh, fc=fc, tt=tt, b_=b_,
                                       last=last:
                                       e.matmul(out=ps[b_][:, :], lhsT=lh,
                                                rhs=xnT_sl(fc, tt * 512, 512),
                                                start=(fc == 0), stop=last),
                                       inc=pe if last else None)
                                if last:
                                    kv_pe[b_] = t
                    wkv_inuse[bi] = kv_pe[7]
                    if h + 1 < H:
                        load_wkv(h + 1)
                    if kv_store[bi]:
                        wait("vector", q_kst[bi], kv_store[bi])
                    phi_t = 0
                    for tt in range(4):
                        phi_t = emit_phi(kv_pe[tt], tt, tt,
                                         phik[bi][:, tt * 512:(tt + 1) * 512])
                    vcp_t = 0
                    for tt in range(4):
                        wait("vector", pe, kv_pe[4 + tt])
                        vcp_t = op("vector", lambda e, tt=tt, bi=bi:
                                   e.tensor_copy(
                                       out=v_b[bi][:, tt * 512:(tt + 1) * 512],
                                       in_=ps[4 + tt][:, :]), inc=vc)
                        bank_free[4 + tt] = (vc, vcp_t)
                    op("vector", lambda e, bi=bi: e.tensor_mul(
                        out=bkv[:], in0=phik[bi][:], in1=v_b[bi][:]))
                    if st_copied[0]:
                        wait("vector", sc, st_copied[0])
                    if dk_used[bi]:
                        wait("scalar", vc, dk_used[bi])
                    if h < 2:
                        wait("scalar", sc, t_sig)
                        wait("scalar", vc, t_onest)
                    dk_fill[bi] = op("scalar", lambda e, bi=bi, h=h: e.mul(
                        out=dk_tl[bi][:], in_=ones_t[:],
                        mul=dk_rep[:, h:h + 1]), inc=sc)
                    wait("vector", sc, dk_fill[bi])
                    op("vector", lambda e, bi=bi: e.tensor_tensor_scan(
                        out=rkv_f[:], data0=dk_tl[bi][:], data1=bkv[:],
                        initial=0.0, op0=OP.mult, op1=OP.add))
                    dk_used[bi] = op(
                        "vector", lambda e, bi=bi: e.tensor_tensor_scan(
                            out=rks_f[:], data0=dk_tl[bi][:], data1=phik[bi][:],
                            initial=0.0, op0=OP.mult, op1=OP.add), inc=vc)
                    wait("scalar", vc, dk_used[bi])
                    op("scalar", lambda e, h=h: e.copy(
                        out=states[:, 2 * h:2 * h + 1],
                        in_=rkv_f[:, TC - 1:TC]))
                    st_copied[0] = op("scalar", lambda e, h=h: e.copy(
                        out=states[:, 2 * h + 1:2 * h + 2],
                        in_=rks_f[:, TC - 1:TC]), inc=sc)
                    wait("scalar", vc, max(vcp_t, phi_t))
                    dma("scalar", kv_s[(2 * h) * 128:(2 * h + 1) * 128, :],
                        phik[bi][:], q_kst[bi])
                    kv_store[bi] = dma(
                        "scalar", kv_s[(2 * h + 1) * 128:(2 * h + 2) * 128, :],
                        v_b[bi][:], q_kst[bi])

                fence()

                dma("gpsimd", cc_i[:, :], states[:], q_cc2)
                wait("gpsimd", q_cc2, q_cc2.n)
                em("gpsimd", lambda e: e.collective_compute(
                    "AllGather", OP.bypass,
                    replica_groups=[list(range(NCORES))],
                    ins=[cc_i.ap().opt()], outs=[cc_o.ap().opt()],
                ).then_inc(cc.h))
                cc.bump(1)
                wait("gpsimd", cc, 1)
                t_g = dma("gpsimd", g_b[:].rearrange("p (r s) -> p r s", r=8),
                          cc_o.ap().rearrange("(r p) s -> p r s", p=128), q_cc2)
                wait("vector", q_cc2, t_g)
                t_gm = op("vector", lambda e: e.tensor_mul(
                    out=gm_b[:], in0=g_b[:], in1=rm_b[:]), inc=vc)
                wait("scalar", vc, t_gm)
                t_gm2 = op("scalar", lambda e: e.copy(
                    out=g_b[:], in_=gm_b[:]), inc=sc)
                wait("vector", sc, t_gm2)
                t_sel = op("vector", lambda e: e.tensor_reduce(
                    out=ssel[:], in_=g_b[:].rearrange("p (r s) -> p s r", r=8),
                    axis=AX.X, op=OP.add), inc=vc)
                if "st_dbg" in dbg:
                    wait("gpsimd", vc, t_sel)
                    dma("gpsimd", st_dbg[:, 0:2 * H], states[:], q_cc2)
                    dma("gpsimd", st_dbg[:, 2 * H:4 * H], ssel[:], q_cc2)

                wq_load = [0, 0]
                wq_inuse = [0, 0]
                kv4_load = [0, 0]
                kv4_used = [0, 0]
                ap_store = 0
                prod_peuse = 0
                den_used = 0
                recip_used = 0

                def load_wq(h):
                    bi = h % 2
                    if wq_inuse[bi]:
                        wait("gpsimd", pe, wq_inuse[bi])
                    wq_load[bi] = dma(
                        "gpsimd",
                        wkv_b[bi][:, 0:NF * 128]
                        .rearrange("p (f c) -> p f c", f=NF),
                        wqkv_d[:, QCOL + h * 128:QCOL + h * 128 + 128]
                        .rearrange("(f p) c -> p f c", p=128), q_w[bi])

                def load_kv4(h):
                    bi = h % 2
                    if kv4_used[bi]:
                        wait("sync", vc, kv4_used[bi])
                    dma("sync", phik[bi][:],
                        kv_s[(2 * h) * 128:(2 * h + 1) * 128, :], q_kv4[bi])
                    kv4_load[bi] = dma(
                        "sync", v_b[bi][:],
                        kv_s[(2 * h + 1) * 128:(2 * h + 2) * 128, :], q_kv4[bi])

                load_wq(0)
                load_kv4(0)
                load_wq(1)
                load_kv4(1)

                for h in range(H):
                    bi = h % 2
                    wait("tensor", q_w[bi], wq_load[bi])
                    q_pe = [0] * 4
                    for bk in range(4):
                        pe_guard(bk)
                    for fc in range(NF):
                        for tt in range(4):
                            last = (fc == NF - 1)
                            lh = wkv_b[bi][:, fc * 128:(fc + 1) * 128]
                            t = op("tensor", lambda e, lh=lh, fc=fc, tt=tt,
                                   last=last:
                                   e.matmul(out=ps[tt][:, :], lhsT=lh,
                                            rhs=xnT_sl(fc, tt * 512, 512),
                                            start=(fc == 0), stop=last),
                                   inc=pe if last else None)
                            if last:
                                q_pe[tt] = t
                    wq_inuse[bi] = q_pe[3]
                    if h + 1 < H:
                        load_wq(h + 1)
                        load_kv4(h + 1)
                    for tt in range(4):
                        emit_phi(q_pe[tt], tt, tt,
                                 q_f[:, tt * 512:(tt + 1) * 512])
                    wait("vector", q_kv4[bi], kv4_load[bi])
                    op("vector", lambda e, bi=bi: e.tensor_mul(
                        out=bkv[:], in0=phik[bi][:], in1=v_b[bi][:]))
                    if dk_used[bi]:
                        wait("scalar", vc, dk_used[bi])
                    dk_fill[bi] = op("scalar", lambda e, bi=bi, h=h: e.mul(
                        out=dk_tl[bi][:], in_=ones_t[:],
                        mul=dk_rep[:, h:h + 1]), inc=sc)
                    wait("vector", sc, dk_fill[bi])
                    op("vector", lambda e, bi=bi, h=h: e.tensor_tensor_scan(
                        out=rkv_f[:], data0=dk_tl[bi][:], data1=bkv[:],
                        initial=ssel[:, 2 * h:2 * h + 1],
                        op0=OP.mult, op1=OP.add))
                    kv4_used[bi] = op(
                        "vector", lambda e, bi=bi, h=h: e.tensor_tensor_scan(
                            out=rks_f[:], data0=dk_tl[bi][:], data1=phik[bi][:],
                            initial=ssel[:, 2 * h + 1:2 * h + 2],
                            op0=OP.mult, op1=OP.add), inc=vc)
                    dk_used[bi] = kv4_used[bi]
                    if prod_peuse:
                        wait("vector", pe, prod_peuse)
                    t_prod = op("vector", lambda e: e.tensor_mul(
                        out=prod_b[:], in0=q_f[:], in1=rks_f[:]), inc=vc)
                    wait("tensor", vc, t_prod)
                    if h == 0:
                        wait("tensor", vc, t_ones)
                    den_pe = [0] * 4
                    for c in range(4):
                        pe_guard(4 + c)
                        den_pe[c] = op("tensor", lambda e, c=c: e.matmul(
                            out=ps[4 + c][0:1, :], lhsT=ones_cb[:],
                            rhs=prod_b[:, c * 512:(c + 1) * 512],
                            start=True, stop=True), inc=pe)
                    prod_peuse = den_pe[3]
                    if recip_used:
                        wait("scalar", pe, recip_used)
                    t_dr = 0
                    for c in range(4):
                        wait("scalar", pe, den_pe[c])
                        t_dr = op("scalar", lambda e, c=c: e.activation(
                            out=den_row[0:1, c * 512:(c + 1) * 512],
                            in_=ps[4 + c][0:1, :], func=AF.Relu,
                            bias=cst_b[0:1, 2:3]), inc=sc)
                        bank_free[4 + c] = (sc, t_dr)
                    op("scalar", lambda e: e.activation(
                        out=recip_row[0:1, :], in_=den_row[0:1, :], func=AF.Ln,
                        bias=cst_b[0:1, 3:4]))
                    t_rc = op("scalar", lambda e: e.activation(
                        out=recip_b[0:1, :], in_=recip_row[0:1, :], func=AF.Exp,
                        scale=-1.0), inc=sc)
                    den_used = t_rc
                    wait("tensor", sc, t_rc)
                    rep_pe = [0] * 4
                    for c in range(4):
                        pe_guard(4 + c)
                        rep_pe[c] = op("tensor", lambda e, c=c: e.matmul(
                            out=ps[4 + c][:, :], lhsT=ones_rb[:],
                            rhs=recip_b[0:1, c * 512:(c + 1) * 512],
                            start=True, stop=True), inc=pe)
                    recip_used = rep_pe[3]
                    op("vector", lambda e: e.tensor_mul(
                        out=num_f[:], in0=q_f[:], in1=rkv_f[:]))
                    if ap_store:
                        wait("vector", q_ap, ap_store)
                    t_ap = 0
                    for c in range(4):
                        wait("vector", pe, rep_pe[c])
                        t_ap = op("vector", lambda e, c=c: e.tensor_tensor(
                            out=apb[:, c * 512:(c + 1) * 512],
                            in0=num_f[:, c * 512:(c + 1) * 512],
                            in1=ps[4 + c][:, :], op=OP.mult), inc=vc)
                        bank_free[4 + c] = (vc, t_ap)
                    wait("scalar", vc, t_ap)
                    ap_store = dma("scalar", ap_s[h * 128:(h + 1) * 128, :],
                                   apb[:], q_ap)

            fence()

            with ExitStack() as sc5:
                wo_b = sbt(sc5, "wo_b", [128, NF * DM], BF16)
                ap_sl = [sbt(sc5, f"ap_sl{i}", [128, NF * 512], BF16)
                         for i in range(2)]
                sev = [sbt(sc5, f"sev{i}", [128, 512], BF16) for i in range(2)]
                for oc in range(NF):
                    dma("gpsimd",
                        wo_b[:, oc * DM:(oc + 1) * DM]
                        .rearrange("p (f c) -> p f c", f=NF),
                        wout_d[:, oc * 128:(oc + 1) * 128]
                        .rearrange("(f p) c -> p f c", p=128), q_wo)
                wait("tensor", q_wo, q_wo.n)
                ap_use = [0, 0]
                sev_store = [0, 0]
                for tt in range(4):
                    bi = tt % 2
                    if ap_use[bi]:
                        wait("scalar", pe, ap_use[bi])
                    t_apl = dma(
                        "scalar",
                        ap_sl[bi][:].rearrange("p (f t) -> p f t", f=NF),
                        ap_s.ap().rearrange("(f p) t -> p f t", p=128)
                        [:, :, tt * 512:(tt + 1) * 512], q_apl[bi])
                    wait("tensor", q_apl[bi], t_apl)
                    last_mm = 0
                    for oc in range(NF):
                        bk = oc % 2
                        si = oc % 2
                        pe_guard(bk)
                        for fc in range(NF):
                            last = (fc == NF - 1)
                            t = op("tensor", lambda e, bi=bi, fc=fc, oc=oc,
                                   bk=bk, last=last:
                                   e.matmul(
                                       out=ps[bk][:, :],
                                       lhsT=wo_b[:, oc * DM + fc * 128:
                                                 oc * DM + fc * 128 + 128],
                                       rhs=ap_sl[bi][:, fc * 512:
                                                     (fc + 1) * 512],
                                       start=(fc == 0), stop=last),
                                   inc=pe if last else None)
                            if last:
                                last_mm = t
                        wait("scalar", pe, last_mm)
                        if sev_store[si]:
                            wait("scalar", q_sev[si], sev_store[si])
                        t_ev = op("scalar", lambda e, si=si, bk=bk: e.copy(
                            out=sev[si][:], in_=ps[bk][:, :]), inc=sc)
                        bank_free[bk] = (sc, t_ev)
                        sev_store[si] = dma(
                            "scalar",
                            s_s[oc * 128:(oc + 1) * 128,
                                tt * 512:(tt + 1) * 512], sev[si][:], q_sev[si])
                    ap_use[bi] = last_mm

            fence()

            with ExitStack() as sc6:
                NWB = 3
                wgT = [sbt(sc6, f"wgT{i}", [128, NF * 512], BF16)
                       for i in range(NWB)]
                wuT = [sbt(sc6, f"wuT{i}", [128, NF * 512], BF16)
                       for i in range(NWB)]
                h_b = [sbt(sc6, f"h_b{i}", [128, TC], BF16) for i in range(2)]
                gact = [sbt(sc6, f"gact{i}", [128, 512], F32) for i in range(2)]
                wT_cv = [0] * NWB
                wT_use = [0] * NWB
                hb_store = [0, 0]
                gact_use = [0, 0]
                wT_pend = []

                def load_wT(ig):
                    bi = ig % NWB
                    if wT_use[bi]:
                        wait("gpsimd", pe, wT_use[bi])
                    t_g = dma("gpsimd",
                              wgT[bi][:].rearrange("p (f c) -> p f c", f=NF),
                              gw_d[:, ig * 512:(ig + 1) * 512]
                              .rearrange("(f p) c -> p f c", p=128), q_wug[bi])
                    t_u = dma("gpsimd",
                              wuT[bi][:].rearrange("p (f c) -> p f c", f=NF),
                              uw_d[:, ig * 512:(ig + 1) * 512]
                              .rearrange("(f p) c -> p f c", p=128), q_wuu[bi])
                    wT_pend.append((bi, t_g, t_u))

                def conv_wT():
                    bi, t_g, t_u = wT_pend.pop(0)
                    wait("scalar", q_wug[bi], t_g)
                    op("scalar", lambda e, bi=bi: e.add(
                        out=wgT[bi][:], in_=wgT[bi][:], add=-128.0))
                    wait("scalar", q_wuu[bi], t_u)
                    wT_cv[bi] = op("scalar", lambda e, bi=bi: e.add(
                        out=wuT[bi][:], in_=wuT[bi][:], add=-128.0), inc=sc)

                load_wT(0)
                conv_wT()
                load_wT(1)
                conv_wT()
                load_wT(2)
                for ig in range(16):
                    bi = ig % NWB
                    wait("tensor", sc, wT_cv[bi])
                    ig_last = 0
                    for ic4 in range(4):
                        ic = ig * 4 + ic4
                        hbi = ic % 2
                        if hb_store[hbi]:
                            wait("vector", q_hst[hbi], hb_store[hbi])
                        t_h = 0
                        for tt in range(4):
                            bg = 2 * (tt % 2)
                            bu = bg + 1
                            pe_guard(bg)
                            pe_guard(bu)
                            g_t = u_t = 0
                            for fc in range(NF):
                                last = (fc == NF - 1)
                                lh = wgT[bi][:, fc * 512 + ic4 * 128:
                                             fc * 512 + ic4 * 128 + 128]
                                tg = op("tensor", lambda e, lh=lh, fc=fc,
                                        tt=tt, bg=bg, last=last:
                                        e.matmul(out=ps[bg][:, :], lhsT=lh,
                                                 rhs=xnT_sl(fc, tt * 512, 512),
                                                 start=(fc == 0), stop=last),
                                        inc=pe if last else None)
                                if last:
                                    g_t = tg
                            for fc in range(NF):
                                last = (fc == NF - 1)
                                lh = wuT[bi][:, fc * 512 + ic4 * 128:
                                             fc * 512 + ic4 * 128 + 128]
                                tu = op("tensor", lambda e, lh=lh, fc=fc,
                                        tt=tt, bu=bu, last=last:
                                        e.matmul(out=ps[bu][:, :], lhsT=lh,
                                                 rhs=xnT_sl(fc, tt * 512, 512),
                                                 start=(fc == 0), stop=last),
                                        inc=pe if last else None)
                                if last:
                                    u_t = tu
                            ig_last = u_t
                            gi = tt % 2
                            wait("scalar", pe, g_t)
                            if gact_use[gi]:
                                wait("scalar", vc, gact_use[gi])
                            t_ga = op("scalar", lambda e, bg=bg, gi=gi, ic=ic:
                                      e.activation(
                                          out=gact[gi][:], in_=ps[bg][:, :],
                                          func=AF.Silu,
                                          scale=sg_b[:, ic:ic + 1]), inc=sc)
                            bank_free[bg] = (sc, t_ga)
                            wait("vector", sc, t_ga)
                            wait("vector", pe, u_t)
                            t_h = op("vector",
                                     lambda e, bu=bu, gi=gi, ic=ic, hbi=hbi,
                                     tt=tt:
                                     e.scalar_tensor_tensor(
                                         out=h_b[hbi][:, tt * 512:
                                                      (tt + 1) * 512],
                                         in0=ps[bu][:, :],
                                         scalar=su_b[:, ic:ic + 1],
                                         in1=gact[gi][:],
                                         op0=OP.mult, op1=OP.mult), inc=vc)
                            gact_use[gi] = t_h
                            bank_free[bu] = (vc, t_h)
                        wait("scalar", vc, t_h)
                        hb_store[hbi] = dma(
                            "scalar", h_s[ic * 128:(ic + 1) * 128, :],
                            h_b[hbi][:], q_hst[hbi])
                        if ic4 == 0 and wT_pend:
                            conv_wT()
                    wT_use[bi] = ig_last
                    if ig + 3 < 16:
                        load_wT(ig + 3)

        fence()

        with ExitStack() as sc7:
            wdT = sbt(sc7, "wdT", [128, NIC * 1024], BF16)
            h_p = [sbt(sc7, f"h_p{i}", [128, 16 * 512], BF16) for i in range(2)]
            gf_t = [sbt(sc7, f"gf_t{i}", [128, 8 * 512], BF16)
                    for i in range(2)]
            gf_store = [0, 0]
            hp_use = [0, 0]
            t_blk = [[None] * 4, [None] * 4]
            blk_pe = [[0] * 4, [0] * 4]
            sems7 = [q_wug[0], q_wug[1], q_wug[2], q_wuu[0]]
            for half in range(2):
                for p4 in range(4):
                    qq = sems7[p4]
                    if half == 1 and blk_pe[0][p4]:
                        wait("gpsimd", pe, blk_pe[0][p4])
                    t_ld = dma(
                        "gpsimd",
                        wdT[:, p4 * 16384:(p4 + 1) * 16384]
                        .rearrange("p (f c) -> p f c", f=16),
                        dw_d[p4 * 2048:(p4 + 1) * 2048,
                             half * 1024:(half + 1) * 1024]
                        .rearrange("(f p) c -> p f c", p=128), qq)
                    wait("scalar", qq, t_ld)
                    tcv = 0
                    for hh in range(2):
                        sl = wdT[:, (p4 * 2 + hh) * 8192:
                                 (p4 * 2 + hh + 1) * 8192]
                        tcv = op("scalar", lambda e, sl=sl: e.add(
                            out=sl, in_=sl, add=-128.0), inc=sc)
                    t_blk[half][p4] = (sc, tcv)
                for slab in range(4):
                    sbi = slab % 2
                    dn_pe = [0] * 8
                    for piece in range(4):
                        pbi = piece % 2
                        if slab == 0:
                            wait("tensor", t_blk[half][piece][0],
                                 t_blk[half][piece][1])
                        if hp_use[pbi]:
                            wait("scalar", pe, hp_use[pbi])
                        t_hp = dma(
                            "scalar",
                            h_p[pbi][:].rearrange("p (c t) -> p c t", c=16),
                            h_s.ap().rearrange("(c p) t -> p c t", p=128)
                            [:, piece * 16:(piece + 1) * 16,
                             slab * 512:(slab + 1) * 512], q_hp[pbi])
                        wait("tensor", q_hp[pbi], t_hp)
                        if piece == 0:
                            for hc in range(8):
                                pe_guard(hc)
                        for icl in range(16):
                            ic = piece * 16 + icl
                            for hc in range(8):
                                first = (ic == 0)
                                last = (ic == NIC - 1)
                                piece_end = (icl == 15 and hc == 7)
                                want_inc = last or piece_end
                                t = op("tensor",
                                       lambda e, ic=ic, hc=hc, pbi=pbi,
                                       icl=icl, first=first, last=last:
                                       e.matmul(
                                           out=ps[hc][:, :],
                                           lhsT=wdT[:, ic * 1024 + hc * 128:
                                                    ic * 1024 + hc * 128 + 128],
                                           rhs=h_p[pbi][:, icl * 512:
                                                        (icl + 1) * 512],
                                           start=first, stop=last),
                                       inc=pe if want_inc else None)
                                if last:
                                    dn_pe[hc] = t
                                if piece_end:
                                    hp_use[pbi] = t
                                    if slab == 3:
                                        blk_pe[half][piece] = t
                    if gf_store[sbi]:
                        wait("scalar", q_gf[sbi], gf_store[sbi])
                    t_gf = 0
                    for hc in range(8):
                        wait("scalar", pe, dn_pe[hc])
                        t_gf = op("scalar", lambda e, hc=hc, half=half,
                                  sbi=sbi:
                                  e.activation(
                                      out=gf_t[sbi][:, hc * 512:
                                                    (hc + 1) * 512],
                                      in_=ps[hc][:, :], func=AF.Silu,
                                      scale=sd_b[:, half * 8 + hc:
                                                 half * 8 + hc + 1]), inc=sc)
                        bank_free[hc] = (sc, t_gf)
                    wait("scalar", sc, t_gf)
                    gf_store[sbi] = dma(
                        "scalar",
                        gf_s.ap().rearrange("(o p) t -> p o t", p=128)
                        [:, half * 8:(half + 1) * 8,
                         slab * 512:(slab + 1) * 512],
                        gf_t[sbi][:].rearrange("p (o t) -> p o t", o=8),
                        q_gf[sbi])

        fence()

        with ExitStack() as sc8:
            NB8 = 4
            q8t = qsem("q8t", "sync", NB8)
            q8x = qsem("q8x", "gpsimd", NB8)
            q8y = qsem("q8y", "scalar", NB8)
            s_t8 = [sbt(sc8, f"s_t8{i}", [128, DM], BF16) for i in range(NB8)]
            g_t8 = [sbt(sc8, f"g_t8{i}", [128, DM], BF16) for i in range(NB8)]
            x_t8 = [sbt(sc8, f"x_t8{i}", [128, DM], F32) for i in range(NB8)]
            sgp = sbt(sc8, "sgp", [128, DM], F32)
            o_t8 = [sbt(sc8, f"o_t8{i}", [128, DM], F32) for i in range(NB8)]
            tl = [0] * NB8
            xl = [0] * NB8
            ost = [0] * NB8
            ouse = [0] * NB8
            for tt in range(NF):
                bi = tt % NB8
                if ouse[bi]:
                    wait("sync", vc, ouse[bi])
                    wait("gpsimd", vc, ouse[bi])
                dma("sync", s_t8[bi][:],
                    s_s[0:DM, tt * 128:(tt + 1) * 128], q8t[bi],
                    transpose=True)
                tl[bi] = dma("sync", g_t8[bi][:],
                             gf_s[0:DM, tt * 128:(tt + 1) * 128], q8t[bi],
                             transpose=True)
                xl[bi] = dma("gpsimd", x_t8[bi][:],
                             x_in[tt * 128:(tt + 1) * 128, :], q8x[bi])
                wait("vector", q8t[bi], tl[bi])
                wait("vector", q8x[bi], xl[bi])
                if ost[bi]:
                    wait("vector", q8y[bi], ost[bi])
                op("vector", lambda e, bi=bi: e.tensor_mul(
                    out=sgp[:], in0=s_t8[bi][:], in1=g_t8[bi][:]))
                ouse[bi] = op("vector", lambda e, bi=bi: e.tensor_add(
                    out=o_t8[bi][:], in0=sgp[:], in1=x_t8[bi][:]), inc=vc)
                wait("scalar", vc, ouse[bi])
                ost[bi] = dma("scalar", y_d[tt * 128:(tt + 1) * 128, :],
                              o_t8[bi][:], q8y[bi])
            for i in range(NB8):
                wait("scalar", q8y[i], q8y[i].n)
                wait("sync", q8t[i], q8t[i].n)
                wait("gpsimd", q8x[i], q8x[i].n)

        with nc.Block() as block:
            @block.sync
            def _(e):
                for f in streams["sync"]:
                    f(e)

            @block.scalar
            def _(e):
                for f in streams["scalar"]:
                    f(e)

            @block.vector
            def _(e):
                for f in streams["vector"]:
                    f(e)

            @block.gpsimd
            def _(e):
                for f in streams["gpsimd"]:
                    f(e)

            @block.tensor
            def _(e):
                for f in streams["tensor"]:
                    f(e)

    return nc


_NC_CACHE = []


def _get_nc():
    if not _NC_CACHE:
        _NC_CACHE.append(build_nc())
    return _NC_CACHE[0]


def kernel(L0_attn_x, L0_attn_w_qkv, L0_attn_w_out, L0_attn_decay_param,
           L0_ffn_gate_w, L0_ffn_gate_scale, L0_ffn_up_w, L0_ffn_up_scale,
           L0_ffn_down_w, L0_ffn_down_scale):
    x = np.ascontiguousarray(np.asarray(L0_attn_x, dtype=np.float32))
    wqkv = np.ascontiguousarray(np.asarray(L0_attn_w_qkv, dtype=np.float32))
    wout = np.ascontiguousarray(np.asarray(L0_attn_w_out, dtype=np.float32))
    dk = np.asarray(L0_attn_decay_param, dtype=np.float32)
    def _u8T(a, rows, cols):
        a = np.ascontiguousarray(np.asarray(a, dtype=np.int32))
        return np.ascontiguousarray(
            a.view(np.uint8).reshape(rows, cols, 4)[:, :, 0].T)
    gw = _u8T(L0_ffn_gate_w, IF, DM)
    uw = _u8T(L0_ffn_up_w, IF, DM)
    dw = _u8T(L0_ffn_down_w, DM, IF)
    gs = np.asarray(L0_ffn_gate_scale, dtype=np.float32)
    us = np.asarray(L0_ffn_up_scale, dtype=np.float32)
    ds = np.asarray(L0_ffn_down_scale, dtype=np.float32)

    dk_rep = np.ascontiguousarray(np.broadcast_to(dk[None, :], (128, H)))
    sg = np.ascontiguousarray(gs.reshape(NIC, 128).T)
    su = np.ascontiguousarray(us.reshape(NIC, 128).T)
    sd = np.ascontiguousarray(ds.reshape(NF, 128).T)

    nc = _get_nc()
    in_maps = []
    for c in range(NCORES):
        b, half = c // 2, c % 2
        rm = np.zeros((128, 256), np.float32)
        if half == 1:
            r = c - 1
            rm[:, r * 32:(r + 1) * 32] = 1.0
        in_maps.append(dict(
            x_in=np.ascontiguousarray(x[b, half * TC:(half + 1) * TC, :]),
            w_qkv=wqkv, w_out=wout, decay_rep=dk_rep,
            gate_wT=gw, up_wT=uw, down_wT=dw,
            sg=sg, su=su, sd=sd, rm=rm,
        ))
    res = run_bass_kernel_spmd(nc, in_maps, core_ids=list(range(NCORES)))
    out = np.empty((B, T, DM), np.float32)
    for c in range(NCORES):
        b, half = c // 2, c % 2
        out[b, half * TC:(half + 1) * TC, :] = res.results[c]["y"]
    return out


# version 08
# speedup vs baseline: 1.2034x; 1.0464x over previous
import numpy as np
from contextlib import ExitStack

import concourse.bass as bass
import concourse.mybir as mybir
from concourse.bass_utils import run_bass_kernel_spmd

F32 = mybir.dt.float32
BF16 = mybir.dt.bfloat16
I32 = mybir.dt.int32
AF = mybir.ActivationFunctionType
OP = mybir.AluOpType
AX = mybir.AxisListType

B, T, DM = 4, 4096, 2048
H = 16
TC = 2048
NF = DM // 128
IF = 8192
NIC = IF // 128
NCORES = 8

ENGS = ("sync", "scalar", "vector", "gpsimd", "tensor")


class Ctr:
    __slots__ = ("h", "n")

    def __init__(self, h):
        self.h = h
        self.n = 0

    def bump(self, k=1):
        self.n += k
        return self.n


def build_nc(dbg=()):
    nc = bass.Bass("TRN2", target_bir_lowering=False)

    def _skind(name):
        return "ExternalOutput" if name in dbg else "Internal"

    x_in = nc.dram_tensor("x_in", [TC, DM], F32, kind="ExternalInput")
    wqkv_d = nc.dram_tensor("w_qkv", [DM, 3 * DM], F32, kind="ExternalInput")
    wout_d = nc.dram_tensor("w_out", [DM, DM], F32, kind="ExternalInput")
    dk_d = nc.dram_tensor("decay_rep", [128, H], F32, kind="ExternalInput")
    U8 = mybir.dt.uint8
    gw_d = nc.dram_tensor("gate_wT", [DM, IF], U8, kind="ExternalInput")
    uw_d = nc.dram_tensor("up_wT", [DM, IF], U8, kind="ExternalInput")
    dw_d = nc.dram_tensor("down_wT", [IF, DM], U8, kind="ExternalInput")
    sg_d = nc.dram_tensor("sg", [128, NIC], F32, kind="ExternalInput")
    su_d = nc.dram_tensor("su", [128, NIC], F32, kind="ExternalInput")
    sd_d = nc.dram_tensor("sd", [128, NF], F32, kind="ExternalInput")
    rm_d = nc.dram_tensor("rm", [128, 256], F32, kind="ExternalInput")

    y_d = nc.dram_tensor("y", [TC, DM], F32, kind="ExternalOutput")

    xn_s = nc.dram_tensor("xn_s", [TC, DM], BF16, kind=_skind("xn_s"))
    kv_s = nc.dram_tensor("kv_s", [2 * H * 128, TC], BF16, kind=_skind("kv_s"))
    ap_s = nc.dram_tensor("ap_s", [DM, TC], BF16, kind=_skind("ap_s"))
    s_s = nc.dram_tensor("s_s", [DM, TC], BF16, kind=_skind("s_s"))
    h_s = nc.dram_tensor("h_s", [IF, TC], BF16, kind=_skind("h_s"))
    gf_s = nc.dram_tensor("gf_s", [DM, TC], BF16, kind=_skind("gf_s"))
    cc_i = nc.dram_tensor("cc_i", [128, 2 * H], F32)
    cc_o = nc.dram_tensor("cc_o", [128 * NCORES, 2 * H], F32)
    st_dbg = nc.dram_tensor("st_dbg", [128, 4 * H], F32, kind=_skind("st_dbg"))

    streams = {e: [] for e in ENGS}

    def em(eng, fn):
        streams[eng].append(fn)

    def wait(eng, ctr, tgt):
        if tgt and tgt > 0:
            em(eng, lambda e, c=ctr, t=tgt: e.wait_ge(c.h, t))

    def dma(q, out, in_, ctr, transpose=False):
        t = ctr.bump(16)
        em(q, lambda e, o=out, i=in_, c=ctr, tr=transpose:
           e.dma_start(out=o, in_=i, transpose=tr).then_inc(c.h, 16))
        return t

    def op(eng, fn, inc=None):
        if inc is None:
            em(eng, fn)
            return None
        t = inc.bump(1)
        em(eng, lambda e, f=fn, c=inc: f(e).then_inc(c.h, 1))
        return t

    with ExitStack() as root:
        def sbt(scope, name, shape, dt):
            return scope.enter_context(nc.sbuf_tensor(name, shape, dt))

        def sem(name):
            return Ctr(root.enter_context(nc.semaphore(name)))

        cc = sem("cc")
        pe = sem("pe")
        vc = sem("vc")
        sc = sem("sc")
        bar = sem("bar")

        dma_ctrs = []

        def qsem(name, eng, n=1):
            out = []
            for i in range(n):
                c = sem(f"{name}{i}")
                dma_ctrs.append((eng, c))
                out.append(c)
            return out if n > 1 else out[0]

        q_c = qsem("q_c", "gpsimd")
        q_x = qsem("q_x", "gpsimd", 2)
        q_wug = qsem("q_wug", "gpsimd", 3)
        q_wuu = qsem("q_wuu", "gpsimd", 3)
        q_w = qsem("q_w", "gpsimd", 2)
        q_kst = qsem("q_kst", "scalar", 2)
        q_kv4 = qsem("q_kv4", "sync", 2)
        q_ap = qsem("q_ap", "scalar", 2)
        q_xn = qsem("q_xn", "scalar", 2)
        q_cc2 = qsem("q_cc2", "gpsimd")
        q_t = qsem("q_t", "sync", 2)
        q_apl = qsem("q_apl", "scalar", 2)
        q_wo = qsem("q_wo", "gpsimd")
        q_sev = qsem("q_sev", "scalar", 2)
        q_hst = qsem("q_hst", "scalar", 2)
        q_hp = qsem("q_hp", "scalar", 2)
        q_gf = qsem("q_gf", "scalar", 2)
        q_y = qsem("q_y", "scalar", 2)
        po = sem("po")

        def fence():
            for eng, c in dma_ctrs:
                wait(eng, c, c.n)
            for e in ENGS:
                em(e, lambda eng: eng.sem_inc(bar.h, 1))
            tgt = bar.bump(5)
            for e in ENGS:
                wait(e, bar, tgt)

        ps = [root.enter_context(nc.psum_tensor(f"ps{i}", [128, 512], F32))
              for i in range(8)]
        bank_free = [None] * 8

        def pe_guard(i):
            if bank_free[i] is not None:
                c, t = bank_free[i]
                wait("tensor", c, t)
                bank_free[i] = None

        dk_raw = sbt(root, "dk_raw", [128, H], F32)
        dk_rep = sbt(root, "dk_rep", [128, H], F32)
        sg_b = sbt(root, "sg_b", [128, NIC], F32)
        su_b = sbt(root, "su_b", [128, NIC], F32)
        sd_b = sbt(root, "sd_b", [128, NF], F32)
        rm_b = sbt(root, "rm_b", [128, 256], F32)
        g_b = sbt(root, "g_b", [128, 256], F32)
        gm_b = sbt(root, "gm_b", [128, 256], F32)
        ssel = sbt(root, "ssel", [128, 2 * H], F32)
        states = sbt(root, "states", [128, 2 * H], F32)
        ones_cb = sbt(root, "ones_cb", [128, 1], BF16)
        ones_rb = sbt(root, "ones_rb", [1, 128], BF16)
        recip_b = sbt(root, "recip_b", [1, TC], BF16)
        cst_b = sbt(root, "cst_b", [128, 4], F32)
        den_row = sbt(root, "den_row", [1, TC], F32)
        recip_row = sbt(root, "recip_row", [1, TC], F32)

        nc.const_aps.register(F32, -128.0, cst_b[:, 1:2])
        t_dk = dma("gpsimd", dk_raw[:], dk_d[:, :], q_c)
        for dst, src in ((sg_b, sg_d), (su_b, su_d), (sd_b, sd_d), (rm_b, rm_d)):
            dma("gpsimd", dst[:], src[:, :], q_c)
        op("vector", lambda e: e.memset(ones_cb[:], 1.0))
        op("vector", lambda e: e.memset(cst_b[:, 0:1], 1e-5))
        op("vector", lambda e: e.memset(cst_b[:, 1:2], -128.0))
        op("vector", lambda e: e.memset(cst_b[:, 2:3], -1e-6))
        op("vector", lambda e: e.memset(cst_b[:, 3:4], 1e-6))
        t_ones = op("vector", lambda e: e.memset(ones_rb[:], 1.0), inc=vc)
        wait("scalar", q_c, q_c.n)
        wait("scalar", vc, t_ones)
        t_sig = op("scalar", lambda e: e.activation(
            out=dk_rep[:], in_=dk_raw[:], func=AF.Sigmoid), inc=sc)

        with ExitStack() as scA:
            xnT = sbt(scA, "xnT", [128, NF * TC], BF16)

            def xnT_sl(fc, t0, tn):
                return xnT[:, fc * TC + t0: fc * TC + t0 + tn]

            with ExitStack() as scL:
                xt = [sbt(scL, f"xt{i}", [128, DM], F32) for i in range(2)]
                xc = [sbt(scL, f"xc{i}", [128, DM], F32) for i in range(2)]
                sqw = [sbt(scL, f"sqw{i}", [128, DM], F32) for i in range(2)]
                s1 = [sbt(scL, f"s1_{i}", [128, 1], F32) for i in range(2)]
                s2 = [sbt(scL, f"s2_{i}", [128, 1], F32) for i in range(2)]
                s3 = [sbt(scL, f"s3_{i}", [128, 1], F32) for i in range(2)]
                xnb = [sbt(scL, f"xnb{i}", [128, DM], BF16) for i in range(2)]
                ln_st = [0, 0]
                ln_use = [0, 0]
                t_r_par = [0, 0]
                for tt in range(NF):
                    bi = tt % 2
                    if ln_use[bi]:
                        wait("gpsimd", vc, ln_use[bi])
                    t_x = dma("gpsimd", xt[bi][:],
                              x_in[tt * 128:(tt + 1) * 128, :], q_x[bi])
                    wait("vector", q_x[bi], t_x)
                    wait("vector", sc, t_r_par[bi])
                    t_sum = op("vector", lambda e, bi=bi: e.reduce_sum(
                        out=s1[bi][:], in_=xt[bi][:], axis=AX.X), inc=vc)
                    wait("scalar", vc, t_sum)
                    t_m = op("scalar", lambda e, bi=bi: e.activation(
                        out=s2[bi][:], in_=s1[bi][:], func=AF.Copy,
                        scale=1.0 / DM), inc=sc)
                    wait("vector", sc, t_m)
                    t_c = op("vector", lambda e, bi=bi: e.tensor_scalar(
                        out=xc[bi][:], in0=xt[bi][:], scalar1=s2[bi][:, 0:1],
                        scalar2=None, op0=OP.subtract), inc=vc)
                    wait("scalar", vc, t_c)
                    t_sq = op("scalar", lambda e, bi=bi: e.activation(
                        out=sqw[bi][:], in_=xc[bi][:], func=AF.Square), inc=sc)
                    wait("vector", sc, t_sq)
                    t_rs = op("vector", lambda e, bi=bi: e.reduce_sum(
                        out=s1[bi][:], in_=sqw[bi][:], axis=AX.X), inc=vc)
                    wait("scalar", vc, t_rs)
                    op("scalar", lambda e, bi=bi: e.activation(
                        out=s2[bi][:], in_=s1[bi][:], func=AF.Ln, scale=1.0 / DM,
                        bias=cst_b[:, 0:1]))
                    t_r = op("scalar", lambda e, bi=bi: e.activation(
                        out=s3[bi][:], in_=s2[bi][:], func=AF.Exp,
                        scale=-0.5), inc=sc)
                    t_r_par[bi] = t_r
                    if ln_st[bi]:
                        wait("vector", q_xn[bi], ln_st[bi])
                    wait("vector", sc, t_r)
                    t_xn = op("vector", lambda e, bi=bi: e.tensor_scalar(
                        out=xnb[bi][:], in0=xc[bi][:], scalar1=s3[bi][:, 0:1],
                        scalar2=None, op0=OP.mult), inc=vc)
                    ln_use[bi] = t_xn
                    wait("scalar", vc, t_xn)
                    ln_st[bi] = dma("scalar", xn_s[tt * 128:(tt + 1) * 128, :],
                                    xnb[bi][:], q_xn[bi])
                wait("sync", q_xn[0], q_xn[0].n)
                wait("sync", q_xn[1], q_xn[1].n)
                for fc in range(NF):
                    dma("sync", xnT_sl(fc, 0, TC),
                        xn_s[0:TC, fc * 128:(fc + 1) * 128], q_t[0],
                        transpose=True)
                t_xnT = q_t[0].n

            fence()

            with ExitStack() as scH:
                ones_t = sbt(scH, "ones_t", [128, TC], BF16)
                dk_tl = [sbt(scH, f"dk_tl{i}", [128, TC], F32) for i in range(2)]
                t_onest = op("vector", lambda e: e.memset(ones_t[:], 1.0),
                             inc=vc)
                wkv_b = [sbt(scH, f"wkv{i}", [128, 2 * NF * 128], BF16)
                         for i in range(2)]
                phik = [sbt(scH, f"phik{i}", [128, TC], BF16) for i in range(2)]
                v_b = [sbt(scH, f"vb{i}", [128, TC], BF16) for i in range(2)]
                bkv = sbt(scH, "bkv", [128, TC], F32)
                t1b = sbt(scH, "t1b", [128, 512], F32)
                t2b = [sbt(scH, f"t2b{i}", [128, 512], F32) for i in range(2)]
                q_f = sbt(scH, "q_f", [128, TC], F32)
                rkv_f = sbt(scH, "rkv_f", [128, TC], F32)
                rks_f = sbt(scH, "rks_f", [128, TC], F32)
                prod_b = [sbt(scH, f"prod_b{i}", [128, TC], BF16)
                          for i in range(2)]
                num_b = sbt(scH, "num_b", [128, TC], F32)
                apb = [sbt(scH, f"apb{i}", [128, TC], BF16) for i in range(2)]

                QCOL, KCOL, VCOL = 0, DM, 2 * DM
                stt_busy = [0, 0]

                def emit_phi(bank_pe_t, bank, tt, dst_ap):
                    i2 = tt % 2
                    wait("scalar", pe, bank_pe_t)
                    if stt_busy[i2]:
                        wait("scalar", vc, stt_busy[i2])
                    op("scalar", lambda e, b=bank: e.activation(
                        out=t1b[:], in_=ps[b][:, :], func=AF.Relu,
                        scale=-1.0))
                    t2t = op("scalar", lambda e, i2=i2: e.activation(
                        out=t2b[i2][:], in_=t1b[:], func=AF.Exp,
                        scale=-1.0), inc=sc)
                    wait("vector", sc, t2t)
                    t = op("vector", lambda e, b=bank, i2=i2, d=dst_ap:
                           e.scalar_tensor_tensor(
                               out=d, in0=ps[b][:, :], scalar=0.0,
                               in1=t2b[i2][:], op0=OP.max, op1=OP.add), inc=vc)
                    stt_busy[i2] = t
                    bank_free[bank] = (vc, t)
                    return t

                wkv_load = [0, 0]
                wkv_inuse = [0, 0]
                kv_store = [0, 0]
                dk_fill = [0, 0]
                dk_used = [0, 0]
                st_copied = [0]

                def load_wkv(h):
                    bi = h % 2
                    if wkv_inuse[bi]:
                        wait("gpsimd", pe, wkv_inuse[bi])
                    for j, col0 in ((0, KCOL + h * 128), (1, VCOL + h * 128)):
                        wkv_load[bi] = dma(
                            "gpsimd",
                            wkv_b[bi][:, j * NF * 128:(j + 1) * NF * 128]
                            .rearrange("p (f c) -> p f c", f=NF),
                            wqkv_d[:, col0:col0 + 128]
                            .rearrange("(f p) c -> p f c", p=128), q_w[bi])

                load_wkv(0)
                load_wkv(1)

                for h in range(H):
                    bi = h % 2
                    wait("tensor", q_w[bi], wkv_load[bi])
                    if h == 0:
                        wait("tensor", q_t[0], t_xnT)
                    kv_pe = [0] * 8
                    for j, base in ((0, 0), (1, 4)):
                        for bk in range(4):
                            pe_guard(base + bk)
                        for fc in range(NF):
                            for tt in range(4):
                                b_ = base + tt
                                last = (fc == NF - 1)
                                lh = wkv_b[bi][:, (j * NF + fc) * 128:
                                               (j * NF + fc) * 128 + 128]
                                t = op("tensor",
                                       lambda e, lh=lh, fc=fc, tt=tt, b_=b_,
                                       last=last:
                                       e.matmul(out=ps[b_][:, :], lhsT=lh,
                                                rhs=xnT_sl(fc, tt * 512, 512),
                                                start=(fc == 0), stop=last),
                                       inc=pe if last else None)
                                if last:
                                    kv_pe[b_] = t
                    wkv_inuse[bi] = kv_pe[7]
                    if h + 1 < H:
                        load_wkv(h + 1)
                    if kv_store[bi]:
                        wait("vector", q_kst[bi], kv_store[bi])
                    phi_t = 0
                    for tt in range(4):
                        phi_t = emit_phi(kv_pe[tt], tt, tt,
                                         phik[bi][:, tt * 512:(tt + 1) * 512])
                    vcp_t = 0
                    for tt in range(4):
                        wait("vector", pe, kv_pe[4 + tt])
                        vcp_t = op("vector", lambda e, tt=tt, bi=bi:
                                   e.tensor_copy(
                                       out=v_b[bi][:, tt * 512:(tt + 1) * 512],
                                       in_=ps[4 + tt][:, :]), inc=vc)
                        bank_free[4 + tt] = (vc, vcp_t)
                    op("vector", lambda e, bi=bi: e.tensor_mul(
                        out=bkv[:], in0=phik[bi][:], in1=v_b[bi][:]))
                    if st_copied[0]:
                        wait("vector", sc, st_copied[0])
                    if dk_used[bi]:
                        wait("scalar", vc, dk_used[bi])
                    if h < 2:
                        wait("scalar", sc, t_sig)
                        wait("scalar", vc, t_onest)
                    dk_fill[bi] = op("scalar", lambda e, bi=bi, h=h: e.mul(
                        out=dk_tl[bi][:], in_=ones_t[:],
                        mul=dk_rep[:, h:h + 1]), inc=sc)
                    wait("vector", sc, dk_fill[bi])
                    op("vector", lambda e, bi=bi: e.tensor_tensor_scan(
                        out=rkv_f[:], data0=dk_tl[bi][:], data1=bkv[:],
                        initial=0.0, op0=OP.mult, op1=OP.add))
                    dk_used[bi] = op(
                        "vector", lambda e, bi=bi: e.tensor_tensor_scan(
                            out=rks_f[:], data0=dk_tl[bi][:], data1=phik[bi][:],
                            initial=0.0, op0=OP.mult, op1=OP.add), inc=vc)
                    wait("scalar", vc, dk_used[bi])
                    op("scalar", lambda e, h=h: e.copy(
                        out=states[:, 2 * h:2 * h + 1],
                        in_=rkv_f[:, TC - 1:TC]))
                    st_copied[0] = op("scalar", lambda e, h=h: e.copy(
                        out=states[:, 2 * h + 1:2 * h + 2],
                        in_=rks_f[:, TC - 1:TC]), inc=sc)
                    wait("scalar", vc, max(vcp_t, phi_t))
                    dma("scalar", kv_s[(2 * h) * 128:(2 * h + 1) * 128, :],
                        phik[bi][:], q_kst[bi])
                    kv_store[bi] = dma(
                        "scalar", kv_s[(2 * h + 1) * 128:(2 * h + 2) * 128, :],
                        v_b[bi][:], q_kst[bi])

                fence()

                dma("gpsimd", cc_i[:, :], states[:], q_cc2)
                wait("gpsimd", q_cc2, q_cc2.n)
                em("gpsimd", lambda e: e.collective_compute(
                    "AllGather", OP.bypass,
                    replica_groups=[list(range(NCORES))],
                    ins=[cc_i.ap().opt()], outs=[cc_o.ap().opt()],
                ).then_inc(cc.h))
                cc.bump(1)

                wq_load = [0, 0]
                wq_inuse = [0, 0]
                kv4_load = [0, 0]
                kv4_used = [0, 0]
                ap_store = [0, 0]
                recip_used = 0

                def load_wq(h):
                    bi = h % 2
                    if wq_inuse[bi]:
                        wait("gpsimd", pe, wq_inuse[bi])
                    wq_load[bi] = dma(
                        "gpsimd",
                        wkv_b[bi][:, 0:NF * 128]
                        .rearrange("p (f c) -> p f c", f=NF),
                        wqkv_d[:, QCOL + h * 128:QCOL + h * 128 + 128]
                        .rearrange("(f p) c -> p f c", p=128), q_w[bi])

                def load_kv4(h):
                    bi = h % 2
                    if kv4_used[bi]:
                        wait("sync", vc, kv4_used[bi])
                    dma("sync", phik[bi][:],
                        kv_s[(2 * h) * 128:(2 * h + 1) * 128, :], q_kv4[bi])
                    kv4_load[bi] = dma(
                        "sync", v_b[bi][:],
                        kv_s[(2 * h + 1) * 128:(2 * h + 2) * 128, :], q_kv4[bi])

                load_wq(0)
                load_kv4(0)
                load_wq(1)
                load_kv4(1)

                wait("gpsimd", cc, 1)
                t_g = dma("gpsimd", g_b[:].rearrange("p (r s) -> p r s", r=8),
                          cc_o.ap().rearrange("(r p) s -> p r s", p=128), q_cc2)
                wait("vector", q_cc2, t_g)
                t_gm = op("vector", lambda e: e.tensor_mul(
                    out=gm_b[:], in0=g_b[:], in1=rm_b[:]), inc=vc)
                wait("scalar", vc, t_gm)
                t_gm2 = op("scalar", lambda e: e.copy(
                    out=g_b[:], in_=gm_b[:]), inc=sc)
                wait("vector", sc, t_gm2)
                t_sel = op("vector", lambda e: e.tensor_reduce(
                    out=ssel[:], in_=g_b[:].rearrange("p (r s) -> p s r", r=8),
                    axis=AX.X, op=OP.add), inc=vc)
                if "st_dbg" in dbg:
                    wait("gpsimd", vc, t_sel)
                    dma("gpsimd", st_dbg[:, 0:2 * H], states[:], q_cc2)
                    dma("gpsimd", st_dbg[:, 2 * H:4 * H], ssel[:], q_cc2)

                st4 = {}

                def den_mm(h):
                    par = h % 2
                    wait("tensor", vc, st4[h]["t_prod"])
                    if h == 0:
                        wait("tensor", vc, t_ones)
                    den_pe = [0] * 4
                    for c in range(4):
                        pe_guard(4 + c)
                        den_pe[c] = op("tensor", lambda e, c=c, par=par:
                                       e.matmul(
                            out=ps[4 + c][0:1, :], lhsT=ones_cb[:],
                            rhs=prod_b[par][:, c * 512:(c + 1) * 512],
                            start=True, stop=True), inc=pe)
                    st4[h]["den_pe"] = den_pe

                def den_chain(h):
                    nonlocal recip_used
                    if recip_used:
                        wait("scalar", pe, recip_used)
                    t_dr = 0
                    for c in range(4):
                        wait("scalar", pe, st4[h]["den_pe"][c])
                        t_dr = op("scalar", lambda e, c=c: e.activation(
                            out=den_row[0:1, c * 512:(c + 1) * 512],
                            in_=ps[4 + c][0:1, :], func=AF.Relu,
                            bias=cst_b[0:1, 2:3]), inc=sc)
                        bank_free[4 + c] = (sc, t_dr)
                    op("scalar", lambda e: e.activation(
                        out=recip_row[0:1, :], in_=den_row[0:1, :], func=AF.Ln,
                        bias=cst_b[0:1, 3:4]))
                    st4[h]["t_rc"] = op("scalar", lambda e: e.activation(
                        out=recip_b[0:1, :], in_=recip_row[0:1, :],
                        func=AF.Exp, scale=-1.0), inc=sc)

                def rep_mm(h):
                    nonlocal recip_used
                    wait("tensor", sc, st4[h]["t_rc"])
                    rep_pe = [0] * 4
                    for c in range(4):
                        pe_guard(4 + c)
                        rep_pe[c] = op("tensor", lambda e, c=c: e.matmul(
                            out=ps[4 + c][:, :], lhsT=ones_rb[:],
                            rhs=recip_b[0:1, c * 512:(c + 1) * 512],
                            start=True, stop=True), inc=pe)
                    st4[h]["rep_pe"] = rep_pe
                    recip_used = rep_pe[3]

                def ap_out(h):
                    abi = h % 2
                    if ap_store[abi]:
                        wait("vector", q_ap[abi], ap_store[abi])
                    rep_pe = st4[h]["rep_pe"]
                    t_ap = 0
                    for c in range(4):
                        wait("vector", pe, rep_pe[c])
                        t_ap = op("vector", lambda e, c=c, abi=abi:
                                  e.tensor_tensor(
                            out=apb[abi][:, c * 512:(c + 1) * 512],
                            in0=num_b[:, c * 512:(c + 1) * 512],
                            in1=ps[4 + c][:, :], op=OP.mult), inc=vc)
                        bank_free[4 + c] = (vc, t_ap)
                    wait("scalar", vc, t_ap)
                    ap_store[abi] = dma(
                        "scalar", ap_s[h * 128:(h + 1) * 128, :],
                        apb[abi][:], q_ap[abi])

                for h in range(H):
                    bi = h % 2
                    par = h % 2
                    st4[h] = {}
                    wait("tensor", q_w[bi], wq_load[bi])
                    q_pe = [0] * 4
                    for bk in range(4):
                        pe_guard(bk)

                    def qmm(f0, f1, q_pe=q_pe, bi=bi):
                        for fc in range(f0, f1):
                            for tt in range(4):
                                last = (fc == NF - 1)
                                lh = wkv_b[bi][:, fc * 128:(fc + 1) * 128]
                                t = op("tensor", lambda e, lh=lh, fc=fc,
                                       tt=tt, last=last:
                                       e.matmul(out=ps[tt][:, :], lhsT=lh,
                                                rhs=xnT_sl(fc, tt * 512, 512),
                                                start=(fc == 0), stop=last),
                                       inc=pe if last else None)
                                if last:
                                    q_pe[tt] = t

                    qmm(0, 8)
                    if h >= 1:
                        den_mm(h - 1)
                        den_chain(h - 1)
                    qmm(8, NF)
                    wq_inuse[bi] = q_pe[3]
                    if h + 1 < H:
                        load_wq(h + 1)
                        load_kv4(h + 1)
                    if h >= 1:
                        rep_mm(h - 1)
                        op("vector", lambda e: e.tensor_mul(
                            out=num_b[:], in0=q_f[:], in1=rkv_f[:]))
                    for tt in range(4):
                        emit_phi(q_pe[tt], tt, tt,
                                 q_f[:, tt * 512:(tt + 1) * 512])
                    wait("vector", q_kv4[bi], kv4_load[bi])
                    op("vector", lambda e, bi=bi: e.tensor_mul(
                        out=bkv[:], in0=phik[bi][:], in1=v_b[bi][:]))
                    if dk_used[bi]:
                        wait("scalar", vc, dk_used[bi])
                    dk_fill[bi] = op("scalar", lambda e, bi=bi, h=h: e.mul(
                        out=dk_tl[bi][:], in_=ones_t[:],
                        mul=dk_rep[:, h:h + 1]), inc=sc)
                    wait("vector", sc, dk_fill[bi])
                    op("vector", lambda e, bi=bi, h=h: e.tensor_tensor_scan(
                        out=rkv_f[:], data0=dk_tl[bi][:], data1=bkv[:],
                        initial=ssel[:, 2 * h:2 * h + 1],
                        op0=OP.mult, op1=OP.add))
                    kv4_used[bi] = op(
                        "vector", lambda e, bi=bi, h=h: e.tensor_tensor_scan(
                            out=rks_f[:], data0=dk_tl[bi][:], data1=phik[bi][:],
                            initial=ssel[:, 2 * h + 1:2 * h + 2],
                            op0=OP.mult, op1=OP.add), inc=vc)
                    dk_used[bi] = kv4_used[bi]
                    if h >= 2:
                        wait("vector", pe, st4[h - 2]["den_pe"][3])
                    st4[h]["t_prod"] = op("vector", lambda e, par=par:
                                          e.tensor_mul(
                        out=prod_b[par][:], in0=q_f[:], in1=rks_f[:]), inc=vc)
                    if h >= 1:
                        ap_out(h - 1)
                den_mm(H - 1)
                den_chain(H - 1)
                rep_mm(H - 1)
                op("vector", lambda e: e.tensor_mul(
                    out=num_b[:], in0=q_f[:], in1=rkv_f[:]))
                ap_out(H - 1)

            fence()

            with ExitStack() as sc5:
                wo_b = sbt(sc5, "wo_b", [128, NF * DM], BF16)
                ap_sl = [sbt(sc5, f"ap_sl{i}", [128, NF * 512], BF16)
                         for i in range(2)]
                sev = [sbt(sc5, f"sev{i}", [128, 512], BF16) for i in range(2)]
                for oc in range(NF):
                    dma("gpsimd",
                        wo_b[:, oc * DM:(oc + 1) * DM]
                        .rearrange("p (f c) -> p f c", f=NF),
                        wout_d[:, oc * 128:(oc + 1) * 128]
                        .rearrange("(f p) c -> p f c", p=128), q_wo)
                wait("tensor", q_wo, q_wo.n)
                ap_use = [0, 0]
                sev_store = [0, 0]
                for tt in range(4):
                    bi = tt % 2
                    if ap_use[bi]:
                        wait("scalar", pe, ap_use[bi])
                    t_apl = dma(
                        "scalar",
                        ap_sl[bi][:].rearrange("p (f t) -> p f t", f=NF),
                        ap_s.ap().rearrange("(f p) t -> p f t", p=128)
                        [:, :, tt * 512:(tt + 1) * 512], q_apl[bi])
                    wait("tensor", q_apl[bi], t_apl)
                    last_mm = 0
                    for oc in range(NF):
                        bk = oc % 2
                        si = oc % 2
                        pe_guard(bk)
                        for fc in range(NF):
                            last = (fc == NF - 1)
                            t = op("tensor", lambda e, bi=bi, fc=fc, oc=oc,
                                   bk=bk, last=last:
                                   e.matmul(
                                       out=ps[bk][:, :],
                                       lhsT=wo_b[:, oc * DM + fc * 128:
                                                 oc * DM + fc * 128 + 128],
                                       rhs=ap_sl[bi][:, fc * 512:
                                                     (fc + 1) * 512],
                                       start=(fc == 0), stop=last),
                                   inc=pe if last else None)
                            if last:
                                last_mm = t
                        wait("scalar", pe, last_mm)
                        if sev_store[si]:
                            wait("scalar", q_sev[si], sev_store[si])
                        t_ev = op("scalar", lambda e, si=si, bk=bk: e.copy(
                            out=sev[si][:], in_=ps[bk][:, :]), inc=sc)
                        bank_free[bk] = (sc, t_ev)
                        sev_store[si] = dma(
                            "scalar",
                            s_s[oc * 128:(oc + 1) * 128,
                                tt * 512:(tt + 1) * 512], sev[si][:], q_sev[si])
                    ap_use[bi] = last_mm

            fence()

            with ExitStack() as sc6:
                NWB = 3
                wgT = [sbt(sc6, f"wgT{i}", [128, NF * 512], BF16)
                       for i in range(NWB)]
                wuT = [sbt(sc6, f"wuT{i}", [128, NF * 512], BF16)
                       for i in range(NWB)]
                h_b = [sbt(sc6, f"h_b{i}", [128, TC], BF16) for i in range(2)]
                gact = [sbt(sc6, f"gact{i}", [128, 512], F32) for i in range(2)]
                wT_cv = [0] * NWB
                wT_use = [0] * NWB
                hb_store = [0, 0]
                gact_use = [0, 0]
                wT_pend = []

                def load_wT(ig):
                    bi = ig % NWB
                    if wT_use[bi]:
                        wait("gpsimd", pe, wT_use[bi])
                    t_g = dma("gpsimd",
                              wgT[bi][:].rearrange("p (f c) -> p f c", f=NF),
                              gw_d[:, ig * 512:(ig + 1) * 512]
                              .rearrange("(f p) c -> p f c", p=128), q_wug[bi])
                    t_u = dma("gpsimd",
                              wuT[bi][:].rearrange("p (f c) -> p f c", f=NF),
                              uw_d[:, ig * 512:(ig + 1) * 512]
                              .rearrange("(f p) c -> p f c", p=128), q_wuu[bi])
                    wT_pend.append((bi, t_g, t_u))

                def conv_wT():
                    bi, t_g, t_u = wT_pend.pop(0)
                    wait("scalar", q_wug[bi], t_g)
                    op("scalar", lambda e, bi=bi: e.add(
                        out=wgT[bi][:], in_=wgT[bi][:], add=-128.0))
                    wait("scalar", q_wuu[bi], t_u)
                    wT_cv[bi] = op("scalar", lambda e, bi=bi: e.add(
                        out=wuT[bi][:], in_=wuT[bi][:], add=-128.0), inc=sc)

                load_wT(0)
                conv_wT()
                load_wT(1)
                conv_wT()
                load_wT(2)
                for ig in range(16):
                    bi = ig % NWB
                    wait("tensor", sc, wT_cv[bi])
                    ig_last = 0
                    for ic4 in range(4):
                        ic = ig * 4 + ic4
                        hbi = ic % 2
                        if hb_store[hbi]:
                            wait("vector", q_hst[hbi], hb_store[hbi])
                        t_h = 0
                        for tt in range(4):
                            bg = 2 * (tt % 2)
                            bu = bg + 1
                            pe_guard(bg)
                            pe_guard(bu)
                            g_t = u_t = 0
                            for fc in range(NF):
                                last = (fc == NF - 1)
                                lh = wgT[bi][:, fc * 512 + ic4 * 128:
                                             fc * 512 + ic4 * 128 + 128]
                                tg = op("tensor", lambda e, lh=lh, fc=fc,
                                        tt=tt, bg=bg, last=last:
                                        e.matmul(out=ps[bg][:, :], lhsT=lh,
                                                 rhs=xnT_sl(fc, tt * 512, 512),
                                                 start=(fc == 0), stop=last),
                                        inc=pe if last else None)
                                if last:
                                    g_t = tg
                            for fc in range(NF):
                                last = (fc == NF - 1)
                                lh = wuT[bi][:, fc * 512 + ic4 * 128:
                                             fc * 512 + ic4 * 128 + 128]
                                tu = op("tensor", lambda e, lh=lh, fc=fc,
                                        tt=tt, bu=bu, last=last:
                                        e.matmul(out=ps[bu][:, :], lhsT=lh,
                                                 rhs=xnT_sl(fc, tt * 512, 512),
                                                 start=(fc == 0), stop=last),
                                        inc=pe if last else None)
                                if last:
                                    u_t = tu
                            ig_last = u_t
                            gi = tt % 2
                            wait("scalar", pe, g_t)
                            if gact_use[gi]:
                                wait("scalar", vc, gact_use[gi])
                            t_ga = op("scalar", lambda e, bg=bg, gi=gi, ic=ic:
                                      e.activation(
                                          out=gact[gi][:], in_=ps[bg][:, :],
                                          func=AF.Silu,
                                          scale=sg_b[:, ic:ic + 1]), inc=sc)
                            bank_free[bg] = (sc, t_ga)
                            wait("vector", sc, t_ga)
                            wait("vector", pe, u_t)
                            t_h = op("vector",
                                     lambda e, bu=bu, gi=gi, ic=ic, hbi=hbi,
                                     tt=tt:
                                     e.scalar_tensor_tensor(
                                         out=h_b[hbi][:, tt * 512:
                                                      (tt + 1) * 512],
                                         in0=ps[bu][:, :],
                                         scalar=su_b[:, ic:ic + 1],
                                         in1=gact[gi][:],
                                         op0=OP.mult, op1=OP.mult), inc=vc)
                            gact_use[gi] = t_h
                            bank_free[bu] = (vc, t_h)
                        wait("scalar", vc, t_h)
                        hb_store[hbi] = dma(
                            "scalar", h_s[ic * 128:(ic + 1) * 128, :],
                            h_b[hbi][:], q_hst[hbi])
                        if ic4 == 0 and wT_pend:
                            conv_wT()
                    wT_use[bi] = ig_last
                    if ig + 3 < 16:
                        load_wT(ig + 3)

        fence()

        with ExitStack() as sc7:
            wdT = sbt(sc7, "wdT", [128, NIC * 1024], BF16)
            h_p = [sbt(sc7, f"h_p{i}", [128, 16 * 512], BF16) for i in range(2)]
            gf_t = [sbt(sc7, f"gf_t{i}", [128, 8 * 512], BF16)
                    for i in range(2)]
            gf_store = [0, 0]
            hp_use = [0, 0]
            t_blk = [[None] * 4, [None] * 4]
            blk_pe = [[0] * 4, [0] * 4]
            sems7 = [q_wug[0], q_wug[1], q_wug[2], q_wuu[0]]
            for half in range(2):
                for p4 in range(4):
                    qq = sems7[p4]
                    if half == 1 and blk_pe[0][p4]:
                        wait("gpsimd", pe, blk_pe[0][p4])
                    t_ld = dma(
                        "gpsimd",
                        wdT[:, p4 * 16384:(p4 + 1) * 16384]
                        .rearrange("p (f c) -> p f c", f=16),
                        dw_d[p4 * 2048:(p4 + 1) * 2048,
                             half * 1024:(half + 1) * 1024]
                        .rearrange("(f p) c -> p f c", p=128), qq)
                    wait("scalar", qq, t_ld)
                    tcv = 0
                    for hh in range(2):
                        sl = wdT[:, (p4 * 2 + hh) * 8192:
                                 (p4 * 2 + hh + 1) * 8192]
                        tcv = op("scalar", lambda e, sl=sl: e.add(
                            out=sl, in_=sl, add=-128.0), inc=sc)
                    t_blk[half][p4] = (sc, tcv)
                for slab in range(4):
                    sbi = slab % 2
                    dn_pe = [0] * 8
                    for piece in range(4):
                        pbi = piece % 2
                        if slab == 0:
                            wait("tensor", t_blk[half][piece][0],
                                 t_blk[half][piece][1])
                        if hp_use[pbi]:
                            wait("scalar", pe, hp_use[pbi])
                        t_hp = dma(
                            "scalar",
                            h_p[pbi][:].rearrange("p (c t) -> p c t", c=16),
                            h_s.ap().rearrange("(c p) t -> p c t", p=128)
                            [:, piece * 16:(piece + 1) * 16,
                             slab * 512:(slab + 1) * 512], q_hp[pbi])
                        wait("tensor", q_hp[pbi], t_hp)
                        if piece == 0:
                            for hc in range(8):
                                pe_guard(hc)
                        for icl in range(16):
                            ic = piece * 16 + icl
                            for hc in range(8):
                                first = (ic == 0)
                                last = (ic == NIC - 1)
                                piece_end = (icl == 15 and hc == 7)
                                want_inc = last or piece_end
                                t = op("tensor",
                                       lambda e, ic=ic, hc=hc, pbi=pbi,
                                       icl=icl, first=first, last=last:
                                       e.matmul(
                                           out=ps[hc][:, :],
                                           lhsT=wdT[:, ic * 1024 + hc * 128:
                                                    ic * 1024 + hc * 128 + 128],
                                           rhs=h_p[pbi][:, icl * 512:
                                                        (icl + 1) * 512],
                                           start=first, stop=last),
                                       inc=pe if want_inc else None)
                                if last:
                                    dn_pe[hc] = t
                                if piece_end:
                                    hp_use[pbi] = t
                                    if slab == 3:
                                        blk_pe[half][piece] = t
                    if gf_store[sbi]:
                        wait("scalar", q_gf[sbi], gf_store[sbi])
                    t_gf = 0
                    for hc in range(8):
                        wait("scalar", pe, dn_pe[hc])
                        t_gf = op("scalar", lambda e, hc=hc, half=half,
                                  sbi=sbi:
                                  e.activation(
                                      out=gf_t[sbi][:, hc * 512:
                                                    (hc + 1) * 512],
                                      in_=ps[hc][:, :], func=AF.Silu,
                                      scale=sd_b[:, half * 8 + hc:
                                                 half * 8 + hc + 1]), inc=sc)
                        bank_free[hc] = (sc, t_gf)
                    wait("scalar", sc, t_gf)
                    gf_store[sbi] = dma(
                        "scalar",
                        gf_s.ap().rearrange("(o p) t -> p o t", p=128)
                        [:, half * 8:(half + 1) * 8,
                         slab * 512:(slab + 1) * 512],
                        gf_t[sbi][:].rearrange("p (o t) -> p o t", o=8),
                        q_gf[sbi])

        fence()

        with ExitStack() as sc8:
            NB8 = 4
            q8t = qsem("q8t", "sync", NB8)
            q8x = qsem("q8x", "gpsimd", NB8)
            q8y = qsem("q8y", "scalar", NB8)
            s_t8 = [sbt(sc8, f"s_t8{i}", [128, DM], BF16) for i in range(NB8)]
            g_t8 = [sbt(sc8, f"g_t8{i}", [128, DM], BF16) for i in range(NB8)]
            x_t8 = [sbt(sc8, f"x_t8{i}", [128, DM], F32) for i in range(NB8)]
            sgp = sbt(sc8, "sgp", [128, DM], F32)
            o_t8 = [sbt(sc8, f"o_t8{i}", [128, DM], F32) for i in range(NB8)]
            tl = [0] * NB8
            xl = [0] * NB8
            ost = [0] * NB8
            ouse = [0] * NB8
            for tt in range(NF):
                bi = tt % NB8
                if ouse[bi]:
                    wait("sync", vc, ouse[bi])
                    wait("gpsimd", vc, ouse[bi])
                dma("sync", s_t8[bi][:],
                    s_s[0:DM, tt * 128:(tt + 1) * 128], q8t[bi],
                    transpose=True)
                tl[bi] = dma("sync", g_t8[bi][:],
                             gf_s[0:DM, tt * 128:(tt + 1) * 128], q8t[bi],
                             transpose=True)
                xl[bi] = dma("gpsimd", x_t8[bi][:],
                             x_in[tt * 128:(tt + 1) * 128, :], q8x[bi])
                wait("vector", q8t[bi], tl[bi])
                wait("vector", q8x[bi], xl[bi])
                if ost[bi]:
                    wait("vector", q8y[bi], ost[bi])
                op("vector", lambda e, bi=bi: e.tensor_mul(
                    out=sgp[:], in0=s_t8[bi][:], in1=g_t8[bi][:]))
                ouse[bi] = op("vector", lambda e, bi=bi: e.tensor_add(
                    out=o_t8[bi][:], in0=sgp[:], in1=x_t8[bi][:]), inc=vc)
                wait("scalar", vc, ouse[bi])
                ost[bi] = dma("scalar", y_d[tt * 128:(tt + 1) * 128, :],
                              o_t8[bi][:], q8y[bi])
            for i in range(NB8):
                wait("scalar", q8y[i], q8y[i].n)
                wait("sync", q8t[i], q8t[i].n)
                wait("gpsimd", q8x[i], q8x[i].n)

        with nc.Block() as block:
            @block.sync
            def _(e):
                for f in streams["sync"]:
                    f(e)

            @block.scalar
            def _(e):
                for f in streams["scalar"]:
                    f(e)

            @block.vector
            def _(e):
                for f in streams["vector"]:
                    f(e)

            @block.gpsimd
            def _(e):
                for f in streams["gpsimd"]:
                    f(e)

            @block.tensor
            def _(e):
                for f in streams["tensor"]:
                    f(e)

    return nc


_NC_CACHE = []


def _get_nc():
    if not _NC_CACHE:
        _NC_CACHE.append(build_nc())
    return _NC_CACHE[0]


def kernel(L0_attn_x, L0_attn_w_qkv, L0_attn_w_out, L0_attn_decay_param,
           L0_ffn_gate_w, L0_ffn_gate_scale, L0_ffn_up_w, L0_ffn_up_scale,
           L0_ffn_down_w, L0_ffn_down_scale):
    x = np.ascontiguousarray(np.asarray(L0_attn_x, dtype=np.float32))
    wqkv = np.ascontiguousarray(np.asarray(L0_attn_w_qkv, dtype=np.float32))
    wout = np.ascontiguousarray(np.asarray(L0_attn_w_out, dtype=np.float32))
    dk = np.asarray(L0_attn_decay_param, dtype=np.float32)
    def _u8T(a, rows, cols):
        a = np.ascontiguousarray(np.asarray(a, dtype=np.int32))
        return np.ascontiguousarray(
            a.view(np.uint8).reshape(rows, cols, 4)[:, :, 0].T)
    gw = _u8T(L0_ffn_gate_w, IF, DM)
    uw = _u8T(L0_ffn_up_w, IF, DM)
    dw = _u8T(L0_ffn_down_w, DM, IF)
    gs = np.asarray(L0_ffn_gate_scale, dtype=np.float32)
    us = np.asarray(L0_ffn_up_scale, dtype=np.float32)
    ds = np.asarray(L0_ffn_down_scale, dtype=np.float32)

    dk_rep = np.ascontiguousarray(np.broadcast_to(dk[None, :], (128, H)))
    sg = np.ascontiguousarray(gs.reshape(NIC, 128).T)
    su = np.ascontiguousarray(us.reshape(NIC, 128).T)
    sd = np.ascontiguousarray(ds.reshape(NF, 128).T)

    nc = _get_nc()
    in_maps = []
    for c in range(NCORES):
        b, half = c // 2, c % 2
        rm = np.zeros((128, 256), np.float32)
        if half == 1:
            r = c - 1
            rm[:, r * 32:(r + 1) * 32] = 1.0
        in_maps.append(dict(
            x_in=np.ascontiguousarray(x[b, half * TC:(half + 1) * TC, :]),
            w_qkv=wqkv, w_out=wout, decay_rep=dk_rep,
            gate_wT=gw, up_wT=uw, down_wT=dw,
            sg=sg, su=su, sd=sd, rm=rm,
        ))
    res = run_bass_kernel_spmd(nc, in_maps, core_ids=list(range(NCORES)))
    out = np.empty((B, T, DM), np.float32)
    for c in range(NCORES):
        b, half = c // 2, c % 2
        out[b, half * TC:(half + 1) * TC, :] = res.results[c]["y"]
    return out


# version 10
# speedup vs baseline: 1.2055x; 1.0018x over previous
import numpy as np
from contextlib import ExitStack

import concourse.bass as bass
import concourse.mybir as mybir
from concourse.bass_utils import run_bass_kernel_spmd

F32 = mybir.dt.float32
BF16 = mybir.dt.bfloat16
I32 = mybir.dt.int32
AF = mybir.ActivationFunctionType
OP = mybir.AluOpType
AX = mybir.AxisListType

B, T, DM = 4, 4096, 2048
H = 16
TC = 2048
NF = DM // 128
IF = 8192
NIC = IF // 128
NCORES = 8

ENGS = ("sync", "scalar", "vector", "gpsimd", "tensor")


class Ctr:
    __slots__ = ("h", "n")

    def __init__(self, h):
        self.h = h
        self.n = 0

    def bump(self, k=1):
        self.n += k
        return self.n


def build_nc(dbg=()):
    nc = bass.Bass("TRN2", target_bir_lowering=False)

    def _skind(name):
        return "ExternalOutput" if name in dbg else "Internal"

    x_in = nc.dram_tensor("x_in", [TC, DM], F32, kind="ExternalInput")
    wqkv_d = nc.dram_tensor("w_qkv", [DM, 3 * DM], F32, kind="ExternalInput")
    wout_d = nc.dram_tensor("w_out", [DM, DM], F32, kind="ExternalInput")
    dk_d = nc.dram_tensor("decay_rep", [128, H], F32, kind="ExternalInput")
    U8 = mybir.dt.uint8
    gw_d = nc.dram_tensor("gate_wT", [DM, IF], U8, kind="ExternalInput")
    uw_d = nc.dram_tensor("up_wT", [DM, IF], U8, kind="ExternalInput")
    dw_d = nc.dram_tensor("down_wT", [IF, DM], U8, kind="ExternalInput")
    sg_d = nc.dram_tensor("sg", [128, NIC], F32, kind="ExternalInput")
    su_d = nc.dram_tensor("su", [128, NIC], F32, kind="ExternalInput")
    sd_d = nc.dram_tensor("sd", [128, NF], F32, kind="ExternalInput")
    rm_d = nc.dram_tensor("rm", [128, 256], F32, kind="ExternalInput")

    y_d = nc.dram_tensor("y", [TC, DM], F32, kind="ExternalOutput")

    xn_s = nc.dram_tensor("xn_s", [TC, DM], BF16, kind=_skind("xn_s"))
    kv_s = nc.dram_tensor("kv_s", [2 * H * 128, TC], BF16, kind=_skind("kv_s"))
    ap_s = nc.dram_tensor("ap_s", [DM, TC], BF16, kind=_skind("ap_s"))
    s_s = nc.dram_tensor("s_s", [DM, TC], BF16, kind=_skind("s_s"))
    h_s = nc.dram_tensor("h_s", [IF, TC], BF16, kind=_skind("h_s"))
    gf_s = nc.dram_tensor("gf_s", [DM, TC], BF16, kind=_skind("gf_s"))
    cc_i = nc.dram_tensor("cc_i", [128, 2 * H], F32)
    cc_o = nc.dram_tensor("cc_o", [128 * NCORES, 2 * H], F32)
    st_dbg = nc.dram_tensor("st_dbg", [128, 4 * H], F32, kind=_skind("st_dbg"))

    streams = {e: [] for e in ENGS}

    def em(eng, fn):
        streams[eng].append(fn)

    def wait(eng, ctr, tgt):
        if tgt and tgt > 0:
            em(eng, lambda e, c=ctr, t=tgt: e.wait_ge(c.h, t))

    def dma(q, out, in_, ctr, transpose=False):
        t = ctr.bump(16)
        em(q, lambda e, o=out, i=in_, c=ctr, tr=transpose:
           e.dma_start(out=o, in_=i, transpose=tr).then_inc(c.h, 16))
        return t

    def op(eng, fn, inc=None):
        if inc is None:
            em(eng, fn)
            return None
        t = inc.bump(1)
        em(eng, lambda e, f=fn, c=inc: f(e).then_inc(c.h, 1))
        return t

    with ExitStack() as root:
        def sbt(scope, name, shape, dt):
            return scope.enter_context(nc.sbuf_tensor(name, shape, dt))

        def sem(name):
            return Ctr(root.enter_context(nc.semaphore(name)))

        cc = sem("cc")
        pe = sem("pe")
        vc = sem("vc")
        sc = sem("sc")
        bar = sem("bar")

        dma_ctrs = []

        def qsem(name, eng, n=1):
            out = []
            for i in range(n):
                c = sem(f"{name}{i}")
                dma_ctrs.append((eng, c))
                out.append(c)
            return out if n > 1 else out[0]

        q_c = qsem("q_c", "gpsimd")
        q_x = qsem("q_x", "gpsimd", 4)
        q_wug = qsem("q_wug", "gpsimd", 3)
        q_wuu = qsem("q_wuu", "gpsimd", 3)
        q_w = qsem("q_w", "gpsimd", 2)
        q_kst = qsem("q_kst", "scalar", 2)
        q_kv4 = qsem("q_kv4", "sync", 2)
        q_ap = qsem("q_ap", "scalar", 2)
        q_xn = qsem("q_xn", "scalar", 4)
        q_cc2 = qsem("q_cc2", "gpsimd")
        q_t = qsem("q_t", "sync", 2)
        q_apl = qsem("q_apl", "sync", 2)
        q_wo = qsem("q_wo", "gpsimd")
        q_sev = qsem("q_sev", "scalar", 2)
        q_hst = qsem("q_hst", "scalar", 2)
        q_hp = qsem("q_hp", "scalar", 2)
        q_hp2 = qsem("q_hp2", "sync", 2)
        q_s7 = qsem("q_s7", "sync", 2)
        q_gf = qsem("q_gf", "scalar", 2)
        q_y = qsem("q_y", "scalar", 2)
        po = sem("po")

        def fence():
            for eng, c in dma_ctrs:
                wait(eng, c, c.n)
            for e in ENGS:
                em(e, lambda eng: eng.sem_inc(bar.h, 1))
            tgt = bar.bump(5)
            for e in ENGS:
                wait(e, bar, tgt)

        ps = [root.enter_context(nc.psum_tensor(f"ps{i}", [128, 512], F32))
              for i in range(8)]
        bank_free = [None] * 8

        def pe_guard(i):
            if bank_free[i] is not None:
                c, t = bank_free[i]
                wait("tensor", c, t)
                bank_free[i] = None

        dk_raw = sbt(root, "dk_raw", [128, H], F32)
        dk_rep = sbt(root, "dk_rep", [128, H], F32)
        sg_b = sbt(root, "sg_b", [128, NIC], F32)
        su_b = sbt(root, "su_b", [128, NIC], F32)
        sd_b = sbt(root, "sd_b", [128, NF], F32)
        rm_b = sbt(root, "rm_b", [128, 256], F32)
        g_b = sbt(root, "g_b", [128, 256], F32)
        gm_b = sbt(root, "gm_b", [128, 256], F32)
        ssel = sbt(root, "ssel", [128, 2 * H], F32)
        states = sbt(root, "states", [128, 2 * H], F32)
        ones_cb = sbt(root, "ones_cb", [128, 1], BF16)
        ones_rb = sbt(root, "ones_rb", [1, 128], BF16)
        cst_b = sbt(root, "cst_b", [128, 4], F32)

        nc.const_aps.register(F32, -128.0, cst_b[:, 1:2])
        t_dk = dma("gpsimd", dk_raw[:], dk_d[:, :], q_c)
        for dst, src in ((sg_b, sg_d), (su_b, su_d), (sd_b, sd_d), (rm_b, rm_d)):
            dma("gpsimd", dst[:], src[:, :], q_c)
        op("vector", lambda e: e.memset(ones_cb[:], 1.0))
        op("vector", lambda e: e.memset(cst_b[:, 0:1], 1e-5))
        op("vector", lambda e: e.memset(cst_b[:, 1:2], -128.0))
        op("vector", lambda e: e.memset(cst_b[:, 2:3], -1e-6))
        op("vector", lambda e: e.memset(cst_b[:, 3:4], 1e-6))
        t_ones = op("vector", lambda e: e.memset(ones_rb[:], 1.0), inc=vc)
        wait("scalar", q_c, q_c.n)
        wait("scalar", vc, t_ones)
        t_sig = op("scalar", lambda e: e.activation(
            out=dk_rep[:], in_=dk_raw[:], func=AF.Sigmoid), inc=sc)

        with ExitStack() as scA:
            xnT = sbt(scA, "xnT", [128, NF * TC], BF16)

            def xnT_sl(fc, t0, tn):
                return xnT[:, fc * TC + t0: fc * TC + t0 + tn]

            with ExitStack() as scL:
                NLP = 4
                xt = [sbt(scL, f"xt{i}", [128, DM], F32) for i in range(NLP)]
                xc = [sbt(scL, f"xc{i}", [128, DM], F32) for i in range(NLP)]
                sqw = [sbt(scL, f"sqw{i}", [128, DM], F32)
                       for i in range(NLP)]
                s1 = [sbt(scL, f"s1_{i}", [128, 1], F32) for i in range(NLP)]
                s2 = [sbt(scL, f"s2_{i}", [128, 1], F32) for i in range(NLP)]
                s3 = [sbt(scL, f"s3_{i}", [128, 1], F32) for i in range(NLP)]
                xnb = [sbt(scL, f"xnb{i}", [128, DM], BF16)
                       for i in range(NLP)]
                ln_st = [0] * NLP
                ln_use = [0] * NLP
                t_r_par = [0] * NLP
                for tt in range(NF):
                    bi = tt % NLP
                    if ln_use[bi]:
                        wait("gpsimd", vc, ln_use[bi])
                    t_x = dma("gpsimd", xt[bi][:],
                              x_in[tt * 128:(tt + 1) * 128, :], q_x[bi])
                    wait("vector", q_x[bi], t_x)
                    wait("vector", sc, t_r_par[bi])
                    t_sum = op("vector", lambda e, bi=bi: e.reduce_sum(
                        out=s1[bi][:], in_=xt[bi][:], axis=AX.X), inc=vc)
                    wait("scalar", vc, t_sum)
                    t_m = op("scalar", lambda e, bi=bi: e.activation(
                        out=s2[bi][:], in_=s1[bi][:], func=AF.Copy,
                        scale=1.0 / DM), inc=sc)
                    wait("vector", sc, t_m)
                    t_c = op("vector", lambda e, bi=bi: e.tensor_scalar(
                        out=xc[bi][:], in0=xt[bi][:], scalar1=s2[bi][:, 0:1],
                        scalar2=None, op0=OP.subtract), inc=vc)
                    wait("scalar", vc, t_c)
                    t_sq = op("scalar", lambda e, bi=bi: e.activation(
                        out=sqw[bi][:], in_=xc[bi][:], func=AF.Square), inc=sc)
                    wait("vector", sc, t_sq)
                    t_rs = op("vector", lambda e, bi=bi: e.reduce_sum(
                        out=s1[bi][:], in_=sqw[bi][:], axis=AX.X), inc=vc)
                    wait("scalar", vc, t_rs)
                    op("scalar", lambda e, bi=bi: e.activation(
                        out=s2[bi][:], in_=s1[bi][:], func=AF.Ln, scale=1.0 / DM,
                        bias=cst_b[:, 0:1]))
                    t_r = op("scalar", lambda e, bi=bi: e.activation(
                        out=s3[bi][:], in_=s2[bi][:], func=AF.Exp,
                        scale=-0.5), inc=sc)
                    t_r_par[bi] = t_r
                    if ln_st[bi]:
                        wait("vector", q_xn[bi], ln_st[bi])
                    wait("vector", sc, t_r)
                    t_xn = op("vector", lambda e, bi=bi: e.tensor_scalar(
                        out=xnb[bi][:], in0=xc[bi][:], scalar1=s3[bi][:, 0:1],
                        scalar2=None, op0=OP.mult), inc=vc)
                    ln_use[bi] = t_xn
                    wait("scalar", vc, t_xn)
                    ln_st[bi] = dma("scalar", xn_s[tt * 128:(tt + 1) * 128, :],
                                    xnb[bi][:], q_xn[bi])
                for i in range(4):
                    wait("sync", q_xn[i], q_xn[i].n)
                for fc in range(NF):
                    dma("sync", xnT_sl(fc, 0, TC),
                        xn_s[0:TC, fc * 128:(fc + 1) * 128], q_t[0],
                        transpose=True)
                t_xnT = q_t[0].n

            fence()

            with ExitStack() as scH:
                ones_t = sbt(scH, "ones_t", [128, TC], BF16)
                dk_tl = [sbt(scH, f"dk_tl{i}", [128, TC], F32) for i in range(2)]
                t_onest = op("vector", lambda e: e.memset(ones_t[:], 1.0),
                             inc=vc)
                wkv_b = [sbt(scH, f"wkv{i}", [128, 2 * NF * 128], BF16)
                         for i in range(2)]
                phik = [sbt(scH, f"phik{i}", [128, TC], BF16) for i in range(2)]
                v_b = [sbt(scH, f"vb{i}", [128, TC], BF16) for i in range(2)]
                bkv = sbt(scH, "bkv", [128, TC], F32)
                t1b = sbt(scH, "t1b", [128, 512], F32)
                t2b = [sbt(scH, f"t2b{i}", [128, 512], F32) for i in range(2)]
                q_f = sbt(scH, "q_f", [128, TC], F32)
                rkv_f = sbt(scH, "rkv_f", [128, TC], F32)
                rks_f = sbt(scH, "rks_f", [128, TC], F32)
                prod_b = [sbt(scH, f"prod_b{i}", [128, TC], BF16)
                          for i in range(2)]
                num_b = sbt(scH, "num_b", [128, TC], F32)
                apb = [sbt(scH, f"apb{i}", [128, TC], BF16) for i in range(2)]
                recip_b = sbt(scH, "recip_b", [1, TC], BF16)
                den_row = sbt(scH, "den_row", [1, TC], F32)
                recip_row = sbt(scH, "recip_row", [1, TC], F32)

                QCOL, KCOL, VCOL = 0, DM, 2 * DM
                stt_busy = [0, 0]

                def emit_phi(bank_pe_t, bank, tt, dst_ap):
                    i2 = tt % 2
                    wait("scalar", pe, bank_pe_t)
                    if stt_busy[i2]:
                        wait("scalar", vc, stt_busy[i2])
                    op("scalar", lambda e, b=bank: e.activation(
                        out=t1b[:], in_=ps[b][:, :], func=AF.Relu,
                        scale=-1.0))
                    t2t = op("scalar", lambda e, i2=i2: e.activation(
                        out=t2b[i2][:], in_=t1b[:], func=AF.Exp,
                        scale=-1.0), inc=sc)
                    wait("vector", sc, t2t)
                    t = op("vector", lambda e, b=bank, i2=i2, d=dst_ap:
                           e.scalar_tensor_tensor(
                               out=d, in0=ps[b][:, :], scalar=0.0,
                               in1=t2b[i2][:], op0=OP.max, op1=OP.add), inc=vc)
                    stt_busy[i2] = t
                    bank_free[bank] = (vc, t)
                    return t

                wkv_load = [0, 0]
                wkv_inuse = [0, 0]
                kv_store = [0, 0]
                dk_fill = [0, 0]
                dk_used = [0, 0]
                st_copied = [0]

                def load_wkv(h):
                    bi = h % 2
                    if wkv_inuse[bi]:
                        wait("gpsimd", pe, wkv_inuse[bi])
                    for j, col0 in ((0, KCOL + h * 128), (1, VCOL + h * 128)):
                        wkv_load[bi] = dma(
                            "gpsimd",
                            wkv_b[bi][:, j * NF * 128:(j + 1) * NF * 128]
                            .rearrange("p (f c) -> p f c", f=NF),
                            wqkv_d[:, col0:col0 + 128]
                            .rearrange("(f p) c -> p f c", p=128), q_w[bi])

                load_wkv(0)
                load_wkv(1)

                for h in range(H):
                    bi = h % 2
                    wait("tensor", q_w[bi], wkv_load[bi])
                    if h == 0:
                        wait("tensor", q_t[0], t_xnT)
                    kv_pe = [0] * 8
                    for j, base in ((0, 0), (1, 4)):
                        for bk in range(4):
                            pe_guard(base + bk)
                        for fc in range(NF):
                            for tt in range(4):
                                b_ = base + tt
                                last = (fc == NF - 1)
                                lh = wkv_b[bi][:, (j * NF + fc) * 128:
                                               (j * NF + fc) * 128 + 128]
                                t = op("tensor",
                                       lambda e, lh=lh, fc=fc, tt=tt, b_=b_,
                                       last=last:
                                       e.matmul(out=ps[b_][:, :], lhsT=lh,
                                                rhs=xnT_sl(fc, tt * 512, 512),
                                                start=(fc == 0), stop=last),
                                       inc=pe if last else None)
                                if last:
                                    kv_pe[b_] = t
                    wkv_inuse[bi] = kv_pe[7]
                    if h + 1 < H:
                        load_wkv(h + 1)
                    if kv_store[bi]:
                        wait("vector", q_kst[bi], kv_store[bi])
                    phi_t = 0
                    for tt in range(4):
                        phi_t = emit_phi(kv_pe[tt], tt, tt,
                                         phik[bi][:, tt * 512:(tt + 1) * 512])
                    vcp_t = 0
                    for tt in range(4):
                        wait("vector", pe, kv_pe[4 + tt])
                        vcp_t = op("vector", lambda e, tt=tt, bi=bi:
                                   e.tensor_copy(
                                       out=v_b[bi][:, tt * 512:(tt + 1) * 512],
                                       in_=ps[4 + tt][:, :]), inc=vc)
                        bank_free[4 + tt] = (vc, vcp_t)
                    op("vector", lambda e, bi=bi: e.tensor_mul(
                        out=bkv[:], in0=phik[bi][:], in1=v_b[bi][:]))
                    if st_copied[0]:
                        wait("vector", sc, st_copied[0])
                    if dk_used[bi]:
                        wait("scalar", vc, dk_used[bi])
                    if h < 2:
                        wait("scalar", sc, t_sig)
                        wait("scalar", vc, t_onest)
                    dk_fill[bi] = op("scalar", lambda e, bi=bi, h=h: e.mul(
                        out=dk_tl[bi][:], in_=ones_t[:],
                        mul=dk_rep[:, h:h + 1]), inc=sc)
                    wait("vector", sc, dk_fill[bi])
                    op("vector", lambda e, bi=bi: e.tensor_tensor_scan(
                        out=rkv_f[:], data0=dk_tl[bi][:], data1=bkv[:],
                        initial=0.0, op0=OP.mult, op1=OP.add))
                    dk_used[bi] = op(
                        "vector", lambda e, bi=bi: e.tensor_tensor_scan(
                            out=rks_f[:], data0=dk_tl[bi][:], data1=phik[bi][:],
                            initial=0.0, op0=OP.mult, op1=OP.add), inc=vc)
                    wait("scalar", vc, dk_used[bi])
                    op("scalar", lambda e, h=h: e.copy(
                        out=states[:, 2 * h:2 * h + 1],
                        in_=rkv_f[:, TC - 1:TC]))
                    st_copied[0] = op("scalar", lambda e, h=h: e.copy(
                        out=states[:, 2 * h + 1:2 * h + 2],
                        in_=rks_f[:, TC - 1:TC]), inc=sc)
                    wait("scalar", vc, max(vcp_t, phi_t))
                    dma("scalar", kv_s[(2 * h) * 128:(2 * h + 1) * 128, :],
                        phik[bi][:], q_kst[bi])
                    kv_store[bi] = dma(
                        "scalar", kv_s[(2 * h + 1) * 128:(2 * h + 2) * 128, :],
                        v_b[bi][:], q_kst[bi])

                fence()

                dma("gpsimd", cc_i[:, :], states[:], q_cc2)
                wait("gpsimd", q_cc2, q_cc2.n)
                em("gpsimd", lambda e: e.collective_compute(
                    "AllGather", OP.bypass,
                    replica_groups=[list(range(NCORES))],
                    ins=[cc_i.ap().opt()], outs=[cc_o.ap().opt()],
                ).then_inc(cc.h))
                cc.bump(1)

                wq_load = [0, 0]
                wq_inuse = [0, 0]
                kv4_load = [0, 0]
                kv4_used = [0, 0]
                ap_store = [0, 0]
                recip_used = 0

                def load_wq(h):
                    bi = h % 2
                    if wq_inuse[bi]:
                        wait("gpsimd", pe, wq_inuse[bi])
                    wq_load[bi] = dma(
                        "gpsimd",
                        wkv_b[bi][:, 0:NF * 128]
                        .rearrange("p (f c) -> p f c", f=NF),
                        wqkv_d[:, QCOL + h * 128:QCOL + h * 128 + 128]
                        .rearrange("(f p) c -> p f c", p=128), q_w[bi])

                def load_kv4(h):
                    bi = h % 2
                    if kv4_used[bi]:
                        wait("sync", vc, kv4_used[bi])
                    dma("sync", phik[bi][:],
                        kv_s[(2 * h) * 128:(2 * h + 1) * 128, :], q_kv4[bi])
                    kv4_load[bi] = dma(
                        "sync", v_b[bi][:],
                        kv_s[(2 * h + 1) * 128:(2 * h + 2) * 128, :], q_kv4[bi])

                load_wq(0)
                load_kv4(0)
                load_wq(1)
                load_kv4(1)

                wait("gpsimd", cc, 1)
                t_g = dma("gpsimd", g_b[:].rearrange("p (r s) -> p r s", r=8),
                          cc_o.ap().rearrange("(r p) s -> p r s", p=128), q_cc2)
                wait("vector", q_cc2, t_g)
                t_gm = op("vector", lambda e: e.tensor_mul(
                    out=gm_b[:], in0=g_b[:], in1=rm_b[:]), inc=vc)
                wait("scalar", vc, t_gm)
                t_gm2 = op("scalar", lambda e: e.copy(
                    out=g_b[:], in_=gm_b[:]), inc=sc)
                wait("vector", sc, t_gm2)
                t_sel = op("vector", lambda e: e.tensor_reduce(
                    out=ssel[:], in_=g_b[:].rearrange("p (r s) -> p s r", r=8),
                    axis=AX.X, op=OP.add), inc=vc)
                if "st_dbg" in dbg:
                    wait("gpsimd", vc, t_sel)
                    dma("gpsimd", st_dbg[:, 0:2 * H], states[:], q_cc2)
                    dma("gpsimd", st_dbg[:, 2 * H:4 * H], ssel[:], q_cc2)

                st4 = {}

                def den_mm(h):
                    par = h % 2
                    wait("tensor", vc, st4[h]["t_prod"])
                    if h == 0:
                        wait("tensor", vc, t_ones)
                    den_pe = [0] * 4
                    for c in range(4):
                        pe_guard(4 + c)
                        den_pe[c] = op("tensor", lambda e, c=c, par=par:
                                       e.matmul(
                            out=ps[4 + c][0:1, :], lhsT=ones_cb[:],
                            rhs=prod_b[par][:, c * 512:(c + 1) * 512],
                            start=True, stop=True), inc=pe)
                    st4[h]["den_pe"] = den_pe

                def den_chain(h):
                    nonlocal recip_used
                    if recip_used:
                        wait("scalar", pe, recip_used)
                    t_dr = 0
                    for c in range(4):
                        wait("scalar", pe, st4[h]["den_pe"][c])
                        t_dr = op("scalar", lambda e, c=c: e.activation(
                            out=den_row[0:1, c * 512:(c + 1) * 512],
                            in_=ps[4 + c][0:1, :], func=AF.Relu,
                            bias=cst_b[0:1, 2:3]), inc=sc)
                        bank_free[4 + c] = (sc, t_dr)
                    op("scalar", lambda e: e.activation(
                        out=recip_row[0:1, :], in_=den_row[0:1, :], func=AF.Ln,
                        bias=cst_b[0:1, 3:4]))
                    st4[h]["t_rc"] = op("scalar", lambda e: e.activation(
                        out=recip_b[0:1, :], in_=recip_row[0:1, :],
                        func=AF.Exp, scale=-1.0), inc=sc)

                def rep_mm(h):
                    nonlocal recip_used
                    wait("tensor", sc, st4[h]["t_rc"])
                    rep_pe = [0] * 4
                    for c in range(4):
                        pe_guard(4 + c)
                        rep_pe[c] = op("tensor", lambda e, c=c: e.matmul(
                            out=ps[4 + c][:, :], lhsT=ones_rb[:],
                            rhs=recip_b[0:1, c * 512:(c + 1) * 512],
                            start=True, stop=True), inc=pe)
                    st4[h]["rep_pe"] = rep_pe
                    recip_used = rep_pe[3]

                def ap_out(h):
                    abi = h % 2
                    if ap_store[abi]:
                        wait("vector", q_ap[abi], ap_store[abi])
                    rep_pe = st4[h]["rep_pe"]
                    t_ap = 0
                    for c in range(4):
                        wait("vector", pe, rep_pe[c])
                        t_ap = op("vector", lambda e, c=c, abi=abi:
                                  e.tensor_tensor(
                            out=apb[abi][:, c * 512:(c + 1) * 512],
                            in0=num_b[:, c * 512:(c + 1) * 512],
                            in1=ps[4 + c][:, :], op=OP.mult), inc=vc)
                        bank_free[4 + c] = (vc, t_ap)
                    wait("scalar", vc, t_ap)
                    ap_store[abi] = dma(
                        "scalar", ap_s[h * 128:(h + 1) * 128, :],
                        apb[abi][:], q_ap[abi])

                for h in range(H):
                    bi = h % 2
                    par = h % 2
                    st4[h] = {}
                    wait("tensor", q_w[bi], wq_load[bi])
                    q_pe = [0] * 4
                    for bk in range(4):
                        pe_guard(bk)

                    def qmm(f0, f1, q_pe=q_pe, bi=bi):
                        for fc in range(f0, f1):
                            for tt in range(4):
                                last = (fc == NF - 1)
                                lh = wkv_b[bi][:, fc * 128:(fc + 1) * 128]
                                t = op("tensor", lambda e, lh=lh, fc=fc,
                                       tt=tt, last=last:
                                       e.matmul(out=ps[tt][:, :], lhsT=lh,
                                                rhs=xnT_sl(fc, tt * 512, 512),
                                                start=(fc == 0), stop=last),
                                       inc=pe if last else None)
                                if last:
                                    q_pe[tt] = t

                    qmm(0, 8)
                    if h >= 1:
                        den_mm(h - 1)
                        den_chain(h - 1)
                    qmm(8, NF)
                    wq_inuse[bi] = q_pe[3]
                    if h + 1 < H:
                        load_wq(h + 1)
                        load_kv4(h + 1)
                    if h >= 1:
                        rep_mm(h - 1)
                        op("vector", lambda e: e.tensor_mul(
                            out=num_b[:], in0=q_f[:], in1=rkv_f[:]))
                    for tt in range(4):
                        emit_phi(q_pe[tt], tt, tt,
                                 q_f[:, tt * 512:(tt + 1) * 512])
                    wait("vector", q_kv4[bi], kv4_load[bi])
                    op("vector", lambda e, bi=bi: e.tensor_mul(
                        out=bkv[:], in0=phik[bi][:], in1=v_b[bi][:]))
                    if dk_used[bi]:
                        wait("scalar", vc, dk_used[bi])
                    dk_fill[bi] = op("scalar", lambda e, bi=bi, h=h: e.mul(
                        out=dk_tl[bi][:], in_=ones_t[:],
                        mul=dk_rep[:, h:h + 1]), inc=sc)
                    wait("vector", sc, dk_fill[bi])
                    op("vector", lambda e, bi=bi, h=h: e.tensor_tensor_scan(
                        out=rkv_f[:], data0=dk_tl[bi][:], data1=bkv[:],
                        initial=ssel[:, 2 * h:2 * h + 1],
                        op0=OP.mult, op1=OP.add))
                    kv4_used[bi] = op(
                        "vector", lambda e, bi=bi, h=h: e.tensor_tensor_scan(
                            out=rks_f[:], data0=dk_tl[bi][:], data1=phik[bi][:],
                            initial=ssel[:, 2 * h + 1:2 * h + 2],
                            op0=OP.mult, op1=OP.add), inc=vc)
                    dk_used[bi] = kv4_used[bi]
                    if h >= 2:
                        wait("vector", pe, st4[h - 2]["den_pe"][3])
                    st4[h]["t_prod"] = op("vector", lambda e, par=par:
                                          e.tensor_mul(
                        out=prod_b[par][:], in0=q_f[:], in1=rks_f[:]), inc=vc)
                    if h >= 1:
                        ap_out(h - 1)
                den_mm(H - 1)
                den_chain(H - 1)
                rep_mm(H - 1)
                op("vector", lambda e: e.tensor_mul(
                    out=num_b[:], in0=q_f[:], in1=rkv_f[:]))
                ap_out(H - 1)

            fence()

            with ExitStack() as sc5:
                wo_b = sbt(sc5, "wo_b", [128, NF * DM], BF16)
                ap_sl = [sbt(sc5, f"ap_sl{i}", [128, NF * 512], BF16)
                         for i in range(2)]
                sev = [sbt(sc5, f"sev{i}", [128, 512], BF16) for i in range(2)]
                for oc in range(NF):
                    dma("gpsimd",
                        wo_b[:, oc * DM:(oc + 1) * DM]
                        .rearrange("p (f c) -> p f c", f=NF),
                        wout_d[:, oc * 128:(oc + 1) * 128]
                        .rearrange("(f p) c -> p f c", p=128), q_wo)
                wait("tensor", q_wo, q_wo.n)
                ap_use = [0, 0]
                sev_store = [0, 0]
                for tt in range(4):
                    bi = tt % 2
                    if ap_use[bi]:
                        wait("sync", pe, ap_use[bi])
                    t_apl = dma(
                        "sync",
                        ap_sl[bi][:].rearrange("p (f t) -> p f t", f=NF),
                        ap_s.ap().rearrange("(f p) t -> p f t", p=128)
                        [:, :, tt * 512:(tt + 1) * 512], q_apl[bi])
                    wait("tensor", q_apl[bi], t_apl)
                    last_mm = 0
                    for oc in range(NF):
                        bk = oc % 2
                        si = oc % 2
                        pe_guard(bk)
                        for fc in range(NF):
                            last = (fc == NF - 1)
                            t = op("tensor", lambda e, bi=bi, fc=fc, oc=oc,
                                   bk=bk, last=last:
                                   e.matmul(
                                       out=ps[bk][:, :],
                                       lhsT=wo_b[:, oc * DM + fc * 128:
                                                 oc * DM + fc * 128 + 128],
                                       rhs=ap_sl[bi][:, fc * 512:
                                                     (fc + 1) * 512],
                                       start=(fc == 0), stop=last),
                                   inc=pe if last else None)
                            if last:
                                last_mm = t
                        wait("scalar", pe, last_mm)
                        if sev_store[si]:
                            wait("scalar", q_sev[si], sev_store[si])
                        t_ev = op("scalar", lambda e, si=si, bk=bk: e.copy(
                            out=sev[si][:], in_=ps[bk][:, :]), inc=sc)
                        bank_free[bk] = (sc, t_ev)
                        sev_store[si] = dma(
                            "scalar",
                            s_s[oc * 128:(oc + 1) * 128,
                                tt * 512:(tt + 1) * 512], sev[si][:], q_sev[si])
                    ap_use[bi] = last_mm

            fence()

            with ExitStack() as sc6:
                NWB = 3
                wgT = [sbt(sc6, f"wgT{i}", [128, NF * 512], BF16)
                       for i in range(NWB)]
                wuT = [sbt(sc6, f"wuT{i}", [128, NF * 512], BF16)
                       for i in range(NWB)]
                h_b = [sbt(sc6, f"h_b{i}", [128, TC], BF16) for i in range(2)]
                gact = [sbt(sc6, f"gact{i}", [128, 512], F32) for i in range(2)]
                wT_cv = [0] * NWB
                wT_use = [0] * NWB
                hb_store = [0, 0]
                gact_use = [0, 0]
                wT_pend = []

                def load_wT(ig):
                    bi = ig % NWB
                    if wT_use[bi]:
                        wait("gpsimd", pe, wT_use[bi])
                    t_g = dma("gpsimd",
                              wgT[bi][:].rearrange("p (f c) -> p f c", f=NF),
                              gw_d[:, ig * 512:(ig + 1) * 512]
                              .rearrange("(f p) c -> p f c", p=128), q_wug[bi])
                    t_u = dma("gpsimd",
                              wuT[bi][:].rearrange("p (f c) -> p f c", f=NF),
                              uw_d[:, ig * 512:(ig + 1) * 512]
                              .rearrange("(f p) c -> p f c", p=128), q_wuu[bi])
                    wT_pend.append((bi, t_g, t_u))

                def conv_wT():
                    bi, t_g, t_u = wT_pend.pop(0)
                    wait("scalar", q_wug[bi], t_g)
                    op("scalar", lambda e, bi=bi: e.add(
                        out=wgT[bi][:], in_=wgT[bi][:], add=-128.0))
                    wait("scalar", q_wuu[bi], t_u)
                    wT_cv[bi] = op("scalar", lambda e, bi=bi: e.add(
                        out=wuT[bi][:], in_=wuT[bi][:], add=-128.0), inc=sc)

                load_wT(0)
                conv_wT()
                load_wT(1)
                conv_wT()
                load_wT(2)
                for ig in range(16):
                    bi = ig % NWB
                    wait("tensor", sc, wT_cv[bi])
                    ig_last = 0
                    for ic4 in range(4):
                        ic = ig * 4 + ic4
                        hbi = ic % 2
                        if hb_store[hbi]:
                            wait("vector", q_hst[hbi], hb_store[hbi])
                        t_h = 0
                        for tt in range(4):
                            bg = 2 * (tt % 2)
                            bu = bg + 1
                            pe_guard(bg)
                            pe_guard(bu)
                            g_t = u_t = 0
                            for fc in range(NF):
                                last = (fc == NF - 1)
                                lh = wgT[bi][:, fc * 512 + ic4 * 128:
                                             fc * 512 + ic4 * 128 + 128]
                                tg = op("tensor", lambda e, lh=lh, fc=fc,
                                        tt=tt, bg=bg, last=last:
                                        e.matmul(out=ps[bg][:, :], lhsT=lh,
                                                 rhs=xnT_sl(fc, tt * 512, 512),
                                                 start=(fc == 0), stop=last),
                                        inc=pe if last else None)
                                if last:
                                    g_t = tg
                            for fc in range(NF):
                                last = (fc == NF - 1)
                                lh = wuT[bi][:, fc * 512 + ic4 * 128:
                                             fc * 512 + ic4 * 128 + 128]
                                tu = op("tensor", lambda e, lh=lh, fc=fc,
                                        tt=tt, bu=bu, last=last:
                                        e.matmul(out=ps[bu][:, :], lhsT=lh,
                                                 rhs=xnT_sl(fc, tt * 512, 512),
                                                 start=(fc == 0), stop=last),
                                        inc=pe if last else None)
                                if last:
                                    u_t = tu
                            ig_last = u_t
                            gi = tt % 2
                            wait("scalar", pe, g_t)
                            if gact_use[gi]:
                                wait("scalar", vc, gact_use[gi])
                            t_ga = op("scalar", lambda e, bg=bg, gi=gi, ic=ic:
                                      e.activation(
                                          out=gact[gi][:], in_=ps[bg][:, :],
                                          func=AF.Silu,
                                          scale=sg_b[:, ic:ic + 1]), inc=sc)
                            bank_free[bg] = (sc, t_ga)
                            wait("vector", sc, t_ga)
                            wait("vector", pe, u_t)
                            t_h = op("vector",
                                     lambda e, bu=bu, gi=gi, ic=ic, hbi=hbi,
                                     tt=tt:
                                     e.scalar_tensor_tensor(
                                         out=h_b[hbi][:, tt * 512:
                                                      (tt + 1) * 512],
                                         in0=ps[bu][:, :],
                                         scalar=su_b[:, ic:ic + 1],
                                         in1=gact[gi][:],
                                         op0=OP.mult, op1=OP.mult), inc=vc)
                            gact_use[gi] = t_h
                            bank_free[bu] = (vc, t_h)
                        wait("scalar", vc, t_h)
                        hb_store[hbi] = dma(
                            "scalar", h_s[ic * 128:(ic + 1) * 128, :],
                            h_b[hbi][:], q_hst[hbi])
                        if ic4 == 0 and wT_pend:
                            conv_wT()
                    wT_use[bi] = ig_last
                    if ig + 3 < 16:
                        load_wT(ig + 3)

        fence()

        with ExitStack() as sc7:
            wdT = sbt(sc7, "wdT", [128, NIC * 1024], BF16)
            h_p = [sbt(sc7, f"h_p{i}", [128, 16 * 512], BF16) for i in range(2)]
            gf_t = [sbt(sc7, f"gf_t{i}", [128, 8 * 512], BF16)
                    for i in range(2)]
            s7 = [sbt(sc7, f"s7_{i}", [128, 8 * 512], BF16) for i in range(2)]
            s7_use = [0, 0]
            gf_store = [0, 0]
            hp_use = [0, 0]
            t_blk = [[None] * 4, [None] * 4]
            blk_pe = [[0] * 4, [0] * 4]
            sems7 = [q_wug[0], q_wug[1], q_wug[2], q_wuu[0]]
            for half in range(2):
                for p4 in range(4):
                    qq = sems7[p4]
                    if half == 1 and blk_pe[0][p4]:
                        wait("gpsimd", pe, blk_pe[0][p4])
                    t_ld = dma(
                        "gpsimd",
                        wdT[:, p4 * 16384:(p4 + 1) * 16384]
                        .rearrange("p (f c) -> p f c", f=16),
                        dw_d[p4 * 2048:(p4 + 1) * 2048,
                             half * 1024:(half + 1) * 1024]
                        .rearrange("(f p) c -> p f c", p=128), qq)
                    wait("scalar", qq, t_ld)
                    tcv = 0
                    for hh in range(2):
                        sl = wdT[:, (p4 * 2 + hh) * 8192:
                                 (p4 * 2 + hh + 1) * 8192]
                        tcv = op("scalar", lambda e, sl=sl: e.add(
                            out=sl, in_=sl, add=-128.0), inc=sc)
                    t_blk[half][p4] = (sc, tcv)
                for slab in range(4):
                    sbi = slab % 2
                    dn_pe = [0] * 8
                    for piece in range(4):
                        pbi = piece % 2
                        if slab == 0:
                            wait("tensor", t_blk[half][piece][0],
                                 t_blk[half][piece][1])
                        hq_eng, hq_sem = (("scalar", q_hp[pbi])
                                          if piece < 2
                                          else ("sync", q_hp2[pbi]))
                        if hp_use[pbi]:
                            wait(hq_eng, pe, hp_use[pbi])
                        t_hp = dma(
                            hq_eng,
                            h_p[pbi][:].rearrange("p (c t) -> p c t", c=16),
                            h_s.ap().rearrange("(c p) t -> p c t", p=128)
                            [:, piece * 16:(piece + 1) * 16,
                             slab * 512:(slab + 1) * 512], hq_sem)
                        wait("tensor", hq_sem, t_hp)
                        if piece == 0:
                            for hc in range(8):
                                pe_guard(hc)
                        for icl in range(16):
                            ic = piece * 16 + icl
                            for hc in range(8):
                                first = (ic == 0)
                                last = (ic == NIC - 1)
                                piece_end = (icl == 15 and hc == 7)
                                want_inc = last or piece_end
                                t = op("tensor",
                                       lambda e, ic=ic, hc=hc, pbi=pbi,
                                       icl=icl, first=first, last=last:
                                       e.matmul(
                                           out=ps[hc][:, :],
                                           lhsT=wdT[:, ic * 1024 + hc * 128:
                                                    ic * 1024 + hc * 128 + 128],
                                           rhs=h_p[pbi][:, icl * 512:
                                                        (icl + 1) * 512],
                                           start=first, stop=last),
                                       inc=pe if want_inc else None)
                                if last:
                                    dn_pe[hc] = t
                                if piece_end:
                                    hp_use[pbi] = t
                                    if slab == 3:
                                        blk_pe[half][piece] = t
                    if s7_use[sbi]:
                        wait("sync", vc, s7_use[sbi])
                    t_s7 = dma(
                        "sync",
                        s7[sbi][:].rearrange("p (o t) -> p o t", o=8),
                        s_s.ap().rearrange("(o p) t -> p o t", p=128)
                        [:, half * 8:(half + 1) * 8,
                         slab * 512:(slab + 1) * 512], q_s7[sbi])
                    if gf_store[sbi]:
                        wait("scalar", q_gf[sbi], gf_store[sbi])
                    t_gf = 0
                    for hc in range(8):
                        wait("scalar", pe, dn_pe[hc])
                        t_gf = op("scalar", lambda e, hc=hc, half=half,
                                  sbi=sbi:
                                  e.activation(
                                      out=gf_t[sbi][:, hc * 512:
                                                    (hc + 1) * 512],
                                      in_=ps[hc][:, :], func=AF.Silu,
                                      scale=sd_b[:, half * 8 + hc:
                                                 half * 8 + hc + 1]), inc=sc)
                        bank_free[hc] = (sc, t_gf)
                    wait("vector", sc, t_gf)
                    wait("vector", q_s7[sbi], t_s7)
                    s7_use[sbi] = op("vector", lambda e, sbi=sbi: e.tensor_mul(
                        out=gf_t[sbi][:], in0=gf_t[sbi][:], in1=s7[sbi][:]),
                        inc=vc)
                    wait("scalar", vc, s7_use[sbi])
                    gf_store[sbi] = dma(
                        "scalar",
                        gf_s.ap().rearrange("(o p) t -> p o t", p=128)
                        [:, half * 8:(half + 1) * 8,
                         slab * 512:(slab + 1) * 512],
                        gf_t[sbi][:].rearrange("p (o t) -> p o t", o=8),
                        q_gf[sbi])

        fence()

        with ExitStack() as sc8:
            NB8 = 4
            q8t = qsem("q8t", "sync", NB8)
            q8x = qsem("q8x", "gpsimd", NB8)
            q8y = qsem("q8y", "scalar", NB8)
            g_t8 = [sbt(sc8, f"g_t8{i}", [128, DM], BF16) for i in range(NB8)]
            x_t8 = [sbt(sc8, f"x_t8{i}", [128, DM], F32) for i in range(NB8)]
            o_t8 = [sbt(sc8, f"o_t8{i}", [128, DM], F32) for i in range(NB8)]
            tl = [0] * NB8
            xl = [0] * NB8
            ost = [0] * NB8
            ouse = [0] * NB8
            for tt in range(NF):
                bi = tt % NB8
                if ouse[bi]:
                    wait("sync", vc, ouse[bi])
                    wait("gpsimd", vc, ouse[bi])
                tl[bi] = dma("sync", g_t8[bi][:],
                             gf_s[0:DM, tt * 128:(tt + 1) * 128], q8t[bi],
                             transpose=True)
                xl[bi] = dma("gpsimd", x_t8[bi][:],
                             x_in[tt * 128:(tt + 1) * 128, :], q8x[bi])
                wait("vector", q8t[bi], tl[bi])
                wait("vector", q8x[bi], xl[bi])
                if ost[bi]:
                    wait("vector", q8y[bi], ost[bi])
                ouse[bi] = op("vector", lambda e, bi=bi: e.tensor_add(
                    out=o_t8[bi][:], in0=g_t8[bi][:], in1=x_t8[bi][:]), inc=vc)
                wait("scalar", vc, ouse[bi])
                ost[bi] = dma("scalar", y_d[tt * 128:(tt + 1) * 128, :],
                              o_t8[bi][:], q8y[bi])
            for i in range(NB8):
                wait("scalar", q8y[i], q8y[i].n)
                wait("sync", q8t[i], q8t[i].n)
                wait("gpsimd", q8x[i], q8x[i].n)

        with nc.Block() as block:
            @block.sync
            def _(e):
                for f in streams["sync"]:
                    f(e)

            @block.scalar
            def _(e):
                for f in streams["scalar"]:
                    f(e)

            @block.vector
            def _(e):
                for f in streams["vector"]:
                    f(e)

            @block.gpsimd
            def _(e):
                for f in streams["gpsimd"]:
                    f(e)

            @block.tensor
            def _(e):
                for f in streams["tensor"]:
                    f(e)

    return nc


_NC_CACHE = []


def _get_nc():
    if not _NC_CACHE:
        _NC_CACHE.append(build_nc())
    return _NC_CACHE[0]


def kernel(L0_attn_x, L0_attn_w_qkv, L0_attn_w_out, L0_attn_decay_param,
           L0_ffn_gate_w, L0_ffn_gate_scale, L0_ffn_up_w, L0_ffn_up_scale,
           L0_ffn_down_w, L0_ffn_down_scale):
    x = np.ascontiguousarray(np.asarray(L0_attn_x, dtype=np.float32))
    wqkv = np.ascontiguousarray(np.asarray(L0_attn_w_qkv, dtype=np.float32))
    wout = np.ascontiguousarray(np.asarray(L0_attn_w_out, dtype=np.float32))
    dk = np.asarray(L0_attn_decay_param, dtype=np.float32)
    def _u8T(a, rows, cols):
        a = np.ascontiguousarray(np.asarray(a, dtype=np.int32))
        return np.ascontiguousarray(
            a.view(np.uint8).reshape(rows, cols, 4)[:, :, 0].T)
    gw = _u8T(L0_ffn_gate_w, IF, DM)
    uw = _u8T(L0_ffn_up_w, IF, DM)
    dw = _u8T(L0_ffn_down_w, DM, IF)
    gs = np.asarray(L0_ffn_gate_scale, dtype=np.float32)
    us = np.asarray(L0_ffn_up_scale, dtype=np.float32)
    ds = np.asarray(L0_ffn_down_scale, dtype=np.float32)

    dk_rep = np.ascontiguousarray(np.broadcast_to(dk[None, :], (128, H)))
    sg = np.ascontiguousarray(gs.reshape(NIC, 128).T)
    su = np.ascontiguousarray(us.reshape(NIC, 128).T)
    sd = np.ascontiguousarray(ds.reshape(NF, 128).T)

    nc = _get_nc()
    in_maps = []
    for c in range(NCORES):
        b, half = c // 2, c % 2
        rm = np.zeros((128, 256), np.float32)
        if half == 1:
            r = c - 1
            rm[:, r * 32:(r + 1) * 32] = 1.0
        in_maps.append(dict(
            x_in=np.ascontiguousarray(x[b, half * TC:(half + 1) * TC, :]),
            w_qkv=wqkv, w_out=wout, decay_rep=dk_rep,
            gate_wT=gw, up_wT=uw, down_wT=dw,
            sg=sg, su=su, sd=sd, rm=rm,
        ))
    res = run_bass_kernel_spmd(nc, in_maps, core_ids=list(range(NCORES)))
    out = np.empty((B, T, DM), np.float32)
    for c in range(NCORES):
        b, half = c // 2, c % 2
        out[b, half * TC:(half + 1) * TC, :] = res.results[c]["y"]
    return out


# version 11
# speedup vs baseline: 1.2376x; 1.0266x over previous
import numpy as np
from contextlib import ExitStack

import concourse.bass as bass
import concourse.mybir as mybir
from concourse.bass_utils import run_bass_kernel_spmd

F32 = mybir.dt.float32
BF16 = mybir.dt.bfloat16
I32 = mybir.dt.int32
AF = mybir.ActivationFunctionType
OP = mybir.AluOpType
AX = mybir.AxisListType

B, T, DM = 4, 4096, 2048
H = 16
TC = 2048
NF = DM // 128
IF = 8192
NIC = IF // 128
NCORES = 8

ENGS = ("sync", "scalar", "vector", "gpsimd", "tensor")


class Ctr:
    __slots__ = ("h", "n")

    def __init__(self, h):
        self.h = h
        self.n = 0

    def bump(self, k=1):
        self.n += k
        return self.n


def build_nc(dbg=()):
    nc = bass.Bass("TRN2", target_bir_lowering=False)

    def _skind(name):
        return "ExternalOutput" if name in dbg else "Internal"

    x_in = nc.dram_tensor("x_in", [TC, DM], F32, kind="ExternalInput")
    wqkv_d = nc.dram_tensor("w_qkv", [DM, 3 * DM], F32, kind="ExternalInput")
    wout_d = nc.dram_tensor("w_out", [DM, DM], F32, kind="ExternalInput")
    dk_d = nc.dram_tensor("decay_rep", [128, H], F32, kind="ExternalInput")
    U8 = mybir.dt.uint8
    gw_d = nc.dram_tensor("gate_wT", [DM, IF], U8, kind="ExternalInput")
    uw_d = nc.dram_tensor("up_wT", [DM, IF], U8, kind="ExternalInput")
    dw_d = nc.dram_tensor("down_wT", [IF, DM], U8, kind="ExternalInput")
    sg_d = nc.dram_tensor("sg", [128, NIC], F32, kind="ExternalInput")
    su_d = nc.dram_tensor("su", [128, NIC], F32, kind="ExternalInput")
    sd_d = nc.dram_tensor("sd", [128, NF], F32, kind="ExternalInput")
    rm_d = nc.dram_tensor("rm", [128, 256], F32, kind="ExternalInput")

    y_d = nc.dram_tensor("y", [TC, DM], F32, kind="ExternalOutput")

    xn_s = nc.dram_tensor("xn_s", [TC, DM], BF16, kind=_skind("xn_s"))
    kv_s = nc.dram_tensor("kv_s", [2 * H * 128, TC], BF16, kind=_skind("kv_s"))
    ap_s = nc.dram_tensor("ap_s", [DM, TC], BF16, kind=_skind("ap_s"))
    s_s = nc.dram_tensor("s_s", [DM, TC], BF16, kind=_skind("s_s"))
    h_s = nc.dram_tensor("h_s", [IF, TC], BF16, kind=_skind("h_s"))
    gf_s = nc.dram_tensor("gf_s", [DM, TC], BF16, kind=_skind("gf_s"))
    cc_i = nc.dram_tensor("cc_i", [128, 2 * H], F32)
    cc_o = nc.dram_tensor("cc_o", [128 * NCORES, 2 * H], F32)
    st_dbg = nc.dram_tensor("st_dbg", [128, 4 * H], F32, kind=_skind("st_dbg"))

    streams = {e: [] for e in ENGS}

    def em(eng, fn):
        streams[eng].append(fn)

    def wait(eng, ctr, tgt):
        if tgt and tgt > 0:
            em(eng, lambda e, c=ctr, t=tgt: e.wait_ge(c.h, t))

    def dma(q, out, in_, ctr, transpose=False):
        t = ctr.bump(16)
        em(q, lambda e, o=out, i=in_, c=ctr, tr=transpose:
           e.dma_start(out=o, in_=i, transpose=tr).then_inc(c.h, 16))
        return t

    def op(eng, fn, inc=None):
        if inc is None:
            em(eng, fn)
            return None
        t = inc.bump(1)
        em(eng, lambda e, f=fn, c=inc: f(e).then_inc(c.h, 1))
        return t

    with ExitStack() as root:
        def sbt(scope, name, shape, dt):
            return scope.enter_context(nc.sbuf_tensor(name, shape, dt))

        def sem(name):
            return Ctr(root.enter_context(nc.semaphore(name)))

        cc = sem("cc")
        pe = sem("pe")
        vc = sem("vc")
        sc = sem("sc")
        bar = sem("bar")

        dma_ctrs = []

        def qsem(name, eng, n=1):
            out = []
            for i in range(n):
                c = sem(f"{name}{i}")
                dma_ctrs.append((eng, c))
                out.append(c)
            return out if n > 1 else out[0]

        q_c = qsem("q_c", "gpsimd")
        q_x = qsem("q_x", "gpsimd", 4)
        q_wug = qsem("q_wug", "gpsimd", 3)
        q_wuu = qsem("q_wuu", "gpsimd", 3)
        q_w = qsem("q_w", "gpsimd", 2)
        q_kst = qsem("q_kst", "scalar", 2)
        q_kv4 = qsem("q_kv4", "sync", 2)
        q_ap = qsem("q_ap", "scalar", 2)
        q_xn = qsem("q_xn", "scalar", 4)
        q_cc2 = qsem("q_cc2", "gpsimd")
        q_t = qsem("q_t", "sync", 2)
        q_apl = qsem("q_apl", "sync", 2)
        q_wo = qsem("q_wo", "gpsimd")
        q_sev = qsem("q_sev", "scalar", 2)
        q_hst = qsem("q_hst", "scalar", 2)
        q_hp = qsem("q_hp", "scalar", 2)
        q_hp2 = qsem("q_hp2", "sync", 2)
        q_s7 = qsem("q_s7", "sync", 2)
        q_gf = qsem("q_gf", "scalar", 2)
        q_y = qsem("q_y", "scalar", 2)
        po = sem("po")

        def fence():
            for eng, c in dma_ctrs:
                wait(eng, c, c.n)
            for e in ENGS:
                em(e, lambda eng: eng.sem_inc(bar.h, 1))
            tgt = bar.bump(5)
            for e in ENGS:
                wait(e, bar, tgt)

        ps = [root.enter_context(nc.psum_tensor(f"ps{i}", [128, 512], F32))
              for i in range(8)]
        bank_free = [None] * 8

        def pe_guard(i):
            if bank_free[i] is not None:
                c, t = bank_free[i]
                wait("tensor", c, t)
                bank_free[i] = None

        dk_raw = sbt(root, "dk_raw", [128, H], F32)
        dk_rep = sbt(root, "dk_rep", [128, H], F32)
        sg_b = sbt(root, "sg_b", [128, NIC], F32)
        su_b = sbt(root, "su_b", [128, NIC], F32)
        sd_b = sbt(root, "sd_b", [128, NF], F32)
        rm_b = sbt(root, "rm_b", [128, 256], F32)
        g_b = sbt(root, "g_b", [128, 256], F32)
        gm_b = sbt(root, "gm_b", [128, 256], F32)
        ssel = sbt(root, "ssel", [128, 2 * H], F32)
        states = sbt(root, "states", [128, 2 * H], F32)
        ones_cb = sbt(root, "ones_cb", [128, 1], BF16)
        ones_rb = sbt(root, "ones_rb", [1, 128], BF16)
        cst_b = sbt(root, "cst_b", [128, 4], F32)

        nc.const_aps.register(F32, -128.0, cst_b[:, 1:2])
        t_dk = dma("gpsimd", dk_raw[:], dk_d[:, :], q_c)
        for dst, src in ((sg_b, sg_d), (su_b, su_d), (sd_b, sd_d), (rm_b, rm_d)):
            dma("gpsimd", dst[:], src[:, :], q_c)
        op("vector", lambda e: e.memset(ones_cb[:], 1.0))
        op("vector", lambda e: e.memset(cst_b[:, 0:1], 1e-5))
        op("vector", lambda e: e.memset(cst_b[:, 1:2], -128.0))
        op("vector", lambda e: e.memset(cst_b[:, 2:3], -1e-6))
        op("vector", lambda e: e.memset(cst_b[:, 3:4], 1e-6))
        t_ones = op("vector", lambda e: e.memset(ones_rb[:], 1.0), inc=vc)
        wait("scalar", q_c, q_c.n)
        wait("scalar", vc, t_ones)
        t_sig = op("scalar", lambda e: e.activation(
            out=dk_rep[:], in_=dk_raw[:], func=AF.Sigmoid), inc=sc)

        with ExitStack() as scA:
            xnT = sbt(scA, "xnT", [128, NF * TC], BF16)

            def xnT_sl(fc, t0, tn):
                return xnT[:, fc * TC + t0: fc * TC + t0 + tn]

            with ExitStack() as scL:
                NLP = 4
                xt = [sbt(scL, f"xt{i}", [128, DM], F32) for i in range(NLP)]
                xc = [sbt(scL, f"xc{i}", [128, DM], F32) for i in range(NLP)]
                sqw = [sbt(scL, f"sqw{i}", [128, DM], F32)
                       for i in range(NLP)]
                s1 = [sbt(scL, f"s1_{i}", [128, 1], F32) for i in range(NLP)]
                s2 = [sbt(scL, f"s2_{i}", [128, 1], F32) for i in range(NLP)]
                s3 = [sbt(scL, f"s3_{i}", [128, 1], F32) for i in range(NLP)]
                xnb = [sbt(scL, f"xnb{i}", [128, DM], BF16)
                       for i in range(NLP)]
                ln_st = [0] * NLP
                ln_use = [0] * NLP
                t_r_par = [0] * NLP
                for tt in range(NF):
                    bi = tt % NLP
                    if ln_use[bi]:
                        wait("gpsimd", vc, ln_use[bi])
                    t_x = dma("gpsimd", xt[bi][:],
                              x_in[tt * 128:(tt + 1) * 128, :], q_x[bi])
                    wait("vector", q_x[bi], t_x)
                    wait("vector", sc, t_r_par[bi])
                    t_sum = op("vector", lambda e, bi=bi: e.reduce_sum(
                        out=s1[bi][:], in_=xt[bi][:], axis=AX.X), inc=vc)
                    wait("scalar", vc, t_sum)
                    t_m = op("scalar", lambda e, bi=bi: e.activation(
                        out=s2[bi][:], in_=s1[bi][:], func=AF.Copy,
                        scale=1.0 / DM), inc=sc)
                    wait("vector", sc, t_m)
                    t_c = op("vector", lambda e, bi=bi: e.tensor_scalar(
                        out=xc[bi][:], in0=xt[bi][:], scalar1=s2[bi][:, 0:1],
                        scalar2=None, op0=OP.subtract), inc=vc)
                    wait("scalar", vc, t_c)
                    t_sq = op("scalar", lambda e, bi=bi: e.activation(
                        out=sqw[bi][:], in_=xc[bi][:], func=AF.Square), inc=sc)
                    wait("vector", sc, t_sq)
                    t_rs = op("vector", lambda e, bi=bi: e.reduce_sum(
                        out=s1[bi][:], in_=sqw[bi][:], axis=AX.X), inc=vc)
                    wait("scalar", vc, t_rs)
                    op("scalar", lambda e, bi=bi: e.activation(
                        out=s2[bi][:], in_=s1[bi][:], func=AF.Ln, scale=1.0 / DM,
                        bias=cst_b[:, 0:1]))
                    t_r = op("scalar", lambda e, bi=bi: e.activation(
                        out=s3[bi][:], in_=s2[bi][:], func=AF.Exp,
                        scale=-0.5), inc=sc)
                    t_r_par[bi] = t_r
                    if ln_st[bi]:
                        wait("vector", q_xn[bi], ln_st[bi])
                    wait("vector", sc, t_r)
                    t_xn = op("vector", lambda e, bi=bi: e.tensor_scalar(
                        out=xnb[bi][:], in0=xc[bi][:], scalar1=s3[bi][:, 0:1],
                        scalar2=None, op0=OP.mult), inc=vc)
                    ln_use[bi] = t_xn
                    wait("scalar", vc, t_xn)
                    ln_st[bi] = dma("scalar", xn_s[tt * 128:(tt + 1) * 128, :],
                                    xnb[bi][:], q_xn[bi])
                for i in range(4):
                    wait("sync", q_xn[i], q_xn[i].n)
                for fc in range(NF):
                    dma("sync", xnT_sl(fc, 0, TC),
                        xn_s[0:TC, fc * 128:(fc + 1) * 128], q_t[0],
                        transpose=True)
                t_xnT = q_t[0].n

            fence()

            with ExitStack() as scH:
                ones_t = sbt(scH, "ones_t", [128, TC], BF16)
                dk_tl = [sbt(scH, f"dk_tl{i}", [128, TC], F32) for i in range(2)]
                t_onest = op("vector", lambda e: e.memset(ones_t[:], 1.0),
                             inc=vc)
                wkv_b = [sbt(scH, f"wkv{i}", [128, 2 * NF * 128], BF16)
                         for i in range(2)]
                phik = [sbt(scH, f"phik{i}", [128, TC], BF16) for i in range(2)]
                v_b = [sbt(scH, f"vb{i}", [128, TC], BF16) for i in range(2)]
                bkv = sbt(scH, "bkv", [128, TC], F32)
                t1b = sbt(scH, "t1b", [128, 512], F32)
                t2b = [sbt(scH, f"t2b{i}", [128, 512], F32) for i in range(2)]
                q_f = sbt(scH, "q_f", [128, TC], F32)
                rkv_f = sbt(scH, "rkv_f", [128, TC], F32)
                rks_f = sbt(scH, "rks_f", [128, TC], F32)
                prod_b = [sbt(scH, f"prod_b{i}", [128, TC], BF16)
                          for i in range(2)]
                num_b = sbt(scH, "num_b", [128, TC], F32)
                apb = [sbt(scH, f"apb{i}", [128, TC], BF16) for i in range(2)]
                recip_b = sbt(scH, "recip_b", [1, TC], BF16)
                den_row = sbt(scH, "den_row", [1, TC], F32)
                recip_row = sbt(scH, "recip_row", [1, TC], F32)

                QCOL, KCOL, VCOL = 0, DM, 2 * DM
                stt_busy = [0, 0]

                def emit_phi(bank_pe_t, bank, tt, dst_ap):
                    i2 = tt % 2
                    wait("scalar", pe, bank_pe_t)
                    if stt_busy[i2]:
                        wait("scalar", vc, stt_busy[i2])
                    op("scalar", lambda e, b=bank: e.activation(
                        out=t1b[:], in_=ps[b][:, :], func=AF.Relu,
                        scale=-1.0))
                    t2t = op("scalar", lambda e, i2=i2: e.activation(
                        out=t2b[i2][:], in_=t1b[:], func=AF.Exp,
                        scale=-1.0), inc=sc)
                    wait("vector", sc, t2t)
                    t = op("vector", lambda e, b=bank, i2=i2, d=dst_ap:
                           e.scalar_tensor_tensor(
                               out=d, in0=ps[b][:, :], scalar=0.0,
                               in1=t2b[i2][:], op0=OP.max, op1=OP.add), inc=vc)
                    stt_busy[i2] = t
                    bank_free[bank] = (vc, t)
                    return t

                wkv_load = [0, 0]
                wkv_inuse = [0, 0]
                kv_store = [0, 0]
                dk_fill = [0, 0]
                dk_used = [0, 0]
                st_copied = [0]

                def load_wkv(h):
                    bi = h % 2
                    if wkv_inuse[bi]:
                        wait("gpsimd", pe, wkv_inuse[bi])
                    for j, col0 in ((0, KCOL + h * 128), (1, VCOL + h * 128)):
                        wkv_load[bi] = dma(
                            "gpsimd",
                            wkv_b[bi][:, j * NF * 128:(j + 1) * NF * 128]
                            .rearrange("p (f c) -> p f c", f=NF),
                            wqkv_d[:, col0:col0 + 128]
                            .rearrange("(f p) c -> p f c", p=128), q_w[bi])

                load_wkv(0)
                load_wkv(1)

                for h in range(H):
                    bi = h % 2
                    wait("tensor", q_w[bi], wkv_load[bi])
                    if h == 0:
                        wait("tensor", q_t[0], t_xnT)
                    kv_pe = [0] * 8
                    for j, base in ((0, 0), (1, 4)):
                        for bk in range(4):
                            pe_guard(base + bk)
                        for fc in range(NF):
                            for tt in range(4):
                                b_ = base + tt
                                last = (fc == NF - 1)
                                lh = wkv_b[bi][:, (j * NF + fc) * 128:
                                               (j * NF + fc) * 128 + 128]
                                t = op("tensor",
                                       lambda e, lh=lh, fc=fc, tt=tt, b_=b_,
                                       last=last:
                                       e.matmul(out=ps[b_][:, :], lhsT=lh,
                                                rhs=xnT_sl(fc, tt * 512, 512),
                                                start=(fc == 0), stop=last),
                                       inc=pe if last else None)
                                if last:
                                    kv_pe[b_] = t
                    wkv_inuse[bi] = kv_pe[7]
                    if h + 1 < H:
                        load_wkv(h + 1)
                    if kv_store[bi]:
                        wait("vector", q_kst[bi], kv_store[bi])
                    phi_t = 0
                    for tt in range(4):
                        phi_t = emit_phi(kv_pe[tt], tt, tt,
                                         phik[bi][:, tt * 512:(tt + 1) * 512])
                    vcp_t = 0
                    for tt in range(4):
                        wait("vector", pe, kv_pe[4 + tt])
                        vcp_t = op("vector", lambda e, tt=tt, bi=bi:
                                   e.tensor_copy(
                                       out=v_b[bi][:, tt * 512:(tt + 1) * 512],
                                       in_=ps[4 + tt][:, :]), inc=vc)
                        bank_free[4 + tt] = (vc, vcp_t)
                    op("vector", lambda e, bi=bi: e.tensor_mul(
                        out=bkv[:], in0=phik[bi][:], in1=v_b[bi][:]))
                    if st_copied[0]:
                        wait("vector", sc, st_copied[0])
                    if dk_used[bi]:
                        wait("scalar", vc, dk_used[bi])
                    if h < 2:
                        wait("scalar", sc, t_sig)
                        wait("scalar", vc, t_onest)
                    dk_fill[bi] = op("scalar", lambda e, bi=bi, h=h: e.mul(
                        out=dk_tl[bi][:], in_=ones_t[:],
                        mul=dk_rep[:, h:h + 1]), inc=sc)
                    wait("vector", sc, dk_fill[bi])
                    op("vector", lambda e, bi=bi: e.tensor_tensor_scan(
                        out=rkv_f[:], data0=dk_tl[bi][:], data1=bkv[:],
                        initial=0.0, op0=OP.mult, op1=OP.add))
                    dk_used[bi] = op(
                        "vector", lambda e, bi=bi: e.tensor_tensor_scan(
                            out=rks_f[:], data0=dk_tl[bi][:], data1=phik[bi][:],
                            initial=0.0, op0=OP.mult, op1=OP.add), inc=vc)
                    wait("scalar", vc, dk_used[bi])
                    op("scalar", lambda e, h=h: e.copy(
                        out=states[:, 2 * h:2 * h + 1],
                        in_=rkv_f[:, TC - 1:TC]))
                    st_copied[0] = op("scalar", lambda e, h=h: e.copy(
                        out=states[:, 2 * h + 1:2 * h + 2],
                        in_=rks_f[:, TC - 1:TC]), inc=sc)
                    wait("scalar", vc, max(vcp_t, phi_t))
                    dma("scalar", kv_s[(2 * h) * 128:(2 * h + 1) * 128, :],
                        phik[bi][:], q_kst[bi])
                    kv_store[bi] = dma(
                        "scalar", kv_s[(2 * h + 1) * 128:(2 * h + 2) * 128, :],
                        v_b[bi][:], q_kst[bi])

                fence()

                dma("gpsimd", cc_i[:, :], states[:], q_cc2)
                wait("gpsimd", q_cc2, q_cc2.n)
                em("gpsimd", lambda e: e.collective_compute(
                    "AllGather", OP.bypass,
                    replica_groups=[list(range(NCORES))],
                    ins=[cc_i.ap().opt()], outs=[cc_o.ap().opt()],
                ).then_inc(cc.h))
                cc.bump(1)

                wq_load = [0, 0]
                wq_inuse = [0, 0]
                kv4_load = [0, 0]
                kv4_used = [0, 0]
                ap_store = [0, 0]
                recip_used = 0

                def load_wq(h):
                    bi = h % 2
                    if wq_inuse[bi]:
                        wait("gpsimd", pe, wq_inuse[bi])
                    wq_load[bi] = dma(
                        "gpsimd",
                        wkv_b[bi][:, 0:NF * 128]
                        .rearrange("p (f c) -> p f c", f=NF),
                        wqkv_d[:, QCOL + h * 128:QCOL + h * 128 + 128]
                        .rearrange("(f p) c -> p f c", p=128), q_w[bi])

                def load_kv4(h):
                    bi = h % 2
                    if kv4_used[bi]:
                        wait("sync", vc, kv4_used[bi])
                    dma("sync", phik[bi][:],
                        kv_s[(2 * h) * 128:(2 * h + 1) * 128, :], q_kv4[bi])
                    kv4_load[bi] = dma(
                        "sync", v_b[bi][:],
                        kv_s[(2 * h + 1) * 128:(2 * h + 2) * 128, :], q_kv4[bi])

                load_wq(0)
                load_kv4(0)
                load_wq(1)
                load_kv4(1)

                wait("gpsimd", cc, 1)
                t_g = dma("gpsimd", g_b[:].rearrange("p (r s) -> p r s", r=8),
                          cc_o.ap().rearrange("(r p) s -> p r s", p=128), q_cc2)
                wait("vector", q_cc2, t_g)
                t_gm = op("vector", lambda e: e.tensor_mul(
                    out=gm_b[:], in0=g_b[:], in1=rm_b[:]), inc=vc)
                wait("scalar", vc, t_gm)
                t_gm2 = op("scalar", lambda e: e.copy(
                    out=g_b[:], in_=gm_b[:]), inc=sc)
                wait("vector", sc, t_gm2)
                t_sel = op("vector", lambda e: e.tensor_reduce(
                    out=ssel[:], in_=g_b[:].rearrange("p (r s) -> p s r", r=8),
                    axis=AX.X, op=OP.add), inc=vc)
                if "st_dbg" in dbg:
                    wait("gpsimd", vc, t_sel)
                    dma("gpsimd", st_dbg[:, 0:2 * H], states[:], q_cc2)
                    dma("gpsimd", st_dbg[:, 2 * H:4 * H], ssel[:], q_cc2)

                st4 = {}

                def den_mm(h):
                    par = h % 2
                    wait("tensor", vc, st4[h]["t_prod"])
                    if h == 0:
                        wait("tensor", vc, t_ones)
                    den_pe = [0] * 4
                    for c in range(4):
                        pe_guard(4 + c)
                        den_pe[c] = op("tensor", lambda e, c=c, par=par:
                                       e.matmul(
                            out=ps[4 + c][0:1, :], lhsT=ones_cb[:],
                            rhs=prod_b[par][:, c * 512:(c + 1) * 512],
                            start=True, stop=True), inc=pe)
                    st4[h]["den_pe"] = den_pe

                def den_chain(h):
                    nonlocal recip_used
                    if recip_used:
                        wait("scalar", pe, recip_used)
                    t_dr = 0
                    for c in range(4):
                        wait("scalar", pe, st4[h]["den_pe"][c])
                        t_dr = op("scalar", lambda e, c=c: e.activation(
                            out=den_row[0:1, c * 512:(c + 1) * 512],
                            in_=ps[4 + c][0:1, :], func=AF.Relu,
                            bias=cst_b[0:1, 2:3]), inc=sc)
                        bank_free[4 + c] = (sc, t_dr)
                    op("scalar", lambda e: e.activation(
                        out=recip_row[0:1, :], in_=den_row[0:1, :], func=AF.Ln,
                        bias=cst_b[0:1, 3:4]))
                    st4[h]["t_rc"] = op("scalar", lambda e: e.activation(
                        out=recip_b[0:1, :], in_=recip_row[0:1, :],
                        func=AF.Exp, scale=-1.0), inc=sc)

                def rep_mm(h):
                    nonlocal recip_used
                    wait("tensor", sc, st4[h]["t_rc"])
                    rep_pe = [0] * 4
                    for c in range(4):
                        pe_guard(4 + c)
                        rep_pe[c] = op("tensor", lambda e, c=c: e.matmul(
                            out=ps[4 + c][:, :], lhsT=ones_rb[:],
                            rhs=recip_b[0:1, c * 512:(c + 1) * 512],
                            start=True, stop=True), inc=pe)
                    st4[h]["rep_pe"] = rep_pe
                    recip_used = rep_pe[3]

                def ap_out(h):
                    abi = h % 2
                    if ap_store[abi]:
                        wait("vector", q_ap[abi], ap_store[abi])
                    rep_pe = st4[h]["rep_pe"]
                    t_ap = 0
                    for c in range(4):
                        wait("vector", pe, rep_pe[c])
                        t_ap = op("vector", lambda e, c=c, abi=abi:
                                  e.tensor_tensor(
                            out=apb[abi][:, c * 512:(c + 1) * 512],
                            in0=num_b[:, c * 512:(c + 1) * 512],
                            in1=ps[4 + c][:, :], op=OP.mult), inc=vc)
                        bank_free[4 + c] = (vc, t_ap)
                    wait("scalar", vc, t_ap)
                    ap_store[abi] = dma(
                        "scalar", ap_s[h * 128:(h + 1) * 128, :],
                        apb[abi][:], q_ap[abi])

                for h in range(H):
                    bi = h % 2
                    par = h % 2
                    st4[h] = {}
                    wait("tensor", q_w[bi], wq_load[bi])
                    q_pe = [0] * 4
                    for bk in range(4):
                        pe_guard(bk)

                    def qmm(f0, f1, q_pe=q_pe, bi=bi):
                        for fc in range(f0, f1):
                            for tt in range(4):
                                last = (fc == NF - 1)
                                lh = wkv_b[bi][:, fc * 128:(fc + 1) * 128]
                                t = op("tensor", lambda e, lh=lh, fc=fc,
                                       tt=tt, last=last:
                                       e.matmul(out=ps[tt][:, :], lhsT=lh,
                                                rhs=xnT_sl(fc, tt * 512, 512),
                                                start=(fc == 0), stop=last),
                                       inc=pe if last else None)
                                if last:
                                    q_pe[tt] = t

                    qmm(0, 8)
                    if h >= 1:
                        den_mm(h - 1)
                        den_chain(h - 1)
                    qmm(8, NF)
                    wq_inuse[bi] = q_pe[3]
                    if h + 1 < H:
                        load_wq(h + 1)
                        load_kv4(h + 1)
                    if h >= 1:
                        rep_mm(h - 1)
                        op("vector", lambda e: e.tensor_mul(
                            out=num_b[:], in0=q_f[:], in1=rkv_f[:]))
                    for tt in range(4):
                        emit_phi(q_pe[tt], tt, tt,
                                 q_f[:, tt * 512:(tt + 1) * 512])
                    wait("vector", q_kv4[bi], kv4_load[bi])
                    op("vector", lambda e, bi=bi: e.tensor_mul(
                        out=bkv[:], in0=phik[bi][:], in1=v_b[bi][:]))
                    if dk_used[bi]:
                        wait("scalar", vc, dk_used[bi])
                    dk_fill[bi] = op("scalar", lambda e, bi=bi, h=h: e.mul(
                        out=dk_tl[bi][:], in_=ones_t[:],
                        mul=dk_rep[:, h:h + 1]), inc=sc)
                    wait("vector", sc, dk_fill[bi])
                    op("vector", lambda e, bi=bi, h=h: e.tensor_tensor_scan(
                        out=rkv_f[:], data0=dk_tl[bi][:], data1=bkv[:],
                        initial=ssel[:, 2 * h:2 * h + 1],
                        op0=OP.mult, op1=OP.add))
                    kv4_used[bi] = op(
                        "vector", lambda e, bi=bi, h=h: e.tensor_tensor_scan(
                            out=rks_f[:], data0=dk_tl[bi][:], data1=phik[bi][:],
                            initial=ssel[:, 2 * h + 1:2 * h + 2],
                            op0=OP.mult, op1=OP.add), inc=vc)
                    dk_used[bi] = kv4_used[bi]
                    if h >= 2:
                        wait("vector", pe, st4[h - 2]["den_pe"][3])
                    st4[h]["t_prod"] = op("vector", lambda e, par=par:
                                          e.tensor_mul(
                        out=prod_b[par][:], in0=q_f[:], in1=rks_f[:]), inc=vc)
                    if h >= 1:
                        ap_out(h - 1)
                den_mm(H - 1)
                den_chain(H - 1)
                rep_mm(H - 1)
                op("vector", lambda e: e.tensor_mul(
                    out=num_b[:], in0=q_f[:], in1=rkv_f[:]))
                ap_out(H - 1)

            fence()

            with ExitStack() as sc5:
                wo_b = sbt(sc5, "wo_b", [128, NF * DM], BF16)
                ap_sl = [sbt(sc5, f"ap_sl{i}", [128, NF * 512], BF16)
                         for i in range(2)]
                sev = [sbt(sc5, f"sev{i}", [128, 512], BF16) for i in range(2)]
                wo_sem = [q_wug[0], q_wug[1], q_wug[2], q_wuu[0],
                          q_wuu[1], q_wuu[2], q_hp2[0], q_hp2[1]]
                wo_tgt = [0] * NF

                def load_wo(oc):
                    wo_tgt[oc] = dma(
                        "gpsimd",
                        wo_b[:, oc * DM:(oc + 1) * DM]
                        .rearrange("p (f c) -> p f c", f=NF),
                        wout_d[:, oc * 128:(oc + 1) * 128]
                        .rearrange("(f p) c -> p f c", p=128), wo_sem[oc % 8])

                for oc in range(8):
                    load_wo(oc)
                ap_use = [0, 0]
                sev_store = [0, 0]
                for tt in range(4):
                    bi = tt % 2
                    if ap_use[bi]:
                        wait("sync", pe, ap_use[bi])
                    t_apl = dma(
                        "sync",
                        ap_sl[bi][:].rearrange("p (f t) -> p f t", f=NF),
                        ap_s.ap().rearrange("(f p) t -> p f t", p=128)
                        [:, :, tt * 512:(tt + 1) * 512], q_apl[bi])
                    wait("tensor", q_apl[bi], t_apl)
                    last_mm = 0
                    for oc in range(NF):
                        bk = oc % 2
                        si = oc % 2
                        if tt == 0:
                            wait("tensor", wo_sem[oc % 8], wo_tgt[oc])
                        pe_guard(bk)
                        for fc in range(NF):
                            last = (fc == NF - 1)
                            t = op("tensor", lambda e, bi=bi, fc=fc, oc=oc,
                                   bk=bk, last=last:
                                   e.matmul(
                                       out=ps[bk][:, :],
                                       lhsT=wo_b[:, oc * DM + fc * 128:
                                                 oc * DM + fc * 128 + 128],
                                       rhs=ap_sl[bi][:, fc * 512:
                                                     (fc + 1) * 512],
                                       start=(fc == 0), stop=last),
                                   inc=pe if last else None)
                            if last:
                                last_mm = t
                        if tt == 0 and oc < 8:
                            wait("gpsimd", pe, last_mm)
                            load_wo(oc + 8)
                        wait("scalar", pe, last_mm)
                        if sev_store[si]:
                            wait("scalar", q_sev[si], sev_store[si])
                        t_ev = op("scalar", lambda e, si=si, bk=bk: e.copy(
                            out=sev[si][:], in_=ps[bk][:, :]), inc=sc)
                        bank_free[bk] = (sc, t_ev)
                        sev_store[si] = dma(
                            "scalar",
                            s_s[oc * 128:(oc + 1) * 128,
                                tt * 512:(tt + 1) * 512], sev[si][:], q_sev[si])
                    ap_use[bi] = last_mm

            fence()

            with ExitStack() as sc6:
                NWB = 3
                wgT = [sbt(sc6, f"wgT{i}", [128, NF * 512], BF16)
                       for i in range(NWB)]
                wuT = [sbt(sc6, f"wuT{i}", [128, NF * 512], BF16)
                       for i in range(NWB)]
                h_b = [sbt(sc6, f"h_b{i}", [128, TC], BF16) for i in range(2)]
                gact = [sbt(sc6, f"gact{i}", [128, 512], F32) for i in range(2)]
                wT_cv = [0] * NWB
                wT_use = [0] * NWB
                hb_store = [0, 0]
                gact_use = [0, 0]
                wT_pend = []

                def load_wT(ig):
                    bi = ig % NWB
                    if wT_use[bi]:
                        wait("gpsimd", pe, wT_use[bi])
                    t_g = dma("gpsimd",
                              wgT[bi][:].rearrange("p (f c) -> p f c", f=NF),
                              gw_d[:, ig * 512:(ig + 1) * 512]
                              .rearrange("(f p) c -> p f c", p=128), q_wug[bi])
                    t_u = dma("gpsimd",
                              wuT[bi][:].rearrange("p (f c) -> p f c", f=NF),
                              uw_d[:, ig * 512:(ig + 1) * 512]
                              .rearrange("(f p) c -> p f c", p=128), q_wuu[bi])
                    wT_pend.append((bi, t_g, t_u))

                def conv_wT():
                    bi, t_g, t_u = wT_pend.pop(0)
                    wait("scalar", q_wug[bi], t_g)
                    op("scalar", lambda e, bi=bi: e.add(
                        out=wgT[bi][:], in_=wgT[bi][:], add=-128.0))
                    wait("scalar", q_wuu[bi], t_u)
                    wT_cv[bi] = op("scalar", lambda e, bi=bi: e.add(
                        out=wuT[bi][:], in_=wuT[bi][:], add=-128.0), inc=sc)

                load_wT(0)
                conv_wT()
                load_wT(1)
                conv_wT()
                load_wT(2)
                for ig in range(16):
                    bi = ig % NWB
                    wait("tensor", sc, wT_cv[bi])
                    ig_last = 0
                    for ic4 in range(4):
                        ic = ig * 4 + ic4
                        hbi = ic % 2
                        if hb_store[hbi]:
                            wait("vector", q_hst[hbi], hb_store[hbi])
                        t_h = 0
                        for tt in range(4):
                            bg = 2 * (tt % 2)
                            bu = bg + 1
                            pe_guard(bg)
                            pe_guard(bu)
                            g_t = u_t = 0
                            for fc in range(NF):
                                last = (fc == NF - 1)
                                lh = wgT[bi][:, fc * 512 + ic4 * 128:
                                             fc * 512 + ic4 * 128 + 128]
                                tg = op("tensor", lambda e, lh=lh, fc=fc,
                                        tt=tt, bg=bg, last=last:
                                        e.matmul(out=ps[bg][:, :], lhsT=lh,
                                                 rhs=xnT_sl(fc, tt * 512, 512),
                                                 start=(fc == 0), stop=last),
                                        inc=pe if last else None)
                                if last:
                                    g_t = tg
                            for fc in range(NF):
                                last = (fc == NF - 1)
                                lh = wuT[bi][:, fc * 512 + ic4 * 128:
                                             fc * 512 + ic4 * 128 + 128]
                                tu = op("tensor", lambda e, lh=lh, fc=fc,
                                        tt=tt, bu=bu, last=last:
                                        e.matmul(out=ps[bu][:, :], lhsT=lh,
                                                 rhs=xnT_sl(fc, tt * 512, 512),
                                                 start=(fc == 0), stop=last),
                                        inc=pe if last else None)
                                if last:
                                    u_t = tu
                            ig_last = u_t
                            gi = tt % 2
                            wait("scalar", pe, g_t)
                            if gact_use[gi]:
                                wait("scalar", vc, gact_use[gi])
                            t_ga = op("scalar", lambda e, bg=bg, gi=gi, ic=ic:
                                      e.activation(
                                          out=gact[gi][:], in_=ps[bg][:, :],
                                          func=AF.Silu,
                                          scale=sg_b[:, ic:ic + 1]), inc=sc)
                            bank_free[bg] = (sc, t_ga)
                            wait("vector", sc, t_ga)
                            wait("vector", pe, u_t)
                            t_h = op("vector",
                                     lambda e, bu=bu, gi=gi, ic=ic, hbi=hbi,
                                     tt=tt:
                                     e.scalar_tensor_tensor(
                                         out=h_b[hbi][:, tt * 512:
                                                      (tt + 1) * 512],
                                         in0=ps[bu][:, :],
                                         scalar=su_b[:, ic:ic + 1],
                                         in1=gact[gi][:],
                                         op0=OP.mult, op1=OP.mult), inc=vc)
                            gact_use[gi] = t_h
                            bank_free[bu] = (vc, t_h)
                        wait("scalar", vc, t_h)
                        hb_store[hbi] = dma(
                            "scalar", h_s[ic * 128:(ic + 1) * 128, :],
                            h_b[hbi][:], q_hst[hbi])
                        if ic4 == 0 and wT_pend:
                            conv_wT()
                    wT_use[bi] = ig_last
                    if ig + 3 < 16:
                        load_wT(ig + 3)

        fence()

        with ExitStack() as sc7:
            wdT = sbt(sc7, "wdT", [128, NIC * 1024], BF16)
            h_p = [sbt(sc7, f"h_p{i}", [128, 16 * 512], BF16) for i in range(2)]
            gf_t = [sbt(sc7, f"gf_t{i}", [128, 8 * 512], BF16)
                    for i in range(2)]
            s7 = [sbt(sc7, f"s7_{i}", [128, 8 * 512], BF16) for i in range(2)]
            s7_use = [0, 0]
            gf_store = [0, 0]
            hp_use = [0, 0]
            t_blk = [[None] * 4, [None] * 4]
            blk_pe = [[0] * 4, [0] * 4]
            sems7 = [q_wug[0], q_wug[1], q_wug[2], q_wuu[0]]
            for half in range(2):
                for p4 in range(4):
                    qq = sems7[p4]
                    if half == 1 and blk_pe[0][p4]:
                        wait("gpsimd", pe, blk_pe[0][p4])
                    t_ld = dma(
                        "gpsimd",
                        wdT[:, p4 * 16384:(p4 + 1) * 16384]
                        .rearrange("p (f c) -> p f c", f=16),
                        dw_d[p4 * 2048:(p4 + 1) * 2048,
                             half * 1024:(half + 1) * 1024]
                        .rearrange("(f p) c -> p f c", p=128), qq)
                    wait("scalar", qq, t_ld)
                    tcv = 0
                    for hh in range(2):
                        sl = wdT[:, (p4 * 2 + hh) * 8192:
                                 (p4 * 2 + hh + 1) * 8192]
                        tcv = op("scalar", lambda e, sl=sl: e.add(
                            out=sl, in_=sl, add=-128.0), inc=sc)
                    t_blk[half][p4] = (sc, tcv)
                for slab in range(4):
                    sbi = slab % 2
                    dn_pe = [0] * 8
                    for piece in range(4):
                        pbi = piece % 2
                        if slab == 0:
                            wait("tensor", t_blk[half][piece][0],
                                 t_blk[half][piece][1])
                        hq_eng, hq_sem = (("scalar", q_hp[pbi])
                                          if piece < 2
                                          else ("sync", q_hp2[pbi]))
                        if hp_use[pbi]:
                            wait(hq_eng, pe, hp_use[pbi])
                        t_hp = dma(
                            hq_eng,
                            h_p[pbi][:].rearrange("p (c t) -> p c t", c=16),
                            h_s.ap().rearrange("(c p) t -> p c t", p=128)
                            [:, piece * 16:(piece + 1) * 16,
                             slab * 512:(slab + 1) * 512], hq_sem)
                        wait("tensor", hq_sem, t_hp)
                        if piece == 0:
                            for hc in range(8):
                                pe_guard(hc)
                        for icl in range(16):
                            ic = piece * 16 + icl
                            for hc in range(8):
                                first = (ic == 0)
                                last = (ic == NIC - 1)
                                piece_end = (icl == 15 and hc == 7)
                                want_inc = last or piece_end
                                t = op("tensor",
                                       lambda e, ic=ic, hc=hc, pbi=pbi,
                                       icl=icl, first=first, last=last:
                                       e.matmul(
                                           out=ps[hc][:, :],
                                           lhsT=wdT[:, ic * 1024 + hc * 128:
                                                    ic * 1024 + hc * 128 + 128],
                                           rhs=h_p[pbi][:, icl * 512:
                                                        (icl + 1) * 512],
                                           start=first, stop=last),
                                       inc=pe if want_inc else None)
                                if last:
                                    dn_pe[hc] = t
                                if piece_end:
                                    hp_use[pbi] = t
                                    if slab == 3:
                                        blk_pe[half][piece] = t
                    if s7_use[sbi]:
                        wait("sync", vc, s7_use[sbi])
                    t_s7 = dma(
                        "sync",
                        s7[sbi][:].rearrange("p (o t) -> p o t", o=8),
                        s_s.ap().rearrange("(o p) t -> p o t", p=128)
                        [:, half * 8:(half + 1) * 8,
                         slab * 512:(slab + 1) * 512], q_s7[sbi])
                    if gf_store[sbi]:
                        wait("scalar", q_gf[sbi], gf_store[sbi])
                    t_gf = 0
                    for hc in range(8):
                        wait("scalar", pe, dn_pe[hc])
                        t_gf = op("scalar", lambda e, hc=hc, half=half,
                                  sbi=sbi:
                                  e.activation(
                                      out=gf_t[sbi][:, hc * 512:
                                                    (hc + 1) * 512],
                                      in_=ps[hc][:, :], func=AF.Silu,
                                      scale=sd_b[:, half * 8 + hc:
                                                 half * 8 + hc + 1]), inc=sc)
                        bank_free[hc] = (sc, t_gf)
                    wait("vector", sc, t_gf)
                    wait("vector", q_s7[sbi], t_s7)
                    s7_use[sbi] = op("vector", lambda e, sbi=sbi: e.tensor_mul(
                        out=gf_t[sbi][:], in0=gf_t[sbi][:], in1=s7[sbi][:]),
                        inc=vc)
                    wait("scalar", vc, s7_use[sbi])
                    gf_store[sbi] = dma(
                        "scalar",
                        gf_s.ap().rearrange("(o p) t -> p o t", p=128)
                        [:, half * 8:(half + 1) * 8,
                         slab * 512:(slab + 1) * 512],
                        gf_t[sbi][:].rearrange("p (o t) -> p o t", o=8),
                        q_gf[sbi])

        fence()

        with ExitStack() as sc8:
            NB8 = 4
            q8t = qsem("q8t", "sync", NB8)
            q8x = qsem("q8x", "gpsimd", NB8)
            q8y = qsem("q8y", "scalar", NB8)
            g_t8 = [sbt(sc8, f"g_t8{i}", [128, DM], BF16) for i in range(NB8)]
            x_t8 = [sbt(sc8, f"x_t8{i}", [128, DM], F32) for i in range(NB8)]
            o_t8 = [sbt(sc8, f"o_t8{i}", [128, DM], F32) for i in range(NB8)]
            tl = [0] * NB8
            xl = [0] * NB8
            ost = [0] * NB8
            ouse = [0] * NB8
            for tt in range(NF):
                bi = tt % NB8
                if ouse[bi]:
                    wait("sync", vc, ouse[bi])
                    wait("gpsimd", vc, ouse[bi])
                tl[bi] = dma("sync", g_t8[bi][:],
                             gf_s[0:DM, tt * 128:(tt + 1) * 128], q8t[bi],
                             transpose=True)
                xl[bi] = dma("gpsimd", x_t8[bi][:],
                             x_in[tt * 128:(tt + 1) * 128, :], q8x[bi])
                wait("vector", q8t[bi], tl[bi])
                wait("vector", q8x[bi], xl[bi])
                if ost[bi]:
                    wait("vector", q8y[bi], ost[bi])
                ouse[bi] = op("vector", lambda e, bi=bi: e.tensor_add(
                    out=o_t8[bi][:], in0=g_t8[bi][:], in1=x_t8[bi][:]), inc=vc)
                wait("scalar", vc, ouse[bi])
                ost[bi] = dma("scalar", y_d[tt * 128:(tt + 1) * 128, :],
                              o_t8[bi][:], q8y[bi])
            for i in range(NB8):
                wait("scalar", q8y[i], q8y[i].n)
                wait("sync", q8t[i], q8t[i].n)
                wait("gpsimd", q8x[i], q8x[i].n)

        with nc.Block() as block:
            @block.sync
            def _(e):
                for f in streams["sync"]:
                    f(e)

            @block.scalar
            def _(e):
                for f in streams["scalar"]:
                    f(e)

            @block.vector
            def _(e):
                for f in streams["vector"]:
                    f(e)

            @block.gpsimd
            def _(e):
                for f in streams["gpsimd"]:
                    f(e)

            @block.tensor
            def _(e):
                for f in streams["tensor"]:
                    f(e)

    return nc


_NC_CACHE = []


def _get_nc():
    if not _NC_CACHE:
        _NC_CACHE.append(build_nc())
    return _NC_CACHE[0]


def kernel(L0_attn_x, L0_attn_w_qkv, L0_attn_w_out, L0_attn_decay_param,
           L0_ffn_gate_w, L0_ffn_gate_scale, L0_ffn_up_w, L0_ffn_up_scale,
           L0_ffn_down_w, L0_ffn_down_scale):
    x = np.ascontiguousarray(np.asarray(L0_attn_x, dtype=np.float32))
    wqkv = np.ascontiguousarray(np.asarray(L0_attn_w_qkv, dtype=np.float32))
    wout = np.ascontiguousarray(np.asarray(L0_attn_w_out, dtype=np.float32))
    dk = np.asarray(L0_attn_decay_param, dtype=np.float32)
    def _u8T(a, rows, cols):
        a = np.ascontiguousarray(np.asarray(a, dtype=np.int32))
        return np.ascontiguousarray(
            a.view(np.uint8).reshape(rows, cols, 4)[:, :, 0].T)
    gw = _u8T(L0_ffn_gate_w, IF, DM)
    uw = _u8T(L0_ffn_up_w, IF, DM)
    dw = _u8T(L0_ffn_down_w, DM, IF)
    gs = np.asarray(L0_ffn_gate_scale, dtype=np.float32)
    us = np.asarray(L0_ffn_up_scale, dtype=np.float32)
    ds = np.asarray(L0_ffn_down_scale, dtype=np.float32)

    dk_rep = np.ascontiguousarray(np.broadcast_to(dk[None, :], (128, H)))
    sg = np.ascontiguousarray(gs.reshape(NIC, 128).T)
    su = np.ascontiguousarray(us.reshape(NIC, 128).T)
    sd = np.ascontiguousarray(ds.reshape(NF, 128).T)

    nc = _get_nc()
    in_maps = []
    for c in range(NCORES):
        b, half = c // 2, c % 2
        rm = np.zeros((128, 256), np.float32)
        if half == 1:
            r = c - 1
            rm[:, r * 32:(r + 1) * 32] = 1.0
        in_maps.append(dict(
            x_in=np.ascontiguousarray(x[b, half * TC:(half + 1) * TC, :]),
            w_qkv=wqkv, w_out=wout, decay_rep=dk_rep,
            gate_wT=gw, up_wT=uw, down_wT=dw,
            sg=sg, su=su, sd=sd, rm=rm,
        ))
    res = run_bass_kernel_spmd(nc, in_maps, core_ids=list(range(NCORES)))
    out = np.empty((B, T, DM), np.float32)
    for c in range(NCORES):
        b, half = c // 2, c % 2
        out[b, half * TC:(half + 1) * TC, :] = res.results[c]["y"]
    return out
